# version 16
# baseline (speedup 1.0000x reference)
import contextlib

import numpy as np
import concourse.bass as bass
import concourse.mybir as mybir
from concourse.bass_utils import run_bass_kernel_spmd

F32 = mybir.dt.float32
BF16 = mybir.dt.bfloat16
AF = mybir.ActivationFunctionType
ALU = mybir.AluOpType
AX = mybir.AxisListType

NCORES = 8
D = 2048
NH, HD, RD = 16, 128, 64
QR, KVR = 1536, 512
DE = 8192
LN_EPS = 1e-5
PI = float(np.pi)
ASCALE = (HD + RD) ** -0.5
BSCALE = HD ** -0.5
DC = D // 128
QC = QR // 128
KC = KVR // 128


class _Rec:
    __slots__ = ("k", "incs", "is_hwdge_dma", "eng", "opidx")

    def __init__(self, k, is_hwdge_dma=False):
        self.k = k
        self.incs = []
        self.is_hwdge_dma = is_hwdge_dma
        self.eng = None
        self.opidx = None

    def then_inc(self, sem, n=1):
        assert not self.incs, f"second sem update on one instruction: {sem}"
        self.incs.append((sem, n))
        self.k.cnt[sem] = self.k.cnt.get(sem, 0) + n
        if self.is_hwdge_dma:
            self.k.hwdge_sems.add(sem)
        if self.eng is not None:
            self.k.chain[self.eng]["att"].append(
                (self.opidx, sem, self.k.cnt[sem]))
        return self


def _ap_span(x):
    try:
        name = x.tensor.name
        off = int(x.offset)
        ext = 1
        for step, count in x.ap:
            ext += abs(int(step)) * (int(count) - 1)
        return (name, off, off + ext)
    except AttributeError:
        return None


class _EngQ:
    def __init__(self, k, name):
        self._k, self._name = k, name

    def __getattr__(self, meth):
        k, name = self._k, self._name

        def call(*a, **kw):
            if meth == "drain" and name in ("vector", "scalar", "gpsimd"):
                st = k.chain[name]
                lastrec = st["recs"].get(st["cur"] - 1)
                if lastrec is not None and not lastrec.incs:
                    return lastrec
            rec = _Rec(k, is_hwdge_dma=(meth == "dma_start"
                                        and name in ("sync", "scalar", "vector")))
            if name in ("vector", "scalar", "gpsimd") and meth not in (
                    "wait_ge", "drain", "nop", "dma_start", "collective_compute"):
                rd, wr = [], []
                for i, x in enumerate(a):
                    s = _ap_span(x)
                    if s:
                        (wr if i == 0 else rd).append(s)
                for kk, x in kw.items():
                    s = _ap_span(x)
                    if s:
                        (wr if kk in ("out", "accum_out") else rd).append(s)
                st = k.chain[name]
                dW, dR = st["dW"], st["dR"]
                ret = st["retired"]

                def maxdep(spans, dd):
                    m = -1
                    for n, lo, hi in spans:
                        for dlo, dhi, oi in dd.get(n, ()):
                            if oi > ret and lo < dhi and dlo < hi and oi > m:
                                m = oi
                    return m

                dep = max(maxdep(rd, dW), maxdep(wr, dW), maxdep(wr, dR))
                if dep >= 0:
                    att = st["att"]
                    cover = None
                    for oi, sm, val in att:
                        if oi >= dep:
                            cover = (oi, sm, val)
                            break
                    if cover is None:
                        oi = None
                        for cand in range(dep, st["cur"]):
                            r = st["recs"].get(cand)
                            if r is not None and not r.incs:
                                oi = cand
                                break
                        if oi is not None:
                            ch = k.sem("ch_" + name)
                            st["recs"][oi].then_inc(ch, 1)
                            cover = att[-1]
                            assert cover[0] == oi
                        else:
                            k.steps.append((name, "drain", (), {}, _Rec(k)))
                            st["retired"] = st["cur"] - 1
                            dW.clear()
                            dR.clear()
                            att.clear()
                    if cover is not None:
                        k.steps.append((name, "wait_ge",
                                        (cover[1], cover[2]), {}, _Rec(k)))
                        newret = cover[0]
                        st["retired"] = newret
                        for dd in (dW, dR):
                            for n in list(dd):
                                kept = [t for t in dd[n] if t[2] > newret]
                                if kept:
                                    dd[n] = kept
                                else:
                                    del dd[n]
                        st["att"] = [x for x in st["att"] if x[0] > newret]
                oi = st["cur"]
                st["cur"] = oi + 1
                st["recs"][oi] = rec
                rec.eng, rec.opidx = name, oi
                for n, lo, hi in wr:
                    dW.setdefault(n, []).append((lo, hi, oi))
                for n, lo, hi in rd:
                    dR.setdefault(n, []).append((lo, hi, oi))
            k.steps.append((name, meth, a, kw, rec))
            if meth == "drain" and name in k.chain:
                st = k.chain[name]
                st["retired"] = st["cur"] - 1
                st["dW"].clear()
                st["dR"].clear()
                st["att"].clear()
            return rec

        return call


class Arena:
    def __init__(self, k, name, total_elems, dt):
        self.tile = k.sbuf(name, [128, total_elems], dt)
        self.name = name
        self.total = total_elems
        self.off = 0
        self.marks = []
        self.peak = 0

    def alloc(self, elems):
        elems = (elems + 63) & ~63
        assert self.off + elems <= self.total, \
            f"arena {self.name} overflow {self.off}+{elems} > {self.total}"
        ap = self.tile[:, self.off:self.off + elems]
        self.off += elems
        self.peak = max(self.peak, self.off)
        return ap

    def push(self):
        self.marks.append(self.off)

    def pop(self):
        self.off = self.marks.pop()


class DBuf:

    def __init__(self, k, name, nslots, channels=("y",)):
        self.k, self.name, self.n = k, name, nslots
        self.channels = channels
        self.rounds = [0] * nslots
        self.seen = set()
        for i in range(nslots):
            for ch in channels:
                k.sem(f"{name}{ch}_l{i}")
            k.sem(f"{name}_f{i}")

    def nexti(self):
        self._gi = getattr(self, "_gi", -1) + 1
        return self._gi

    def pre_load(self, eng, i):
        s = i % self.n
        if i not in self.seen:
            self.seen.add(i)
            self.rounds[s] += 1
        r = self.rounds[s] - 1
        if r >= 1:
            self.k.wait(eng, f"{self.name}_f{s}", val=r)

    def load_dma(self, eng, i, ch="y", **dma):
        s = i % self.n
        eng.dma_start(**dma).then_inc(f"{self.name}{ch}_l{s}", 16)

    def ready(self, eng, i):
        s = i % self.n
        for ch in self.channels:
            self.k.wait(eng, f"{self.name}{ch}_l{s}")

    def free(self, rec, i):
        rec.then_inc(f"{self.name}_f{i % self.n}", 1)

    def wait_freed(self, engq, i):
        self.k.wait(engq, f"{self.name}_f{i % self.n}", val=i // self.n + 1)


class KBuild:
    ENGS = ("tensor", "vector", "scalar", "gpsimd", "sync")

    def __init__(self, tok=1024, debug=False):
        self.nc = bass.Bass("TRN2", target_bir_lowering=False)
        self.es = contextlib.ExitStack()
        self.steps = []
        self.sems = {}
        self.cnt = {}
        self.tok = tok
        self.debug = debug
        self.dbg_out = {}
        self.chain = {e: {"cur": 0, "retired": -1, "dW": {}, "dR": {},
                          "att": [], "recs": {}} for e in self.ENGS}
        self.hwdge_sems = set()
        self.dbufs = {}
        self.T = _EngQ(self, "tensor")
        self.V = _EngQ(self, "vector")
        self.S = _EngQ(self, "scalar")
        self.G = _EngQ(self, "gpsimd")
        self.Y = _EngQ(self, "sync")

    def sem(self, name):
        if name not in self.sems:
            self.sems[name] = self.es.enter_context(self.nc.semaphore(name))
            self.cnt[name] = 0
        return name

    def sbuf(self, name, shape, dt=F32):
        return self.es.enter_context(self.nc.sbuf_tensor(name, shape, dt))

    def psum(self, name, shape, dt=F32):
        return self.es.enter_context(self.nc.psum_tensor(name, shape, dt))

    def dram(self, name, shape, dt=F32, kind=None):
        if kind is None:
            return self.nc.dram_tensor(name, shape, dt).ap()
        return self.nc.dram_tensor(name, shape, dt, kind=kind).ap()

    def dbg(self, name, shape, dt=F32):
        ap = self.dram("dbg_" + name, shape, dt, kind="ExternalOutput")
        self.dbg_out[name] = ap
        return ap

    def at(self, sem):
        return self.cnt.get(sem, 0)

    def wait(self, engq, sem, val=None):
        engq.wait_ge(sem, self.at(sem) if val is None else val)

    def dbuf(self, name, nslots, channels=("y",)):
        if name not in self.dbufs:
            self.dbufs[name] = DBuf(self, name, nslots, channels)
        return self.dbufs[name]

    def fence(self):
        engs = [("tensor", self.T), ("vector", self.V), ("scalar", self.S),
                ("gpsimd", self.G), ("sync", self.Y)]
        for e, _ in engs:
            self.sem("bar_" + e)
        for s in sorted(self.hwdge_sems):
            self.wait(self.G, s)
        marks = {}
        for e, q in engs:
            q.drain().then_inc("bar_" + e, 1)
            marks[e] = self.at("bar_" + e)
        for e, q in engs:
            for o, _ in engs:
                if o != e:
                    q.wait_ge("bar_" + o, marks[o])

    def finalize(self):
        nc = self.nc
        block = self.es.enter_context(nc.Block())
        per = {e: [s for s in self.steps if s[0] == e] for e in self.ENGS}
        sems = self.sems

        def run(eng_obj, lst):
            for _, meth, a, kw, rec in lst:
                if meth == "wait_ge":
                    eng_obj.wait_ge(sems[a[0]], a[1])
                    continue
                instr = getattr(eng_obj, meth)(*a, **kw)
                for s, n in rec.incs:
                    instr.then_inc(sems[s], n)

        @block.tensor
        def _(e):
            run(e, per["tensor"])

        @block.vector
        def _(e):
            run(e, per["vector"])

        @block.scalar
        def _(e):
            run(e, per["scalar"])

        @block.gpsimd
        def _(e):
            run(e, per["gpsimd"])

        @block.sync
        def _(e):
            run(e, per["sync"])

        return nc


class Ctx:
    pass


def bank_split(k):
    return (k.tok // 2) * 4 >= 2048


def mm3(k, ps, whi_c, wlo_c, m1, m2, start, stop):
    T = k.T
    T.matmul(ps, whi_c, m1, start=start, stop=False)
    T.matmul(ps, wlo_c, m1, start=False, stop=False)
    return T.matmul(ps, whi_c, m2, start=False, stop=stop)


def emit_ln_split(k, cx, src_dram, ring, nchunks, out1, out2, dim,
                  gamma_cols=None, beta_cols=None, tag="", fout=None,
                  fstage=None):
    T, V, S, Y = k.T, k.V, k.S, k.Y
    tok, tokh = k.tok, k.tok // 2
    sA, sB = cx.lnsc1, cx.lnsc2
    mm_s, ev_s, sc_s, ld_s = ("lnmm" + tag, "lnev" + tag, "lnsc" + tag,
                              "lnld" + tag)
    for s in (mm_s, ev_s, sc_s):
        k.sem(s)
    db = k.dbuf(ld_s, 2)

    def load(c, con_eng):
        i = db.nexti()
        db.pre_load(Y, i)
        t = ring[:, (i % 2) * tok:(i % 2) * tok + tok]
        db.load_dma(Y, i, out=t, in_=src_dram[c * 128:(c + 1) * 128, :])
        db.ready(con_eng, i)
        return t, i

    iring = []
    for c in range(nchunks):
        x_c, i = load(c, V)
        iring.append(i)
        db.ready(T, i)
        sq = (sA if c % 2 == 0 else sB)[:, 0:tok]
        if c >= 2:
            db.wait_freed(V, iring[c - 1])
        V.tensor_tensor(sq, x_c, x_c, ALU.mult).then_inc(ev_s, 1)
        k.wait(T, ev_s)
        for half in range(2):
            hs = slice(half * tokh, (half + 1) * tokh)
            first = (c == 0 and (half == 0 or bank_split(k)))
            last = (c == nchunks - 1 and (half == 1 or bank_split(k)))
            T.matmul(cx.ps_pv[0:1, hs], cx.ones_col[:, :], x_c[:, hs],
                     start=first, stop=last)
            mmi = T.matmul(cx.ps_sg[0:1, hs], cx.ones_col[:, :], sq[:, hs],
                           start=first, stop=last)
        db.free(mmi, i)
    db.wait_freed(V, iring[-1])
    mu = cx.row(0)
    ex2 = cx.row(1)
    mmu = cx.row(2)
    rsg = cx.row(3)
    rsg2 = cx.row(4)
    V.tensor_scalar(mu, cx.ps_pv[0:1, 0:tok], 1.0 / dim, None, ALU.mult)
    V.tensor_scalar(ex2, cx.ps_sg[0:1, 0:tok], 1.0 / dim, None, ALU.mult)
    V.tensor_tensor(mmu, mu, mu, ALU.mult)
    V.tensor_tensor(ex2, ex2, mmu, ALU.subtract)
    V.tensor_scalar(ex2, ex2, LN_EPS, None, ALU.add)
    V.reciprocal(rsg, ex2)
    V.drain().then_inc(ev_s, 1)
    k.wait(S, ev_s)
    S.activation(rsg2, rsg, AF.Sqrt).then_inc(sc_s, 1)
    k.wait(T, sc_s)
    for half in range(2):
        sl = slice(half * tokh, (half + 1) * tokh)
        T.matmul(cx.ps_mm0[:, sl], cx.ones_row[:, :], mu[:, sl],
                 start=True, stop=True)
        bci = T.matmul(cx.ps_mm1[:, sl], cx.ones_row[:, :], rsg2[:, sl],
                       start=True, stop=True)
    bci.then_inc(mm_s, 1)
    k.wait(V, mm_s)
    mu_b = cx.bcA[:, 0:tok]
    rs_b = cx.bcB[:, 0:tok]
    V.tensor_copy(mu_b, cx.ps_mm0[:, 0:tok])
    V.tensor_copy(rs_b, cx.ps_mm1[:, 0:tok]).then_inc(ev_s, 1)
    k.wait(T, ev_s)
    st_s = "lnst" + tag
    k.sem(st_s)
    for c0 in range(0, nchunks, 2):
        cs = [c for c in (c0, c0 + 1) if c < nchunks]
        grp = []
        for c in cs:
            x_c, i = load(c, V)
            grp.append((x_c, (sA if c % 2 == 0 else sB)[:, 0:tok], c, i))
        for xc, t, c, i in grp:
            db.free(V.tensor_tensor(t, xc, mu_b, ALU.subtract), i)
        for xc, t, c, i in grp:
            V.tensor_tensor(t, t, rs_b, ALU.mult)
        if gamma_cols is not None:
            for xc, t, c, i in grp:
                V.tensor_scalar(t, t, gamma_cols(c), beta_cols(c),
                                ALU.mult, ALU.add)
        if fout is not None:
            for xc, t, c, i in grp:
                k.wait(V, st_s)
                stg = fstage[:, (c % 2) * tok:(c % 2) * tok + tok]
                V.tensor_copy(stg, t).then_inc(ev_s, 1)
                k.wait(Y, ev_s)
                Y.dma_start(out=fout[c * 128:(c + 1) * 128, :],
                            in_=stg).then_inc(st_s, 16)
            continue
        for xc, t, c, i in grp:
            V.tensor_copy(out1(c), t)
        for xc, t, c, i in grp:
            V.tensor_tensor(out2(c), t, out1(c),
                            ALU.subtract).then_inc(ev_s, 1)
    if fout is not None:
        k.wait(Y, st_s)
    return ev_s


def _dbg_fm(k, cx, name, tiles, nchunks):
    V, Y = k.V, k.Y
    tok = k.tok
    d = k.dbg(name, [nchunks * 128, tok], F32)
    s = "dbgs"
    k.sem(s)
    for c in range(nchunks):
        t = (cx.lnsc1 if c % 2 == 0 else cx.lnsc2)[:, 0:tok]
        if c >= 2:
            k.wait(V, s)
        a, b = tiles(c)
        V.tensor_tensor(t, a, b, ALU.add).then_inc("ev", 1)
        k.wait(Y, "ev")
        Y.dma_start(out=d[c * 128:(c + 1) * 128, :], in_=t).then_inc(s, 16)
    k.wait(V, s)


def _emit_trig_tables(k, cx, tokoff):
    T, V, S, G, Y = k.T, k.V, k.S, k.G, k.Y
    tok = k.tok
    tokh = tok // 2
    k.sem("tr")
    k.sem("trs")
    G.iota(cx.row(5), [[1, tok]], base=0, channel_multiplier=0,
           allow_small_or_imprecise_dtypes=True)
    G.iota(cx.row(6)[0:1, 0:64], [[0, 2], [1, 32]], base=0, channel_multiplier=0,
           allow_small_or_imprecise_dtypes=True).then_inc("tr", 1)
    Y.dma_start(out=cx.row(7)[0:1, 0:1], in_=tokoff[:, :]).then_inc("tr", 16)
    k.wait(V, "tr")
    trow = cx.row(5)
    V.tensor_scalar(trow, trow, cx.row(7)[0:1, 0:1], None, ALU.add)
    V.drain().then_inc("tr", 1)
    LNC1, LNC2 = 0.2878231406211853, -3.9969294540753708e-09
    V.tensor_scalar(cx.row(7)[0:1, 0:64], cx.row(6)[0:1, 0:64], -LNC1, None,
                    ALU.mult)
    V.scalar_tensor_tensor(cx.row(6)[0:1, 0:64], cx.row(6)[0:1, 0:64], -LNC2,
                           cx.row(7)[0:1, 0:64], ALU.mult, ALU.add)
    V.drain().then_inc("tr", 1)
    k.wait(S, "tr")
    S.activation(cx.row(6)[0:1, 0:64], cx.row(6)[0:1, 0:64],
                 AF.Exp).then_inc("trs", 1)
    k.wait(T, "trs")
    k.wait(T, "tr")
    for half in range(2):
        hs = slice(half * tokh, (half + 1) * tokh)
        r = T.matmul(cx.ps_pv[0:64, hs], cx.row(6)[0:1, 0:64],
                     trow[:, hs], start=True, stop=True)
    r.then_inc("tr", 1)
    k.wait(V, "tr")
    ang = cx.lnsc1[0:64, 0:tok]
    u = cx.lnsc2[0:64, 0:tok]
    w = cx.bcA[0:64, 0:tok]
    cmp = cx.bcB[0:64, 0:tok]
    V.tensor_copy(ang, cx.ps_pv[0:64, 0:tok])
    V.tensor_scalar(u, ang, 1.0 / (2 * PI), 0.5, ALU.mult, ALU.add)
    V.tensor_copy(w, u)
    for j in range(11, -1, -1):
        V.tensor_scalar(cmp, w, float(2 ** j), None, ALU.is_ge)
        V.scalar_tensor_tensor(w, cmp, -float(2 ** j), w, ALU.mult, ALU.add)
    V.tensor_tensor(u, u, w, ALU.subtract)
    CW1, CW2, CW3 = 6.28125, 0.0019350051879882812, 3.0199160505617328e-07
    V.scalar_tensor_tensor(w, u, -CW1, ang, ALU.mult, ALU.add)
    V.scalar_tensor_tensor(w, u, -CW2, w, ALU.mult, ALU.add)
    V.scalar_tensor_tensor(u, u, -CW3, w, ALU.mult, ALU.add)
    for _ in range(2):
        V.tensor_scalar(cmp, u, PI, None, ALU.is_gt)
        V.scalar_tensor_tensor(u, cmp, -2 * PI, u, ALU.mult, ALU.add)
        V.tensor_scalar(cmp, u, -PI, None, ALU.is_lt)
        V.scalar_tensor_tensor(u, cmp, 2 * PI, u, ALU.mult, ALU.add)
    ca = cx.lnsc1[64:128, 0:tok]
    V.tensor_scalar(ca, u[0:64, 0:tok].base_shift(64) if False else u, PI / 2,
                    None, ALU.add) if False else None
    ca = w
    V.tensor_scalar(ca, u, PI / 2, None, ALU.add)
    V.tensor_scalar(cmp, ca, PI, None, ALU.is_gt)
    V.scalar_tensor_tensor(ca, cmp, -2 * PI, ca, ALU.mult, ALU.add)
    V.drain().then_inc("tr", 1)
    k.wait(S, "tr")
    cosd = cx.trig[0:64, 0:1024]
    sind = cx.trig[0:64, 1024:2048]
    S.activation(cosd[0:64, 0:tok], ca, AF.Sin)
    S.activation(sind[0:64, 0:tok], u, AF.Sin).then_inc("trs", 1)
    k.wait(V, "trs")
    V.tensor_scalar(sind[0:32, 0:tok], sind[0:32, 0:tok], -1.0, None, ALU.mult)
    V.drain().then_inc("tr", 1)
    k.wait(T, "tr")




def emit_attention(k, cx, af32, abf, tag, rope, q_scr, qn_scr, nga_out,
                   kga_outs, vga_outs, KROWS, cc0, out_writer):
    T, V, S, G, Y = k.T, k.V, k.S, k.G, k.Y
    tok = k.tok
    tokh = tok // 2
    NKC = tok // 128

    gn = af32.alloc(tok)
    k.wait(Y, "cc", val=cc0 + 7)
    Y.dma_start(out=gn[0:NCORES * NH, 0:tok], in_=nga_out[:, :]).then_inc("xld", 16)
    k.wait(V, "xld")
    V.tensor_reduce(cx.lnsc1[0:NCORES * NH, 0:1], gn[0:NCORES * NH, 0:tok],
                    AX.X, ALU.max)
    V.drain().then_inc("ev", 1)
    k.wait(T, "ev")
    T.transpose(cx.ps_sg[0:1, 0:128], cx.lnsc1[:, 0:1],
                cx.permt[:, :]).then_inc("mm", 1)
    k.wait(V, "mm")
    V.tensor_copy(cx.knrow[0:1, 0:128], cx.ps_sg[0:1, 0:128])
    for h in range(NH):
        V.tensor_reduce(cx.knrow[0:1, 128 + h:129 + h],
                        cx.knrow[0:1, h * NCORES:(h + 1) * NCORES], AX.X, ALU.max)
    V.drain().then_inc("ev", 1)
    k.wait(S, "ev")
    S.activation(cx.knrow[0:1, 128:128 + NH], cx.knrow[0:1, 128:128 + NH],
                 AF.Sqrt).then_inc("sc", 1)
    k.wait(V, "sc")
    k.wait(T, "sc")
    kn_s = cx.knrow[0:1, 128:128 + NH]

    nq = 4 if rope else 2
    qdb = k.dbuf("atqdb", 1)
    qslot = abf.alloc(nq * tok)
    nkv = 5 if rope else 4
    kvdb = k.dbuf("atkvdb", 2, channels=("y", "g", "s"))
    kvslots = [abf.alloc(nkv * tok) for _ in range(2)]
    pf32 = [af32.alloc(tok) for _ in range(3)]
    p1r = [abf.alloc(tok) for _ in range(3)]
    p2r = [abf.alloc(tok) for _ in range(3)]
    sigr = [af32.alloc(tok) for _ in range(2)]
    sigi = af32.alloc(tok)
    sd_s, pe_s, pp_s, pu_s, hf_s, mh_s = ("atsd", "atpe", "atpp", "atpu",
                                          "athf", "atmh")
    pc_s, pg_s, sg_s = "atpc", "atpg", "atsg"
    for s in (sd_s, pe_s, pp_s, pu_s, hf_s, mh_s, pc_s, pg_s, sg_s):
        k.sem(s)
    nsd, npe = k.at(sd_s), k.at(pe_s)
    nps, npu = k.at(pp_s), k.at(pu_s)
    npc, npg, nsg = k.at(pc_s), k.at(pg_s), k.at(sg_s)
    hf0 = k.at(hf_s)
    jloc = 0
    ev_sigi = [k.at("ev")]
    pend_kv = [None]
    pend_pv = []

    def emit_pv(pv):
        nonlocal npu
        k.wait(T, pp_s, val=pv["ppt"])
        if pv["first"] and pv["h"] > 0:
            k.wait(T, "ev")
        for qb in range(2):
            qsl = slice(qb * tokh, (qb + 1) * tokh)
            rpv = T.matmul(cx.ps_pv[:, qsl], pv["v1"][:, pv["kcs"]],
                           pv["p1"][:, qsl],
                           start=(pv["first"] and (qb == 0 or bank_split(k))),
                           stop=False)
            if qb == 0 and pend_kv[0] is not None:
                kvdb.free(rpv, pend_kv[0])
                pend_kv[0] = None
            T.matmul(cx.ps_pv[:, qsl], pv["v2"][:, pv["kcs"]],
                     pv["p1"][:, qsl], start=False, stop=False)
            r3 = T.matmul(cx.ps_pv[:, qsl], pv["v1"][:, pv["kcs"]],
                          pv["p2"][:, qsl], start=False,
                          stop=(pv["last"] and (qb == 1 or bank_split(k))))
        r3.then_inc(pu_s, 1)
        npu += 1
        if pv["endrr"]:
            pend_kv[0] = pv["gi"]

    mr2 = [cx.mrow, cx.mrow2]
    mh_ev = [0] * (NH + 1)
    head_pu0 = []

    def mprep(hh):
        if hh > 0:
            k.wait(Y, "ev", val=mh_ev[hh - 1])
        Y.dma_start(out=cx.bcB[0:1, 0:tok],
                    in_=qn_scr[hh:hh + 1, :]).then_inc(mh_s, 16)
        k.wait(S, mh_s)
        S.activation(cx.bcB[0:1, 0:tok], cx.bcB[0:1, 0:tok],
                     AF.Sqrt).then_inc("sc", 1)
        k.wait(V, "sc")
        if hh >= 2:
            k.wait(V, pu_s, val=head_pu0[hh - 1])
        V.tensor_scalar(mr2[hh % 2][0:1, 0:tok], cx.bcB[0:1, 0:tok],
                        kn_s[0:1, hh:hh + 1], -1.0, ALU.mult, ALU.mult)
        V.drain().then_inc("ev", 1)
        mh_ev[hh] = k.at("ev")

    for h in range(NH):
        head_pu0.append(npu)
        if h == 0:
            mprep(0)
        if h > 0 or hf0 > 0:
            k.wait(Y, hf_s, val=hf0 + h)
        k.wait(T, "ev", val=mh_ev[h])
        qi = qdb.nexti()
        qdb.pre_load(Y, qi)
        for kind in range(nq):
            qdb.load_dma(Y, qi, out=qslot[:, kind * tok:(kind + 1) * tok],
                         in_=q_scr[(h * 4 + kind) * 128:(h * 4 + kind + 1) * 128, :])
        qc1, qc2 = qslot[:, 0:tok], qslot[:, tok:2 * tok]
        if rope:
            qra, qrb = qslot[:, 2 * tok:3 * tok], qslot[:, 3 * tok:4 * tok]
        qdb.ready(T, qi)
        kidx_h0 = npu
        kidx_loc0 = jloc
        kga_out = kga_outs[h // 4]
        vga_out = vga_outs[h // 4]
        for rr in range(NCORES):
            gi = kvdb.nexti()
            ch, le = "y", Y
            g4 = h // 4
            k.wait(le, "cc", val=cc0 + (9 if g4 == 3 else 2 * (g4 + 1)))
            kvdb.pre_load(le, gi)
            kv = kvslots[gi % 2]
            kb = rr * (NH // 4) * KROWS + (h % 4) * KROWS
            kvdb.load_dma(le, gi, ch=ch, out=kv[:, 0:tok],
                          in_=kga_out[kb:kb + 128, :])
            kvdb.load_dma(le, gi, ch=ch, out=kv[:, tok:2 * tok],
                          in_=kga_out[kb + 128:kb + 256, :])
            if rope:
                kvdb.load_dma(le, gi, ch=ch, out=kv[0:64, 2 * tok:3 * tok],
                              in_=kga_out[kb + 256:kb + 320, :])
                kvdb.load_dma(le, gi, ch=ch, out=kv[64:128, 2 * tok:3 * tok],
                              in_=kga_out[kb + 320:kb + 384, :])
            vof = 3 * tok if rope else 2 * tok
            vb = rr * (NH // 4) * 2 * tok + (h % 4) * 2 * tok
            kvdb.load_dma(le, gi, ch=ch, out=kv[:, vof:vof + tok],
                          in_=vga_out[vb:vb + tok, :].rearrange("(t p) d -> p t d", p=128))
            kvdb.load_dma(le, gi, ch=ch, out=kv[:, vof + tok:vof + 2 * tok],
                          in_=vga_out[vb + tok:vb + 2 * tok, :].rearrange("(t p) d -> p t d", p=128))
            if gi >= 2:
                while pend_pv:
                    emit_pv(pend_pv.pop(0))
            kvdb.ready(T, gi)
            khi, klo = kv[:, 0:tok], kv[:, tok:2 * tok]
            if rope:
                krp = kv[:, 2 * tok:3 * tok]
            v1, v2 = kv[:, vof:vof + tok], kv[:, vof + tok:vof + 2 * tok]
            for kc in range(NKC):
                kidx = npe
                first = rr == 0 and kc == 0
                last = rr == NCORES - 1 and kc == NKC - 1
                ps = cx.ps_mm0 if kidx % 2 == 0 else cx.ps_mm1
                if kidx >= 2:
                    k.wait(T, pe_s, val=npe - 1)
                if h > 0 and jloc - kidx_loc0 < 2:
                    k.wait(T, "ev", val=ev_sigi[0])
                kcs = slice(kc * 128, (kc + 1) * 128)
                for qb in range(2):
                    qsl = slice(qb * tokh, (qb + 1) * tokh)
                    pss = ps[:, qsl]
                    T.matmul(pss, khi[:, kcs], qc1[:, qsl],
                             start=(qb == 0 or bank_split(k)), stop=False)
                    T.matmul(pss, klo[:, kcs], qc1[:, qsl], start=False, stop=False)
                    T.matmul(pss, khi[:, kcs], qc2[:, qsl], start=False, stop=False)
                    if rope:
                        T.matmul(pss, krp[:, kcs], qra[:, qsl], start=False, stop=False)
                        T.matmul(pss, krp[:, kcs], qrb[:, qsl], start=False, stop=False)
                    r2 = T.matmul(pss, cx.ones_rowb[:, :],
                                  mr2[h % 2][0:1, qsl],
                                  start=False, stop=(qb == 1 or bank_split(k)))
                r2.then_inc(sd_s, 1)
                nsd += 1
                if len(pend_pv) >= 2:
                    emit_pv(pend_pv.pop(0))
                k.wait(S, sd_s, val=nsd)
                pf = pf32[jloc % 3]
                p1, p2 = p1r[jloc % 3], p2r[jloc % 3]
                if jloc >= 3:
                    k.wait(S, pu_s, val=npu - 1)
                    k.wait(S, pg_s, val=npg - 2)
                S.activation(pf[:, 0:tok], ps[:, 0:tok], AF.Exp).then_inc(pe_s, 1)
                npe += 1
                k.wait(G, pe_s, val=npe)
                Rh = sigr[h % 2]
                if first:
                    if h >= 2:
                        k.wait(G, sg_s, val=nsg - 1)
                    G.tensor_copy(Rh[:, 0:tok], pf[:, 0:tok]).then_inc(pg_s, 1)
                else:
                    G.tensor_tensor(Rh[:, 0:tok], Rh[:, 0:tok], pf[:, 0:tok],
                                    ALU.add).then_inc(pg_s, 1)
                npg += 1
                k.wait(V, pe_s, val=npe)
                if jloc >= 3:
                    k.wait(V, pu_s, val=npu - 1)
                V.tensor_copy(p1[:, 0:tok], pf[:, 0:tok])
                V.tensor_tensor(p2[:, 0:tok], pf[:, 0:tok], p1[:, 0:tok],
                                ALU.subtract).then_inc(pp_s, 1)
                nps += 1
                pend_pv.append({"first": first, "last": last, "h": h,
                                "v1": v1, "v2": v2, "kcs": kcs,
                                "p1": p1, "p2": p2, "gi": gi,
                                "endrr": kc == NKC - 1, "ppt": nps})
                jloc += 1
            if rr == 1 and h + 1 < NH:
                mprep(h + 1)
        while pend_pv:
            emit_pv(pend_pv.pop(0))
        k.wait(T, pg_s, val=npg)
        k.wait(T, "ev")
        r5s = []
        for half in range(2):
            hs = slice(half * tokh, (half + 1) * tokh)
            r5s.append(T.matmul(cx.ps_sg[0:1, hs], cx.ones_col[:, :],
                                Rh[:, hs], start=True, stop=True))
        if pend_kv[0] is not None:
            kvdb.free(r5s[0], pend_kv[0])
            pend_kv[0] = None
        r5s[1].then_inc(sg_s, 1)
        nsg += 1
        k.wait(V, sg_s, val=nsg)
        V.reciprocal(cx.row(7), cx.ps_sg[0:1, 0:tok]).then_inc("ev", 1)
        k.wait(T, "ev")
        bb = cx.ps_mm1 if npe % 2 == 0 else cx.ps_mm0
        r4s = []
        for half in range(2):
            hs = slice(half * tokh, (half + 1) * tokh)
            r4s.append(T.matmul(bb[:, hs], cx.ones_row[:, :],
                                cx.row(7)[0:1, hs], start=True, stop=True))
        qdb.free(r4s[0], qi)
        r4s[1].then_inc(hf_s, 1)
        k.wait(V, hf_s)
        V.tensor_copy(sigi[:, 0:tok], bb[:, 0:tok]).then_inc("ev", 1)
        ev_sigi[0] = k.at("ev")
        out_writer(h, cx.ps_pv, sigi)


class WQState:
    pass


def wq_setup(k, cx, shards, dims):
    T, V, S, G, Y = k.T, k.V, k.S, k.G, k.Y
    st = WQState()
    st.ring = k.sbuf("wqring", [128, 1024], F32)
    st.tq = k.sbuf("wqtq", [128, 1024], BF16)
    st.acc = k.sbuf("wqacc", [128, 128], F32)
    st.dbs = k.dbuf("wqs", 1, channels=("s",))
    st.shards, st.dims = shards, dims
    st.nst = 0
    st.scale_units = []
    st.quant_units = []
    st.bases = []
    k.sem("wqev")
    base = 0
    for i, ((wsh, tg, tfull), (nr, ncl, mode)) in enumerate(zip(shards, dims)):
        nr8 = nr // NCORES
        nrc, cw = nr8 // 128, min(ncl, 1024)
        ncc = max(ncl // 1024, 1)
        ntl = nrc * ncc
        st.bases.append((base, ntl))
        for rc in range(nrc):
            for cc in range(ncc):
                def su(wsh=wsh, rc=rc, cc=cc, cw=cw, col=base + rc * ncc + cc):
                    ti = st.dbs.nexti()
                    st.dbs.pre_load(S, ti)
                    t = st.ring[:, 0:cw]
                    st.dbs.load_dma(S, ti, ch="s", out=t,
                                    in_=wsh[rc * 128:(rc + 1) * 128,
                                            cc * cw:(cc + 1) * cw])
                    st.dbs.ready(V, ti)
                    st.dbs.free(
                        V.tensor_reduce(st.acc[:, col:col + 1], t, AX.X,
                                        ALU.add, apply_absolute_value=True),
                        ti)
                st.scale_units.append(su)

                def qu(wsh=wsh, tg=tg, rc=rc, cc=cc, cw=cw, i=i):
                    ti = st.dbs.nexti()
                    st.dbs.pre_load(S, ti)
                    t = st.ring[:, 0:cw]
                    st.dbs.load_dma(S, ti, ch="s", out=t,
                                    in_=wsh[rc * 128:(rc + 1) * 128,
                                            cc * cw:(cc + 1) * cw])
                    st.dbs.ready(V, ti)
                    if st.nst >= 1:
                        k.wait(V, "wqst", val=16 * st.nst)
                    rcol = cx.wq[:, 2 * i + 1:2 * i + 2]
                    sAq = cx.bcA[:, 0:cw]
                    sBq = cx.bcB[:, 0:cw]
                    to = st.tq[:, 0:cw]
                    V.tensor_scalar(sAq, t, rcol, 0.5, ALU.mult, ALU.is_ge)
                    st.dbs.free(V.tensor_scalar(sBq, t, rcol, -0.5,
                                                ALU.mult, ALU.is_le), ti)
                    V.tensor_tensor(to, sAq, sBq,
                                    ALU.subtract).then_inc("wqev", 1)
                    k.wait(Y, "wqev")
                    Y.dma_start(out=tg[rc * 128:(rc + 1) * 128,
                                       cc * cw:(cc + 1) * cw],
                                in_=to).then_inc("wqst", 16)
                    st.nst += 1
                st.quant_units.append(qu)
        base += ntl + 1
    return st


def wq_scales_finish(k, cx, st, wqsum_in, wqsum_out):
    T, V, S, G, Y = k.T, k.V, k.S, k.G, k.Y
    dims = st.dims
    prow = cx.row(4)
    G.memset(prow[0:1, 7:8], 1.0).then_inc("gp", 1)
    for base, ntl in st.bases:
        V.tensor_reduce(st.acc[:, base + ntl:base + ntl + 1],
                        st.acc[:, base:base + ntl], AX.X, ALU.add)
    V.drain().then_inc("ev", 1)
    k.wait(T, "ev")
    for i, (base, ntl) in enumerate(st.bases):
        T.matmul(cx.ps_sg[0:1, i:i + 1], cx.ones_col[:, :],
                 st.acc[:, base + ntl:base + ntl + 1],
                 start=True, stop=True).then_inc("mm", 1)
    k.wait(V, "mm")
    for i in range(7):
        V.tensor_copy(prow[0:1, i:i + 1], cx.ps_sg[0:1, i:i + 1])
    V.drain().then_inc("ev", 1)
    k.wait(Y, "ev")
    k.wait(Y, "gp")
    Y.dma_start(out=wqsum_in[0:1, 0:8], in_=prow[0:1, 0:8]).then_inc("st", 16)
    k.wait(G, "st")
    G.collective_compute("AllGather", ALU.bypass,
                         replica_groups=[list(range(NCORES))],
                         ins=[wqsum_in.opt()],
                         outs=[wqsum_out.opt()]).then_inc("ccq", 1)
    k.wait(Y, "ccq")
    Y.dma_start(out=cx.bcA[0:8, 0:8], in_=wqsum_out[:, :]).then_inc("xld", 16)
    k.wait(T, "xld")
    T.matmul(cx.ps_sg[0:1, 0:8], cx.ones_col[0:8, 0:1], cx.bcA[0:8, 0:8],
             start=True, stop=True).then_inc("mm", 1)
    for i in range(8):
        v = 1.0 / (dims[i][0] * dims[i][1]) if i < len(dims) else 1.0
        G.memset(cx.row(5)[0:1, i:i + 1], v)
    G.drain().then_inc("gp", 1)
    k.wait(V, "mm")
    k.wait(V, "gp")
    m8 = cx.row(6)[0:1, 0:16]
    V.tensor_tensor(m8[0:1, 0:8], cx.ps_sg[0:1, 0:8], cx.row(5)[0:1, 0:8],
                    ALU.mult)
    V.tensor_scalar(m8[0:1, 0:3], m8[0:1, 0:3], 1e-8, None, ALU.add)
    V.tensor_scalar(m8[0:1, 3:7], m8[0:1, 3:7], 1e-5, None, ALU.max)
    V.reciprocal(m8[0:1, 8:16], m8[0:1, 0:8])
    V.drain().then_inc("ev", 1)
    k.wait(T, "ev")
    T.matmul(cx.ps_mm0[:, 0:16], cx.ones_row[:, :], m8[0:1, 0:16],
             start=True, stop=True).then_inc("mm", 1)
    k.wait(V, "mm")
    for i in range(7):
        V.tensor_copy(cx.wq[:, 2 * i:2 * i + 1], cx.ps_mm0[:, i:i + 1])
        V.tensor_copy(cx.wq[:, 2 * i + 1:2 * i + 2],
                      cx.ps_mm0[:, 8 + i:9 + i])
    V.drain().then_inc("ev", 1)
    k.wait(T, "ev")
    k.wait(G, "ev")
    k.wait(Y, "ev")


def emit_ffn(k, cx, af32, abf, xn1, xn2, x_in, x_out, t_up, t_gate, t_down,
             ffnacc, tag):
    T, V, S, G, Y = k.T, k.V, k.S, k.G, k.Y
    k.wait(Y, "ccq")
    tok = k.tok
    tokh = tok // 2
    ET = 512
    NET = DE // ET
    ECT = ET // 128

    af32.push()
    abf.push()
    af32.push()
    ring = af32.alloc(2 * tok)
    emit_ln_split(k, cx, x_in, ring, DC,
                  lambda c: xn1[:, c * tok:(c + 1) * tok],
                  lambda c: xn2[:, c * tok:(c + 1) * tok], D, tag="ln")
    af32.pop()
    k.fence()

    su_col = cx.wq[:, 0:1]
    sg_col = cx.wq[:, 2:3]
    sd_col = cx.wq[:, 4:5]
    hid1 = abf.alloc(ET * tok // 128)
    hid2 = abf.alloc(ET * tok // 128)
    tdb = k.dbuf("fftdb", 2, channels=("y", "g"))
    tslots = [abf.alloc(2 * DC * 128) for _ in range(2)]
    ddb = k.dbuf("ffddb", 2, channels=("s",))
    dslots = [abf.alloc(ECT * 128) for _ in range(2)]
    af32.push()
    su_sb2 = [af32.alloc(tok), af32.alloc(tok)]
    hf = af32.alloc(tok)
    stg = af32.alloc(2 * tok)
    hs_s, rs_s = "ffhs", "ffrs"
    for s in (hs_s, rs_s):
        k.sem(s)
    for j in range(4):
        k.sem(f"ffdnc{j}")
    nhs = k.at(hs_s)
    gec0 = 1 if k.at(hs_s) > 0 else 0

    for et in range(NET):
        for ec in range(ECT):
            gec = tdb.nexti()
            e0 = et * ET + ec * 128
            tdb.pre_load(Y, gec)
            tdb.pre_load(G, gec)
            ts_ = tslots[gec % 2]
            tdb.load_dma(Y, gec, ch="y", out=ts_[:, 0:DC * 128].rearrange("p (c e) -> p c e", e=128),
                         in_=t_up[:, e0:e0 + 128].rearrange("(c p) e -> p c e", p=128))
            tdb.load_dma(G, gec, ch="g", out=ts_[:, DC * 128:2 * DC * 128].rearrange("p (c e) -> p c e", e=128),
                         in_=t_gate[:, e0:e0 + 128].rearrange("(c p) e -> p c e", p=128))
            tdb.ready(T, gec)
            tu = ts_[:, 0:DC * 128]
            tg = ts_[:, DC * 128:2 * DC * 128]
            if ec == 0:
                k.wait(T, "ev")
            if gec >= 1 or gec0:
                k.wait(T, hs_s, val=nhs)
            for c in range(DC):
                cs = slice(c * 128, (c + 1) * 128)
                for half in range(2):
                    hsl = slice(half * tokh, (half + 1) * tokh)
                    st0 = (c == 0 and (half == 0 or bank_split(k)))
                    sp0 = (c == DC - 1 and (half == 1 or bank_split(k)))
                    T.matmul(cx.ps_mm0[:, hsl], tu[:, cs],
                             xn1[:, c * tok + half * tokh:c * tok + half * tokh + tokh],
                             start=st0, stop=False)
                    ru = T.matmul(cx.ps_mm0[:, hsl], tu[:, cs],
                                  xn2[:, c * tok + half * tokh:c * tok + half * tokh + tokh],
                                  start=False, stop=sp0)
            ru.then_inc("mm", 1)
            su_sb = su_sb2[gec % 2]
            k.wait(S, "mm")
            S.activation(su_sb[:, 0:tok], cx.ps_mm0[:, 0:tok], AF.Silu,
                         scale=su_col).then_inc(hs_s, 1)
            nhs += 1
            k.wait(T, "ev")
            for c in range(DC):
                cs = slice(c * 128, (c + 1) * 128)
                for half in range(2):
                    hsl = slice(half * tokh, (half + 1) * tokh)
                    st0 = (c == 0 and (half == 0 or bank_split(k)))
                    sp0 = (c == DC - 1 and (half == 1 or bank_split(k)))
                    T.matmul(cx.ps_mm1[:, hsl], tg[:, cs],
                             xn1[:, c * tok + half * tokh:c * tok + half * tokh + tokh],
                             start=st0, stop=False)
                    rg = T.matmul(cx.ps_mm1[:, hsl], tg[:, cs],
                                  xn2[:, c * tok + half * tokh:c * tok + half * tokh + tokh],
                                  start=False, stop=sp0)
            tdb.free(rg, gec)
            tdb.wait_freed(V, gec)
            k.wait(V, hs_s, val=nhs)
            h1 = hid1[:, ec * tok:(ec + 1) * tok]
            h2 = hid2[:, ec * tok:(ec + 1) * tok]
            V.scalar_tensor_tensor(hf[:, 0:tok], cx.ps_mm1[:, 0:tok], sg_col,
                                   su_sb[:, 0:tok], ALU.mult, ALU.mult)
            V.tensor_copy(h1, hf[:, 0:tok])
            V.tensor_tensor(h2, hf[:, 0:tok], h1, ALU.subtract).then_inc("ev", 1)
        k.wait(T, "ev")
        ev_d0 = k.at("ev")
        for j in range(DC):
            dj = ddb.nexti()
            ddb.pre_load(S, dj)
            ds = dslots[dj % 2]
            ddb.load_dma(S, dj, ch="s", out=ds[:, 0:ECT * 128].rearrange("p (a e) -> p a e", e=128),
                         in_=t_down[et * ET:(et + 1) * ET, j * 128:(j + 1) * 128].rearrange("(a p) e -> p a e", p=128))
            ddb.ready(T, dj)
            ps_d = cx.ps_pv if j % 2 == 0 else cx.ps_mm0
            if j >= 2:
                k.wait(T, "ev", val=ev_d0 + j - 1)
            else:
                k.wait(T, "ev")
            for ec in range(ECT):
                ecs = slice(ec * 128, (ec + 1) * 128)
                for half in range(2):
                    hsl = slice(half * tokh, (half + 1) * tokh)
                    st0 = (ec == 0 and (half == 0 or bank_split(k)))
                    sp0 = (ec == ECT - 1 and (half == 1 or bank_split(k)))
                    T.matmul(ps_d[:, hsl], ds[:, ecs],
                             hid1[:, ec * tok + half * tokh:ec * tok + half * tokh + tokh],
                             start=st0, stop=False)
                    rd = T.matmul(ps_d[:, hsl], ds[:, ecs],
                                  hid2[:, ec * tok + half * tokh:ec * tok + half * tokh + tokh],
                                  start=False, stop=sp0)
            ddb.free(rd, dj)
            ddb.wait_freed(V, dj)
            if et > 0 or j >= 2:
                k.wait(V, f"ffdnc{(j - 2) % 4}")
            so = stg[:, (j % 2) * tok:(j % 2) * tok + tok]
            V.tensor_scalar(so, ps_d[:, 0:tok], sd_col, None,
                            ALU.mult).then_inc("ev", 1)
            k.wait(G, "ev")
            if et > 0:
                k.wait(G, f"ffdnc{j % 4}")
            G.dma_start(out=ffnacc[j * 128:(j + 1) * 128, :], in_=so,
                        accum_op=(ALU.add if et > 0 else ALU.bypass)).then_inc(f"ffdnc{j % 4}", 16)
    for j in range(4):
        k.wait(Y, f"ffdnc{j}")
    af32.pop()
    k.fence()
    xres = af32.alloc(2 * tok)
    xo = af32.alloc(2 * tok)
    xdb = k.dbuf("ffxdb", 2)
    k.sem(rs_s)
    for j in range(DC):
        xi = xdb.nexti()
        xdb.pre_load(Y, xi)
        xdb.load_dma(Y, xi, out=xres[:, (j % 2) * tok:(j % 2) * tok + tok],
                     in_=x_in[j * 128:(j + 1) * 128, :])
        xdb.load_dma(Y, xi, out=xo[:, (j % 2) * tok:(j % 2) * tok + tok],
                     in_=ffnacc[j * 128:(j + 1) * 128, :])
        xdb.ready(V, xi)
        k.wait(V, rs_s)
        vv = V.tensor_tensor(xo[:, (j % 2) * tok:(j % 2) * tok + tok],
                             xo[:, (j % 2) * tok:(j % 2) * tok + tok],
                             xres[:, (j % 2) * tok:(j % 2) * tok + tok], ALU.add)
        xdb.free(vv, xi)
        xdb.wait_freed(Y, xi)
        Y.dma_start(out=x_out[j * 128:(j + 1) * 128, :],
                    in_=xo[:, (j % 2) * tok:(j % 2) * tok + tok]).then_inc(rs_s, 16)
    k.wait(Y, rs_s)
    abf.pop()
    af32.pop()
    k.fence()


def emit_actquant(k, cx, af32, abf, src_dram, xq1, gb, tag):
    T, V, S, G, Y = k.T, k.V, k.S, k.G, k.Y
    tok = k.tok
    tokh = tok // 2
    ring = af32.alloc(2 * tok)
    sb = af32.alloc(tok)
    db = k.dbuf("lnldln", 2)

    def load(c, eng):
        i = db.nexti()
        db.pre_load(Y, i)
        t = ring[:, (i % 2) * tok:(i % 2) * tok + tok]
        db.load_dma(Y, i, out=t, in_=src_dram[c * 128:(c + 1) * 128, :])
        db.ready(eng, i)
        return t, i

    Ma, Mb = cx.lnsc1[:, 0:tok], cx.lnsc2[:, 0:tok]
    for c in range(DC):
        x_c, i = load(c, S)
        S.activation(x_c, x_c, AF.Abs).then_inc("sc", 1)
        k.wait(V, "sc")
        M = Ma if c % 2 == 0 else Mb
        if c < 2:
            vmx = V.tensor_copy(M, x_c)
        else:
            vmx = V.tensor_tensor(M, M, x_c, ALU.max)
        db.free(vmx, i)
    V.tensor_tensor(cx.bcA[:, 0:tok], Ma, Mb, ALU.max)
    V.drain().then_inc("ev", 1)
    k.wait(T, "ev")
    nb = tok // 128
    for b in range(nb):
        T.transpose(cx.ps_mm0[:, b * 128:(b + 1) * 128],
                    cx.bcA[:, b * 128:(b + 1) * 128],
                    cx.identt[:, :]).then_inc("mm", 1)
    k.wait(V, "mm")
    for b in range(nb):
        V.tensor_reduce(cx.bcB[:, b:b + 1], cx.ps_mm0[:, b * 128:(b + 1) * 128],
                        AX.X, ALU.max)
    V.drain().then_inc("ev", 1)
    k.wait(T, "ev")
    T.transpose(cx.ps_mm1[0:nb, 0:128], cx.bcB[:, 0:nb],
                cx.identt[:, :]).then_inc("mm", 1)
    k.wait(V, "mm")
    V.tensor_copy(cx.lnsc1[0:nb, 0:128], cx.ps_mm1[0:nb, 0:128])
    V.drain().then_inc("ev", 1)
    k.wait(Y, "ev")
    aq_s = "st"
    k.sem(aq_s)
    Y.dma_start(out=cx.row(5), in_=cx.lnsc1[0:nb, 0:128]).then_inc(aq_s, 16)
    k.wait(V, aq_s)
    mrowq = cx.row(5)
    srow = cx.row(6)
    grow = cx.row(7)
    V.tensor_scalar(mrowq, mrowq, 1e-5, None, ALU.max)
    V.drain()
    V.reciprocal(srow, mrowq)
    V.tensor_scalar(grow, mrowq, 1.0 / 127.0, None, ALU.mult)
    V.drain()
    V.tensor_scalar(srow, srow, 127.0, None, ALU.mult)
    V.drain().then_inc("ev", 1)
    k.wait(T, "ev")
    for half in range(2):
        hs = slice(half * tokh, (half + 1) * tokh)
        T.matmul(cx.ps_mm0[:, hs], cx.ones_row[:, :], srow[0:1, hs],
                 start=True, stop=True)
        rb = T.matmul(cx.ps_mm1[:, hs], cx.ones_row[:, :], grow[0:1, hs],
                      start=True, stop=True)
    rb.then_inc("mm", 1)
    k.wait(V, "mm")
    V.tensor_copy(sb[:, 0:tok], cx.ps_mm0[:, 0:tok])
    V.tensor_copy(gb[:, 0:tok], cx.ps_mm1[:, 0:tok]).then_inc("ev", 1)
    k.wait(T, "ev")
    for c0 in range(0, DC, 2):
        grp = []
        for c in (c0, c0 + 1):
            x_c, i = load(c, V)
            w = (cx.lnsc1 if c % 2 == 0 else cx.lnsc2)[:, 0:tok]
            cmp = (cx.bcA if c % 2 == 0 else cx.bcB)[:, 0:tok]
            grp.append((x_c, w, cmp, c, i))
        for x_c, w, cmp, c, i in grp:
            V.tensor_tensor(x_c, x_c, sb[:, 0:tok], ALU.mult)
        for x_c, w, cmp, c, i in grp:
            V.tensor_scalar(x_c, x_c, 128.5, None, ALU.add)
        for x_c, w, cmp, c, i in grp:
            V.tensor_copy(w, x_c)
        for j in range(7, -1, -1):
            for x_c, w, cmp, c, i in grp:
                V.tensor_scalar(cmp, w, float(2 ** j), None, ALU.is_ge)
            for x_c, w, cmp, c, i in grp:
                V.scalar_tensor_tensor(w, cmp, -float(2 ** j), w,
                                       ALU.mult, ALU.add)
        for x_c, w, cmp, c, i in grp:
            V.tensor_tensor(x_c, x_c, w, ALU.subtract)
        for x_c, w, cmp, c, i in grp:
            db.free(V.tensor_scalar(xq1(c), x_c, -128.0, None, ALU.add), i)


def emit_attn_b(k, cx, af32, abf, x_in, x_out, xnb_scr, aob_scr,
                tb_q, tb_k, tb_v, tb_o, q_scr, qn_scr,
                kga_ins, kga_outs, vga_ins, vga_outs, nga_in, nga_out, KROWS,
                xn1, xn2):
    T, V, S, G, Y = k.T, k.V, k.S, k.G, k.Y
    tok = k.tok
    tokh = tok // 2
    NKC = tok // 128
    kgb0, vgb0 = k.at("kga"), k.at("vga")

    af32.push()
    abf.push()
    ring = af32.alloc(2 * tok)
    fstage = af32.alloc(2 * tok)
    emit_ln_split(k, cx, x_in, ring, DC, None, None, D, tag="ln",
                  fout=xnb_scr, fstage=fstage)
    k.fence()
    af32.pop()
    abf.pop()

    af32.push()
    abf.push()
    xq = xn1
    gb = af32.alloc(tok)
    emit_actquant(k, cx, af32, abf, xnb_scr,
                  lambda c: xq[:, c * tok:(c + 1) * tok], gb, "bx")
    k.fence()

    V.tensor_scalar(cx.wq[:, 14:15], cx.wq[:, 6:7], BSCALE, None, ALU.mult)
    V.drain().then_inc("ev", 1)
    k.wait(T, "ev")
    mq_col = cx.wq[:, 14:15]
    mk_col = cx.wq[:, 8:9]
    mv_col = cx.wq[:, 10:11]

    uc = af32.alloc(tok)
    sq1 = af32.alloc(tok)
    bq1, bq2 = abf.alloc(tok), abf.alloc(tok)
    bk1, bk2 = abf.alloc(tok), abf.alloc(tok)
    bv1, bv2 = abf.alloc(tok), abf.alloc(tok)
    tbdb = k.dbuf("tbdb", 2, channels=("y", "g", "s"))
    tbslots = [abf.alloc(3 * DC * 128) for _ in range(2)]
    nbn = [0]
    pass

    def int_block(tb_dram, tslice, j, ps):
        for c in range(DC):
            cs = slice(c * 128, (c + 1) * 128)
            for half in range(2):
                hsl = slice(half * tokh, (half + 1) * tokh)
                r = T.matmul(ps[:, hsl], tslice[:, cs],
                             xq[:, c * tok + half * tokh:c * tok + half * tokh + tokh],
                             start=(c == 0 and (half == 0 or bank_split(k))),
                             stop=(c == DC - 1 and (half == 1 or bank_split(k))))
        return r

    def bnorm(src_ap, dst_row):
        V.tensor_tensor(sq1[:, 0:tok], src_ap, src_ap, ALU.mult).then_inc("ev", 1)
        k.wait(T, "ev")
        for half in range(2):
            hs = slice(half * tokh, (half + 1) * tokh)
            r = T.matmul(cx.ps_sg[0:1, hs], cx.ones_col[:, :], sq1[:, hs],
                         start=(half == 0 or bank_split(k)),
                         stop=(half == 1 or bank_split(k)))
        r.then_inc("mm", 1)
        row = cx.row(4) if nbn[0] % 2 == 0 else cx.row(6)
        nbn[0] += 1
        k.wait(V, "nga")
        k.wait(V, "mm")
        V.tensor_copy(row, cx.ps_sg[0:1, 0:tok])
        V.drain().then_inc("ev", 1)
        k.wait(Y, "ev")
        Y.dma_start(out=dst_row, in_=row).then_inc("nga", 16)

    k.wait(Y, "ccq")
    k.wait(G, "ccq")
    k.wait(S, "ccq")
    for h in range(NH):
        tbi = tbdb.nexti()
        ts_ = tbslots[tbi % 2]
        for mi, (tb, le, ch) in enumerate(((tb_q, Y, "y"), (tb_k, G, "g"),
                                           (tb_v, S, "s"))):
            tbdb.pre_load(le, tbi)
            tbdb.load_dma(le, tbi, ch=ch,
                          out=ts_[:, mi * DC * 128:(mi + 1) * DC * 128].rearrange("p (c e) -> p c e", e=128),
                          in_=tb[:, h * 128:(h + 1) * 128].rearrange("(c p) e -> p c e", p=128))
        tbdb.ready(T, tbi)
        tq = ts_[:, 0:DC * 128]
        tk = ts_[:, DC * 128:2 * DC * 128]
        tv = ts_[:, 2 * DC * 128:3 * DC * 128]
        ps = (cx.ps_mm0 if h % 2 == 0 else cx.ps_mm1)[:, 0:tok]
        int_block(tb_q, tq, h, ps).then_inc("mm", 1)
        k.wait(V, "mm")
        V.scalar_tensor_tensor(uc[:, 0:tok], ps, mq_col, gb[:, 0:tok],
                               ALU.mult, ALU.mult).then_inc("ev", 1)
        k.wait(T, "ev")
        k.wait(V, "qst")
        V.tensor_copy(bq1[:, 0:tok], uc[:, 0:tok])
        V.tensor_tensor(bq2[:, 0:tok], uc[:, 0:tok], bq1[:, 0:tok],
                        ALU.subtract)
        bnorm(uc[:, 0:tok], qn_scr[h:h + 1, :])
        V.drain().then_inc("ev", 1)
        k.wait(Y, "ev")
        base = (h * 4) * 128
        Y.dma_start(out=q_scr[base:base + 128, :], in_=bq1[:, 0:tok]).then_inc("qst", 16)
        Y.dma_start(out=q_scr[base + 128:base + 256, :], in_=bq2[:, 0:tok]).then_inc("qst", 16)
        psk = (cx.ps_mm1 if h % 2 == 0 else cx.ps_mm0)[:, 0:tok]
        int_block(tb_k, tk, h, psk).then_inc("mm", 1)
        k.wait(V, "mm")
        V.scalar_tensor_tensor(uc[:, 0:tok], psk, mk_col, gb[:, 0:tok],
                               ALU.mult, ALU.mult).then_inc("ev", 1)
        k.wait(T, "ev")
        k.wait(V, "kga")
        V.tensor_copy(bk1[:, 0:tok], uc[:, 0:tok])
        V.tensor_tensor(bk2[:, 0:tok], uc[:, 0:tok], bk1[:, 0:tok],
                        ALU.subtract)
        bnorm(uc[:, 0:tok], nga_in[h:h + 1, :])
        V.drain().then_inc("ev", 1)
        k.wait(Y, "ev")
        kgi = kga_ins[h // 4]
        kb = (h % 4) * KROWS
        Y.dma_start(out=kgi[kb:kb + 128, :], in_=bk1[:, 0:tok]).then_inc("kga", 16)
        Y.dma_start(out=kgi[kb + 128:kb + 256, :], in_=bk2[:, 0:tok]).then_inc("kga", 16)
        psv = (cx.ps_mm0 if h % 2 == 0 else cx.ps_mm1)[:, 0:tok]
        rv = int_block(tb_v, tv, h, psv)
        tbdb.free(rv, tbi)
        tbdb.wait_freed(V, tbi)
        V.scalar_tensor_tensor(uc[:, 0:tok], psv, mv_col, gb[:, 0:tok],
                               ALU.mult, ALU.mult).then_inc("ev", 1)
        k.wait(T, "ev")
        k.wait(V, "vga")
        for t0 in range(0, NKC, 2):
            ts2 = [t for t in (t0, t0 + 1) if t < NKC]
            for i, t in enumerate(ts2):
                T.transpose(cx.ps_pv[:, i * 512:i * 512 + 128],
                            uc[:, t * 128:(t + 1) * 128],
                            cx.identt[:, :]).then_inc("mm", 1)
            k.wait(V, "mm")
            for i, t in enumerate(ts2):
                psl = cx.ps_pv[:, i * 512:i * 512 + 128]
                V.tensor_copy(bv1[:, t * 128:(t + 1) * 128], psl)
            for i, t in enumerate(ts2):
                psl = cx.ps_pv[:, i * 512:i * 512 + 128]
                V.tensor_tensor(bv2[:, t * 128:(t + 1) * 128], psl,
                                bv1[:, t * 128:(t + 1) * 128],
                                ALU.subtract).then_inc("ev", 1)
            k.wait(T, "ev")
        k.wait(Y, "ev")
        vgi = vga_ins[h // 4]
        vb = (h % 4) * 2 * tok
        Y.dma_start(out=vgi[vb:vb + tok, :].rearrange("(t p) d -> p t d", p=128),
                    in_=bv1[:, 0:tok]).then_inc("vga", 16)
        Y.dma_start(out=vgi[vb + tok:vb + 2 * tok, :].rearrange("(t p) d -> p t d", p=128),
                    in_=bv2[:, 0:tok]).then_inc("vga", 16)
    abf.pop()
    af32.pop()
    k.fence()

    cc0b = k.at("cc")
    for g in range(4):
        if g == 3:
            k.wait(G, "nga")
            k.wait(G, "qst")
            G.collective_compute("AllGather", ALU.bypass,
                                 replica_groups=[list(range(NCORES))],
                                 ins=[nga_in.opt()],
                                 outs=[nga_out.opt()]).then_inc("cc", 1)
        k.wait(G, "kga", val=kgb0 + 32 * 4 * (g + 1))
        G.collective_compute("AllGather", ALU.bypass,
                             replica_groups=[list(range(NCORES))],
                             ins=[kga_ins[g].opt()],
                             outs=[kga_outs[g].opt()]).then_inc("cc", 1)
        k.wait(G, "vga", val=vgb0 + 32 * 4 * (g + 1))
        G.collective_compute("AllGather", ALU.bypass,
                             replica_groups=[list(range(NCORES))],
                             ins=[vga_ins[g].opt()],
                             outs=[vga_outs[g].opt()]).then_inc("cc", 1)

    af32.push()
    abf.push()
    stg = af32.alloc(2 * tok)
    pass

    def b_writer(h, ps_pv, sigi):
        so = stg[:, (h % 2) * tok:(h % 2) * tok + tok]
        k.wait(V, "st")
        V.tensor_tensor(so, ps_pv[:, 0:tok], sigi[:, 0:tok],
                        ALU.mult).then_inc("ev", 1)
        k.wait(Y, "ev")
        Y.dma_start(out=aob_scr[h * 128:(h + 1) * 128, :], in_=so).then_inc("st", 16)

    emit_attention(k, cx, af32, abf, "bb", False, q_scr, qn_scr, nga_out,
                   kga_outs, vga_outs, KROWS, cc0b, b_writer)
    k.wait(Y, "st")
    abf.pop()
    af32.pop()
    k.fence()

    af32.push()
    abf.push()
    oq = xn1
    gb2 = af32.alloc(tok)
    emit_actquant(k, cx, af32, abf, aob_scr,
                  lambda c: oq[:, c * tok:(c + 1) * tok], gb2, "bo")
    k.fence()
    V.tensor_scalar(cx.wq[:, 15:16], cx.wq[:, 12:13], 1.0, None, ALU.mult)
    V.drain().then_inc("ev", 1)
    k.wait(T, "ev")
    mo_col = cx.wq[:, 15:16]
    todb = k.dbuf("tbdb", 2)
    toslots = [abf.alloc(DC * 128) for _ in range(2)]
    xrdb2 = k.dbuf("xrdb", 2)
    xres = af32.alloc(2 * tok)
    xo2 = af32.alloc(2 * tok)
    pass
    for j in range(DC):
        toi = todb.nexti()
        todb.pre_load(Y, toi)
        ts_ = toslots[toi % 2]
        todb.load_dma(Y, toi, out=ts_[:, 0:DC * 128].rearrange("p (c e) -> p c e", e=128),
                      in_=tb_o[:, j * 128:(j + 1) * 128].rearrange("(c p) e -> p c e", p=128))
        todb.ready(T, toi)
        ps = (cx.ps_mm0 if j % 2 == 0 else cx.ps_mm1)[:, 0:tok]
        for c in range(DC):
            cs = slice(c * 128, (c + 1) * 128)
            for half in range(2):
                hsl = slice(half * tokh, (half + 1) * tokh)
                r = T.matmul(ps[:, hsl], ts_[:, cs],
                             oq[:, c * tok + half * tokh:c * tok + half * tokh + tokh],
                             start=(c == 0 and (half == 0 or bank_split(k))),
                             stop=(c == DC - 1 and (half == 1 or bank_split(k))))
        todb.free(r, toi)
        x2i = xrdb2.nexti()
        xrdb2.pre_load(Y, x2i)
        xrdb2.load_dma(Y, x2i, out=xres[:, (j % 2) * tok:(j % 2) * tok + tok],
                       in_=x_in[j * 128:(j + 1) * 128, :])
        xrdb2.ready(V, x2i)
        todb.wait_freed(V, toi)
        k.wait(V, "xr1d")
        V.scalar_tensor_tensor(xo2[:, (j % 2) * tok:(j % 2) * tok + tok], ps,
                               mo_col, gb2[:, 0:tok], ALU.mult, ALU.mult)
        vv = V.tensor_tensor(xo2[:, (j % 2) * tok:(j % 2) * tok + tok],
                             xo2[:, (j % 2) * tok:(j % 2) * tok + tok],
                             xres[:, (j % 2) * tok:(j % 2) * tok + tok], ALU.add)
        xrdb2.free(vv, x2i)
        xrdb2.wait_freed(k.T, x2i)
        xrdb2.wait_freed(Y, x2i)
        Y.dma_start(out=x_out[j * 128:(j + 1) * 128, :],
                    in_=xo2[:, (j % 2) * tok:(j % 2) * tok + tok]).then_inc("xr1d", 16)
    k.wait(Y, "xr1d")
    abf.pop()
    af32.pop()
    k.fence()


def emit_program(k: KBuild):
    T, V, S, G, Y = k.T, k.V, k.S, k.G, k.Y
    tok = k.tok
    tokh = tok // 2
    NKC = tok // 128

    xfm = k.dram("xfm", [D, tok], F32, kind="ExternalInput")
    ident = k.dram("ident", [128, 128], F32, kind="ExternalInput")
    permhr = k.dram("permhr", [128, 128], F32, kind="ExternalInput")
    tokoff = k.dram("tokoff", [1, 1], F32, kind="ExternalInput")
    w_qdown = k.dram("w_qdown", [D, QR], F32, kind="ExternalInput")
    q_gamma = k.dram("q_gamma", [QR, 1], F32, kind="ExternalInput")
    q_beta = k.dram("q_beta", [QR, 1], F32, kind="ExternalInput")
    w_qup = k.dram("w_qup", [QR, D], F32, kind="ExternalInput")
    w_qrope = k.dram("w_qrope", [QR, NH * RD], F32, kind="ExternalInput")
    w_kvdown = k.dram("w_kvdown", [D, KVR], F32, kind="ExternalInput")
    kv_gamma = k.dram("kv_gamma", [KVR, 1], F32, kind="ExternalInput")
    kv_beta = k.dram("kv_beta", [KVR, 1], F32, kind="ExternalInput")
    w_kup = k.dram("w_kup", [KVR, D], F32, kind="ExternalInput")
    w_vup = k.dram("w_vup", [KVR, D], F32, kind="ExternalInput")
    w_krope = k.dram("w_krope", [KVR, NH * RD], F32, kind="ExternalInput")
    w_out = k.dram("w_out", [D, D], F32, kind="ExternalInput")
    wsh_up = k.dram("wsh_up", [D // NCORES, DE], F32, kind="ExternalInput")
    wsh_gate = k.dram("wsh_gate", [D // NCORES, DE], F32, kind="ExternalInput")
    wsh_down = k.dram("wsh_down", [DE // NCORES, D], F32, kind="ExternalInput")
    wsh_bq = k.dram("wsh_bq", [D // NCORES, D], F32, kind="ExternalInput")
    wsh_bk = k.dram("wsh_bk", [D // NCORES, D], F32, kind="ExternalInput")
    wsh_bv = k.dram("wsh_bv", [D // NCORES, D], F32, kind="ExternalInput")
    wsh_bo = k.dram("wsh_bo", [D // NCORES, D], F32, kind="ExternalInput")
    out_fm = k.dram("out_fm", [D, tok], F32, kind="ExternalOutput")

    xr1 = k.dram("xr1", [D, tok], F32)
    xr2 = k.dram("xr2", [D, tok], F32)
    xr3 = k.dram("xr3", [D, tok], F32)
    xnb_scr = k.dram("xnb_scr", [D, tok], F32)
    aob_scr = k.dram("aob_scr", [D, tok], F32)
    ffnacc = k.dram("ffnacc", [D, tok], F32)
    t_up = k.dram("t_up", [D, DE], BF16)
    t_gate = k.dram("t_gate", [D, DE], BF16)
    t_down = k.dram("t_down", [DE, D], BF16)
    tb_q = k.dram("tb_q", [D, D], BF16)
    tb_k = k.dram("tb_k", [D, D], BF16)
    tb_v = k.dram("tb_v", [D, D], BF16)
    tb_o = k.dram("tb_o", [D, D], BF16)
    tg_up = k.dram("tg_up", [D // NCORES, DE], BF16)
    tg_gate = k.dram("tg_gate", [D // NCORES, DE], BF16)
    tg_down = k.dram("tg_down", [DE // NCORES, D], BF16)
    tg_bq = k.dram("tg_bq", [D // NCORES, D], BF16)
    tg_bk = k.dram("tg_bk", [D // NCORES, D], BF16)
    tg_bv = k.dram("tg_bv", [D // NCORES, D], BF16)
    tg_bo = k.dram("tg_bo", [D // NCORES, D], BF16)
    wqsum_in = k.dram("wqsum_in", [1, 8], F32)
    wqsum_out = k.dram("wqsum_out", [8, 8], F32)
    cq_scr = k.dram("cq_scr", [QR, tok], F32)
    ckv_scr = k.dram("ckv_scr", [KVR, tok], F32)
    qn_scr = k.dram("qn_scr", [NH, tok], F32)
    q_scr = k.dram("q_scr", [NH * 4 * 128, tok], BF16)
    KROWS = 384
    NHG = NH // 4
    kga_ins = [k.dram(f"kga_in{g}", [NHG * KROWS, tok], BF16)
               for g in range(4)]
    kga_outs = [k.dram(f"kga_out{g}", [NCORES * NHG * KROWS, tok], BF16)
                for g in range(4)]
    vga_ins = [k.dram(f"vga_in{g}", [NHG * 2 * tok, HD], BF16)
               for g in range(4)]
    vga_outs = [k.dram(f"vga_out{g}", [NCORES * NHG * 2 * tok, HD], BF16)
                for g in range(4)]
    nga_in = k.dram("nga_in", [NH, tok], F32)
    nga_out = k.dram("nga_out", [NCORES * NH, tok], F32)

    cx = Ctx()
    cx.ps_mm0 = k.psum("ps_mm0", [128, 1024], F32)
    cx.ps_mm1 = k.psum("ps_mm1", [128, 1024], F32)
    cx.ps_pv = k.psum("ps_pv", [128, 1024], F32)
    cx.ps_sg = k.psum("ps_sg", [1, 1024], F32)

    cx.ones_col = k.sbuf("ones_col", [128, 1], F32)
    cx.ones_row = k.sbuf("ones_row", [1, 128], F32)
    cx.ones_colb = k.sbuf("ones_colb", [128, 1], BF16)
    cx.ones_rowb = k.sbuf("ones_rowb", [1, 128], BF16)
    cx.identt = k.sbuf("identt", [128, 128], F32)
    cx.permt = k.sbuf("permt", [128, 128], F32)
    cx.bcA = k.sbuf("bcA", [128, 1024], F32)
    cx.bcB = k.sbuf("bcB", [128, 1024], F32)
    cx.lnsc1 = k.sbuf("lnsc1", [128, 1024], F32)
    cx.lnsc2 = k.sbuf("lnsc2", [128, 1024], F32)
    cx.trig = k.sbuf("trig", [128, 2 * 1024], F32)
    cx.gbq = k.sbuf("gbq", [128, 2 * QC], F32)
    cx.gbkv = k.sbuf("gbkv", [128, 2 * KC], F32)
    cx.rows = k.sbuf("rows", [1, 7 * 1024], F32)
    cx.knrow = k.sbuf("knrow", [1, 256], F32)
    cx.mrow = k.sbuf("mrow", [1, 1024], BF16)
    cx.mrow2 = k.sbuf("mrow2", [1, 1024], BF16)
    cx.wq = k.sbuf("wq", [128, 16], F32)

    def rowv(i):
        if i == 7:
            i = 2
        return cx.rows[0:1, i * 1024:i * 1024 + k.tok]
    cx.row = rowv

    af32 = Arena(k, "af32", 9216, F32)
    abf = Arena(k, "abf", 52 * 1024, BF16)

    for s in ("gp", "xld", "mm", "ev", "sc", "st", "cc", "ccq", "wqst",
              "kga", "vga", "nga", "qst", "wsp"):
        k.sem(s)

    wq_dims = [(D, DE, "ffn"), (D, DE, "ffn"), (DE, D, "ffn"),
               (D, D, "b"), (D, D, "b"), (D, D, "b"), (D, D, "b")]
    wq_shards = [(wsh_up, tg_up, t_up), (wsh_gate, tg_gate, t_gate),
                 (wsh_down, tg_down, t_down), (wsh_bq, tg_bq, tb_q),
                 (wsh_bk, tg_bk, tb_k), (wsh_bv, tg_bv, tb_v),
                 (wsh_bo, tg_bo, tb_o)]
    wqst = wq_setup(k, cx, wq_shards, wq_dims)

    G.memset(cx.ones_col[:, :], 1.0)
    G.memset(cx.ones_row[:, :], 1.0)
    G.memset(cx.ones_colb[:, :], 1.0)
    G.memset(cx.ones_rowb[:, :], 1.0).then_inc("gp", 1)
    Y.dma_start(out=cx.identt[:, :], in_=ident[:, :]).then_inc("gp", 16)
    Y.dma_start(out=cx.permt[:, :], in_=permhr[:, :]).then_inc("gp", 16)

    xn1 = abf.alloc(DC * tok)
    xn2 = abf.alloc(DC * tok)
    af32.push()
    xring = af32.alloc(2 * tok)
    k.wait(T, "gp")
    emit_ln_split(k, cx, xfm, xring, DC,
                  lambda c: xn1[:, c * tok:(c + 1) * tok],
                  lambda c: xn2[:, c * tok:(c + 1) * tok], D, tag="ln")
    af32.pop()
    k.fence()

    if k.debug:
        _dbg_fm(k, cx, "xn_a", lambda c: (xn1[:, c * tok:(c + 1) * tok],
                                          xn2[:, c * tok:(c + 1) * tok]), DC)

    for c in range(QC):
        Y.dma_start(out=cx.gbq[:, c:c + 1],
                    in_=q_gamma[c * 128:(c + 1) * 128, :]).then_inc("xld", 16)
        Y.dma_start(out=cx.gbq[:, QC + c:QC + c + 1],
                    in_=q_beta[c * 128:(c + 1) * 128, :]).then_inc("xld", 16)
    for c in range(KC):
        Y.dma_start(out=cx.gbkv[:, c:c + 1],
                    in_=kv_gamma[c * 128:(c + 1) * 128, :]).then_inc("xld", 16)
        Y.dma_start(out=cx.gbkv[:, KC + c:KC + c + 1],
                    in_=kv_beta[c * 128:(c + 1) * 128, :]).then_inc("xld", 16)

    wdb = k.dbuf("wdb", 2, channels=("g", "y"))
    abf.push()
    af32.push()
    wslots = [(abf.alloc(DC * 128), abf.alloc(DC * 128), af32.alloc(DC * 128))
              for _ in range(2)]
    evring = af32.alloc(2 * tok)

    nevp = [k.at("ev")]

    def proj_block(jg_unused, w_dram, j, dst_dram, dst_row0):
        jg = wdb.nexti()
        whi, wlo, wraw = wslots[jg % 2]
        wdb.pre_load(G, jg)
        wdb.pre_load(Y, jg)
        for c in range(DC):
            rs = slice(c * 128, (c + 1) * 128)
            cs = slice(c * 128, (c + 1) * 128)
            wdb.load_dma(G, jg, ch="g", out=whi[:, cs], in_=w_dram[rs, j * 128:(j + 1) * 128])
            wdb.load_dma(Y, jg, ch="y", out=wraw[:, cs], in_=w_dram[rs, j * 128:(j + 1) * 128])
        wdb.ready(V, jg)
        for c in range(DC):
            cs = slice(c * 128, (c + 1) * 128)
            V.tensor_tensor(wlo[:, cs], wraw[:, cs], whi[:, cs],
                            ALU.subtract).then_inc("wsp", 1)
        k.wait(T, "wsp")
        if jg >= 2:
            k.wait(T, "ev", val=nevp[0] + jg - 1)
        ps = (cx.ps_mm0 if jg % 2 == 0 else cx.ps_mm1)[:, 0:tok]
        for c in range(DC):
            cs = slice(c * 128, (c + 1) * 128)
            for half in range(2):
                hs = slice(half * tokh, (half + 1) * tokh)
                r = mm3(k, ps[:, hs], whi[:, cs], wlo[:, cs],
                        xn1[:, c * tok + half * tokh:c * tok + half * tokh + tokh],
                        xn2[:, c * tok + half * tokh:c * tok + half * tokh + tokh],
                        start=(c == 0 and (half == 0 or bank_split(k))),
                        stop=(c == DC - 1 and (half == 1 or bank_split(k))))
        wdb.free(r, jg)
        wdb.wait_freed(V, jg)
        k.wait(V, "st")
        ev = evring[:, (jg % 2) * tok:(jg % 2) * tok + tok]
        V.tensor_copy(ev, ps).then_inc("ev", 1)
        k.wait(Y, "ev")
        Y.dma_start(out=dst_dram[dst_row0:dst_row0 + 128, :], in_=ev).then_inc("st", 16)

    jg = 0
    for j in range(QC):
        proj_block(jg, w_qdown, j, cq_scr, j * 128)
        for u in wqst.scale_units[jg * 4:jg * 4 + 4]:
            u()
        jg += 1
    for j in range(KC):
        proj_block(jg, w_kvdown, j, ckv_scr, j * 128)
        for u in wqst.scale_units[jg * 4:jg * 4 + 4]:
            u()
        jg += 1
    abf.pop()
    af32.pop()
    k.fence()

    cqn1 = xn1[:, 0:QC * tok]
    cqn2 = xn2[:, 0:QC * tok]
    ckvn1 = xn1[:, QC * tok:(QC + KC) * tok]
    ckvn2 = xn2[:, QC * tok:(QC + KC) * tok]
    k.wait(V, "xld")
    k.wait(Y, "st")
    af32.push()
    lring = af32.alloc(2 * tok)
    emit_ln_split(k, cx, cq_scr, lring, QC,
                  lambda c: cqn1[:, c * tok:(c + 1) * tok],
                  lambda c: cqn2[:, c * tok:(c + 1) * tok], QR,
                  gamma_cols=lambda c: cx.gbq[:, c:c + 1],
                  beta_cols=lambda c: cx.gbq[:, QC + c:QC + c + 1], tag="ln")
    k.fence()
    emit_ln_split(k, cx, ckv_scr, lring, KC,
                  lambda c: ckvn1[:, c * tok:(c + 1) * tok],
                  lambda c: ckvn2[:, c * tok:(c + 1) * tok], KVR,
                  gamma_cols=lambda c: cx.gbkv[:, c:c + 1],
                  beta_cols=lambda c: cx.gbkv[:, KC + c:KC + c + 1], tag="ln")
    af32.pop()
    k.fence()

    if k.debug:
        _dbg_fm(k, cx, "cqn", lambda c: (cqn1[:, c * tok:(c + 1) * tok],
                                         cqn2[:, c * tok:(c + 1) * tok]), QC)
        _dbg_fm(k, cx, "ckvn", lambda c: (ckvn1[:, c * tok:(c + 1) * tok],
                                          ckvn2[:, c * tok:(c + 1) * tok]), KC)

    wq_scales_finish(k, cx, wqst, wqsum_in, wqsum_out)

    kga0, vga0 = k.at("kga"), k.at("vga")
    _emit_trig_tables(k, cx, tokoff)

    af32.push()
    abf.push()
    uc = af32.alloc(tok)
    ur = af32.alloc(tok)
    usw = af32.alloc(tok)
    ut2 = af32.alloc(tok)
    sq1 = af32.alloc(tok)
    b1 = abf.alloc(tok)
    b2 = abf.alloc(tok)
    b3 = abf.alloc(tok)
    b4 = abf.alloc(tok)
    wudb = k.dbuf("wudb", 2, channels=("g", "y"))
    wkdb = wudb
    wuslots = [(abf.alloc(QC * 128), abf.alloc(QC * 128), af32.alloc(QC * 128))
               for _ in range(2)]
    k.sem("swp")

    def up_block(db, ug_unused, w_dram, col0, ncol, nch, ps, c1, c2):
        ug = db.nexti()
        whi, wlo, wraw = wuslots[ug % 2]
        db.pre_load(G, ug)
        db.pre_load(Y, ug)
        for c in range(nch):
            rs = slice(c * 128, (c + 1) * 128)
            cs = slice(c * ncol, (c + 1) * ncol)
            db.load_dma(G, ug, ch="g", out=whi[:, cs], in_=w_dram[rs, col0:col0 + ncol])
            db.load_dma(Y, ug, ch="y", out=wraw[:, cs], in_=w_dram[rs, col0:col0 + ncol])
        db.ready(V, ug)
        for c in range(nch):
            cs = slice(c * ncol, (c + 1) * ncol)
            V.tensor_tensor(wlo[:, cs], wraw[:, cs], whi[:, cs],
                            ALU.subtract).then_inc("wsp", 1)
        k.wait(T, "wsp")
        for c in range(nch):
            cs = slice(c * ncol, (c + 1) * ncol)
            for half in range(2):
                hs = slice(half * tokh, (half + 1) * tokh)
                r = mm3(k, ps[:, hs], whi[:, cs], wlo[:, cs],
                        c1[:, c * tok + half * tokh:c * tok + half * tokh + tokh],
                        c2[:, c * tok + half * tokh:c * tok + half * tokh + tokh],
                        start=(c == 0 and (half == 0 or bank_split(k))),
                        stop=(c == nch - 1 and (half == 1 or bank_split(k))))
        db.free(r, ug)
        return ug

    def rope_apply(u64, scale):
        k.wait(Y, "ev")
        Y.dma_start(out=usw[0:32, 0:tok], in_=u64[32:64, 0:tok]).then_inc("swp", 16)
        Y.dma_start(out=usw[32:64, 0:tok], in_=u64[0:32, 0:tok]).then_inc("swp", 16)
        k.wait(V, "swp")
        cosb = cx.trig[0:64, 0:tok]
        sinb = cx.trig[0:64, 1024:1024 + tok]
        V.tensor_tensor(u64, u64, cosb, ALU.mult)
        V.tensor_tensor(ut2[0:64, 0:tok], usw[0:64, 0:tok], sinb, ALU.mult)
        V.tensor_tensor(u64, u64, ut2[0:64, 0:tok], ALU.add)
        if scale != 1.0:
            V.tensor_scalar(u64, u64, scale, None, ALU.mult)

    def head_norm(content_ap, rope_ap, dst_dram_row):
        V.tensor_tensor(sq1[:, 0:tok], content_ap, content_ap,
                        ALU.mult).then_inc("ev", 1)
        k.wait(T, "ev")
        for half in range(2):
            hs = slice(half * tokh, (half + 1) * tokh)
            T.matmul(cx.ps_sg[0:1, hs], cx.ones_col[:, :], sq1[:, hs],
                     start=(half == 0 or bank_split(k)), stop=False)
        V.tensor_tensor(ut2[0:64, 0:tok], rope_ap, rope_ap,
                        ALU.mult).then_inc("ev", 1)
        k.wait(T, "ev")
        for half in range(2):
            hs = slice(half * tokh, (half + 1) * tokh)
            r = T.matmul(cx.ps_sg[0:1, hs], cx.ones_col[0:64, :],
                         ut2[0:64, hs], start=False,
                         stop=(half == 1 or bank_split(k)))
        r.then_inc("mm", 1)
        k.wait(V, "mm")
        V.tensor_copy(cx.row(7), cx.ps_sg[0:1, 0:tok])
        V.drain().then_inc("ev", 1)
        k.wait(T, "ev")
        k.wait(Y, "ev")
        Y.dma_start(out=dst_dram_row, in_=cx.row(7)).then_inc("nga", 16)
        k.wait(V, "nga")

    ug = 0
    for h in range(NH):
        k.wait(T, "ev")
        ps = (cx.ps_mm0 if h % 2 == 0 else cx.ps_mm1)[:, 0:tok]
        psr = (cx.ps_mm1 if h % 2 == 0 else cx.ps_mm0)[0:64, 0:tok]
        ugi = up_block(wudb, ug, w_qup, h * 128, 128, QC, ps, cqn1, cqn2)
        ug += 1
        wudb.wait_freed(V, ugi)
        V.tensor_scalar(uc[:, 0:tok], ps, ASCALE, None, ALU.mult).then_inc("ev", 1)
        ugi = up_block(wudb, ug, w_qrope, h * RD, RD, QC, psr, cqn1, cqn2)
        ug += 1
        wudb.wait_freed(V, ugi)
        k.wait(V, "swp")
        V.tensor_copy(ur[0:64, 0:tok], psr).then_inc("ev", 1)
        rope_apply(ur[0:64, 0:tok], ASCALE)
        V.tensor_copy(b1[:, 0:tok], uc[:, 0:tok])
        V.tensor_copy(b3[0:64, 0:tok], ur[0:64, 0:tok])
        V.tensor_tensor(b2[:, 0:tok], uc[:, 0:tok], b1[:, 0:tok], ALU.subtract)
        V.tensor_tensor(b4[0:64, 0:tok], ur[0:64, 0:tok], b3[0:64, 0:tok],
                        ALU.subtract)
        head_norm(uc[:, 0:tok], ur[0:64, 0:tok], qn_scr[h:h + 1, :])
        V.drain().then_inc("ev", 1)
        k.wait(Y, "ev")
        base = (h * 4) * 128
        Y.dma_start(out=q_scr[base:base + 128, :], in_=b1[:, 0:tok]).then_inc("qst", 16)
        Y.dma_start(out=q_scr[base + 128:base + 256, :], in_=b2[:, 0:tok]).then_inc("qst", 16)
        Y.dma_start(out=q_scr[base + 256:base + 320, :], in_=b3[0:64, 0:tok]).then_inc("qst", 16)
        Y.dma_start(out=q_scr[base + 320:base + 384, :], in_=b4[0:64, 0:tok]).then_inc("qst", 16)
        Y.dma_start(out=q_scr[base + 384:base + 448, :], in_=b4[0:64, 0:tok]).then_inc("qst", 16)
        Y.dma_start(out=q_scr[base + 448:base + 512, :], in_=b3[0:64, 0:tok]).then_inc("qst", 16)
        k.wait(V, "qst")
        for u in wqst.quant_units[2 * h:2 * h + 2]:
            u()

    for h in range(NH):
        k.wait(T, "ev")
        ps = (cx.ps_mm0 if h % 2 == 0 else cx.ps_mm1)[:, 0:tok]
        psr = (cx.ps_mm1 if h % 2 == 0 else cx.ps_mm0)[0:64, 0:tok]
        ugi = up_block(wkdb, ug, w_kup, h * 128, 128, KC, ps, ckvn1, ckvn2)
        ug += 1
        wkdb.wait_freed(V, ugi)
        V.tensor_copy(uc[:, 0:tok], ps).then_inc("ev", 1)
        ugi = up_block(wkdb, ug, w_krope, h * RD, RD, KC, psr, ckvn1, ckvn2)
        ug += 1
        wkdb.wait_freed(V, ugi)
        k.wait(V, "swp")
        V.tensor_copy(ur[0:64, 0:tok], psr).then_inc("ev", 1)
        rope_apply(ur[0:64, 0:tok], 1.0)
        V.tensor_copy(b1[:, 0:tok], uc[:, 0:tok])
        V.tensor_copy(b3[0:64, 0:tok], ur[0:64, 0:tok])
        V.tensor_tensor(b2[:, 0:tok], uc[:, 0:tok], b1[:, 0:tok], ALU.subtract)
        V.tensor_tensor(b4[0:64, 0:tok], ur[0:64, 0:tok], b3[0:64, 0:tok],
                        ALU.subtract)
        head_norm(uc[:, 0:tok], ur[0:64, 0:tok], nga_in[h:h + 1, :])
        V.drain().then_inc("ev", 1)
        k.wait(Y, "ev")
        kgi = kga_ins[h // 4]
        kb = (h % 4) * KROWS
        Y.dma_start(out=kgi[kb:kb + 128, :], in_=b1[:, 0:tok]).then_inc("kga", 16)
        Y.dma_start(out=kgi[kb + 128:kb + 256, :], in_=b2[:, 0:tok]).then_inc("kga", 16)
        Y.dma_start(out=kgi[kb + 256:kb + 320, :], in_=b3[0:64, 0:tok]).then_inc("kga", 16)
        Y.dma_start(out=kgi[kb + 320:kb + 384, :], in_=b4[0:64, 0:tok]).then_inc("kga", 16)
        psv = (cx.ps_mm0 if h % 2 == 0 else cx.ps_mm1)[:, 0:tok]
        ugi = up_block(wkdb, ug, w_vup, h * 128, 128, KC, psv, ckvn1, ckvn2)
        ug += 1
        wkdb.wait_freed(V, ugi)
        k.wait(V, "kga")
        V.tensor_copy(uc[:, 0:tok], psv).then_inc("ev", 1)
        k.wait(T, "ev")
        for t0 in range(0, NKC, 2):
            ts2 = [t for t in (t0, t0 + 1) if t < NKC]
            for i, t in enumerate(ts2):
                T.transpose(cx.ps_pv[:, i * 512:i * 512 + 128],
                            uc[:, t * 128:(t + 1) * 128],
                            cx.identt[:, :]).then_inc("mm", 1)
            k.wait(V, "mm")
            for i, t in enumerate(ts2):
                psl = cx.ps_pv[:, i * 512:i * 512 + 128]
                V.tensor_copy(b1[:, t * 128:(t + 1) * 128], psl)
            for i, t in enumerate(ts2):
                psl = cx.ps_pv[:, i * 512:i * 512 + 128]
                V.tensor_tensor(b2[:, t * 128:(t + 1) * 128], psl,
                                b1[:, t * 128:(t + 1) * 128],
                                ALU.subtract).then_inc("ev", 1)
            k.wait(T, "ev")
        k.wait(Y, "ev")
        vgi = vga_ins[h // 4]
        vb = (h % 4) * 2 * tok
        Y.dma_start(out=vgi[vb:vb + tok, :].rearrange("(t p) d -> p t d", p=128),
                    in_=b1[:, 0:tok]).then_inc("vga", 16)
        Y.dma_start(out=vgi[vb + tok:vb + 2 * tok, :].rearrange("(t p) d -> p t d", p=128),
                    in_=b2[:, 0:tok]).then_inc("vga", 16)
        k.wait(V, "vga")
        for u in wqst.quant_units[32 + 2 * h:32 + 2 * h + 2]:
            u()

    abf.pop()
    af32.pop()
    k.fence()

    cc0a = k.at("cc")
    for g in range(4):
        if g == 3:
            k.wait(G, "nga")
            k.wait(G, "qst")
            G.collective_compute("AllGather", ALU.bypass,
                                 replica_groups=[list(range(NCORES))],
                                 ins=[nga_in.opt()],
                                 outs=[nga_out.opt()]).then_inc("cc", 1)
        k.wait(G, "kga", val=kga0 + 64 * 4 * (g + 1))
        G.collective_compute("AllGather", ALU.bypass,
                             replica_groups=[list(range(NCORES))],
                             ins=[kga_ins[g].opt()],
                             outs=[kga_outs[g].opt()]).then_inc("cc", 1)
        k.wait(G, "vga", val=vga0 + 32 * 4 * (g + 1))
        G.collective_compute("AllGather", ALU.bypass,
                             replica_groups=[list(range(NCORES))],
                             ins=[vga_ins[g].opt()],
                             outs=[vga_outs[g].opt()]).then_inc("cc", 1)
    k.wait(G, "wqst")
    for tin, tout in ((tg_up, t_up), (tg_gate, t_gate), (tg_down, t_down),
                      (tg_bq, tb_q), (tg_bk, tb_k), (tg_bv, tb_v),
                      (tg_bo, tb_o)):
        G.collective_compute("AllGather", ALU.bypass,
                             replica_groups=[list(range(NCORES))],
                             ins=[tin.opt()], outs=[tout.opt()]).then_inc("ccq", 1)

    af32.push()
    abf.push()
    aout1 = xn1
    aout2 = xn2

    def a_writer(h, ps_pv, sigi):
        V.tensor_tensor(cx.lnsc1[:, 0:tok], ps_pv[:, 0:tok], sigi[:, 0:tok],
                        ALU.mult)
        V.tensor_copy(aout1[:, h * tok:(h + 1) * tok], cx.lnsc1[:, 0:tok])
        V.tensor_tensor(aout2[:, h * tok:(h + 1) * tok], cx.lnsc1[:, 0:tok],
                        aout1[:, h * tok:(h + 1) * tok],
                        ALU.subtract).then_inc("ev", 1)

    emit_attention(k, cx, af32, abf, "aa", True, q_scr, qn_scr, nga_out,
                   kga_outs, vga_outs, KROWS, cc0a, a_writer)

    if k.debug:
        _dbg_fm(k, cx, "aout", lambda c: (aout1[:, c * tok:(c + 1) * tok],
                                          aout2[:, c * tok:(c + 1) * tok]), NH)

    abf.pop()
    af32.pop()
    k.fence()
    abf.push()
    af32.push()
    wodb = k.dbuf("wdb", 2, channels=("g", "y"))
    woslots = [(abf.alloc(DC * 128), abf.alloc(DC * 128), af32.alloc(DC * 128))
               for _ in range(2)]
    xrdb = k.dbuf("xrdb", 2)
    xres = af32.alloc(2 * tok)
    xo = af32.alloc(2 * tok)
    k.sem("wsp")
    k.sem("xr1d")
    for j in range(DC):
        wj = wodb.nexti()
        whi, wlo, wraw = woslots[wj % 2]
        wodb.pre_load(G, wj)
        wodb.pre_load(Y, wj)
        for c in range(DC):
            rs = slice(c * 128, (c + 1) * 128)
            cs = slice(c * 128, (c + 1) * 128)
            wodb.load_dma(G, wj, ch="g", out=whi[:, cs], in_=w_out[rs, j * 128:(j + 1) * 128])
            wodb.load_dma(Y, wj, ch="y", out=wraw[:, cs], in_=w_out[rs, j * 128:(j + 1) * 128])
        wodb.ready(V, wj)
        for c in range(DC):
            cs = slice(c * 128, (c + 1) * 128)
            V.tensor_tensor(wlo[:, cs], wraw[:, cs], whi[:, cs],
                            ALU.subtract).then_inc("wsp", 1)
        k.wait(T, "wsp")
        ps = (cx.ps_mm0 if j % 2 == 0 else cx.ps_mm1)[:, 0:tok]
        for c in range(DC):
            cs = slice(c * 128, (c + 1) * 128)
            for half in range(2):
                hs = slice(half * tokh, (half + 1) * tokh)
                r = mm3(k, ps[:, hs], whi[:, cs], wlo[:, cs],
                        aout1[:, c * tok + half * tokh:c * tok + half * tokh + tokh],
                        aout2[:, c * tok + half * tokh:c * tok + half * tokh + tokh],
                        start=(c == 0 and (half == 0 or bank_split(k))),
                        stop=(c == DC - 1 and (half == 1 or bank_split(k))))
        wodb.free(r, wj)
        xj = xrdb.nexti()
        xrdb.pre_load(Y, xj)
        xrdb.load_dma(Y, xj, out=xres[:, (j % 2) * tok:(j % 2) * tok + tok],
                      in_=xfm[j * 128:(j + 1) * 128, :])
        xrdb.ready(V, xj)
        wodb.wait_freed(V, wj)
        k.wait(V, "xr1d")
        vv = V.tensor_tensor(xo[:, (j % 2) * tok:(j % 2) * tok + tok], ps,
                             xres[:, (j % 2) * tok:(j % 2) * tok + tok], ALU.add)
        xrdb.free(vv, xj)
        xrdb.wait_freed(k.T, xj)
        xrdb.wait_freed(Y, xj)
        Y.dma_start(out=xr1[j * 128:(j + 1) * 128, :],
                    in_=xo[:, (j % 2) * tok:(j % 2) * tok + tok]).then_inc("xr1d", 16)
    abf.pop()
    af32.pop()
    k.fence()

    emit_ffn(k, cx, af32, abf, xn1, xn2, xr1, xr2, t_up, t_gate, t_down,
             ffnacc, "f1")
    if k.debug:
        _dbg_dram(k, cx, af32, "x2", xr2)

    emit_attn_b(k, cx, af32, abf, xr2, xr3, xnb_scr, aob_scr,
                tb_q, tb_k, tb_v, tb_o, q_scr, qn_scr,
                kga_ins, kga_outs, vga_ins, vga_outs, nga_in, nga_out, KROWS,
                xn1, xn2)
    if k.debug:
        _dbg_dram(k, cx, af32, "x3", xr3)

    emit_ffn(k, cx, af32, abf, xn1, xn2, xr3, out_fm, t_up, t_gate, t_down,
             ffnacc, "f2")


def _dbg_dram(k, cx, af32, name, src_dram):
    V, Y = k.V, k.Y
    tok = k.tok
    d = k.dbg(name, [D, tok], F32)
    s = "dbgs"
    k.sem(s)
    af32.push()
    tmp = af32.alloc(2 * tok)
    for c in range(DC):
        k.wait(Y, s)
        Y.dma_start(out=tmp[:, (c % 2) * tok:(c % 2) * tok + tok],
                    in_=src_dram[c * 128:(c + 1) * 128, :]).then_inc("xld", 16)
        k.wait(Y, "xld")
        Y.dma_start(out=d[c * 128:(c + 1) * 128, :],
                    in_=tmp[:, (c % 2) * tok:(c % 2) * tok + tok]).then_inc(s, 16)
    k.wait(Y, s)
    af32.pop()


_NC_CACHE = {}


def build_nc(tok=1024, debug=False):
    key = (tok, bool(debug))
    if key not in _NC_CACHE:
        k = KBuild(tok=tok, debug=debug)
        emit_program(k)
        nc = k.finalize()
        _NC_CACHE[key] = (nc, sorted(k.dbg_out))
    return _NC_CACHE[key]


def _in_maps(inputs, tok=1024):
    x = np.asarray(inputs["L0_attn_a_x"])
    ident = np.eye(128, dtype=np.float32)
    permhr = np.zeros((128, 128), dtype=np.float32)
    for p in range(128):
        permhr[p, (p % NH) * NCORES + p // NH] = 1.0
    wmap = {
        "w_qdown": inputs["L0_attn_a_Wq_down"],
        "q_gamma": np.asarray(inputs["L0_attn_a_q_gamma"]).reshape(-1, 1),
        "q_beta": np.asarray(inputs["L0_attn_a_q_beta"]).reshape(-1, 1),
        "w_qup": inputs["L0_attn_a_Wq_up"],
        "w_qrope": inputs["L0_attn_a_Wq_rope"],
        "w_kvdown": inputs["L0_attn_a_Wkv_down"],
        "kv_gamma": np.asarray(inputs["L0_attn_a_kv_gamma"]).reshape(-1, 1),
        "kv_beta": np.asarray(inputs["L0_attn_a_kv_beta"]).reshape(-1, 1),
        "w_kup": inputs["L0_attn_a_Wk_up"],
        "w_vup": inputs["L0_attn_a_Wv_up"],
        "w_krope": inputs["L0_attn_a_Wk_rope"],
        "w_out": inputs["L0_attn_a_Wout"],
    }
    wmap = {kk: np.ascontiguousarray(np.asarray(v), dtype=np.float32)
            for kk, v in wmap.items()}
    shsrc = {
        "wsh_up": np.asarray(inputs["L0_ffn_W_up"]).T,
        "wsh_gate": np.asarray(inputs["L0_ffn_W_gate"]).T,
        "wsh_down": np.asarray(inputs["L0_ffn_W_down"]).T,
        "wsh_bq": np.asarray(inputs["L0_attn_b_w_q"]).T,
        "wsh_bk": np.asarray(inputs["L0_attn_b_w_k"]).T,
        "wsh_bv": np.asarray(inputs["L0_attn_b_w_v"]).T,
        "wsh_bo": np.asarray(inputs["L0_attn_b_w_o"]).T,
    }
    maps = []
    for c in range(NCORES):
        sl = np.ascontiguousarray(x[0, c * tok:(c + 1) * tok, :].T)
        m = {"xfm": sl, "ident": ident, "permhr": permhr,
             "tokoff": np.full((1, 1), float(c * tok), np.float32)}
        m.update(wmap)
        for kk, w in shsrc.items():
            nr8 = w.shape[0] // NCORES
            m[kk] = np.ascontiguousarray(
                w[c * nr8:(c + 1) * nr8, :], dtype=np.float32)
        maps.append(m)
    return maps


def kernel(**inputs):
    tok = np.asarray(inputs["L0_attn_a_x"]).shape[1] // NCORES
    nc, _ = build_nc(tok=tok, debug=False)
    res = run_bass_kernel_spmd(nc, _in_maps(inputs, tok), core_ids=list(range(NCORES)))
    out = np.concatenate([np.asarray(r["out_fm"]).T[None] for r in res.results], axis=1)
    return out


def kernel_debug(**inputs):
    tok = np.asarray(inputs["L0_attn_a_x"]).shape[1] // NCORES
    nc, dbg_names = build_nc(tok=tok, debug=True)
    res = run_bass_kernel_spmd(nc, _in_maps(inputs, tok), core_ids=list(range(NCORES)))
    out = np.concatenate([np.asarray(r["out_fm"]).T[None] for r in res.results], axis=1)
    dbg = {n: [np.asarray(r["dbg_" + n]) for r in res.results] for n in dbg_names}
    return out, dbg


# version 17
# speedup vs baseline: 1.0278x; 1.0278x over previous
import contextlib

import numpy as np
import concourse.bass as bass
import concourse.mybir as mybir
from concourse.bass_utils import run_bass_kernel_spmd

F32 = mybir.dt.float32
BF16 = mybir.dt.bfloat16
AF = mybir.ActivationFunctionType
ALU = mybir.AluOpType
AX = mybir.AxisListType

NCORES = 8
D = 2048
NH, HD, RD = 16, 128, 64
QR, KVR = 1536, 512
DE = 8192
LN_EPS = 1e-5
PI = float(np.pi)
ASCALE = (HD + RD) ** -0.5
BSCALE = HD ** -0.5
DC = D // 128
QC = QR // 128
KC = KVR // 128


class _Rec:
    __slots__ = ("k", "incs", "is_hwdge_dma", "eng", "opidx")

    def __init__(self, k, is_hwdge_dma=False):
        self.k = k
        self.incs = []
        self.is_hwdge_dma = is_hwdge_dma
        self.eng = None
        self.opidx = None

    def then_inc(self, sem, n=1):
        assert not self.incs, f"second sem update on one instruction: {sem}"
        self.incs.append((sem, n))
        self.k.cnt[sem] = self.k.cnt.get(sem, 0) + n
        if self.is_hwdge_dma:
            self.k.hwdge_sems.add(sem)
        if self.eng is not None:
            self.k.chain[self.eng]["att"].append(
                (self.opidx, sem, self.k.cnt[sem]))
        return self


def _ap_span(x):
    try:
        name = x.tensor.name
        off = int(x.offset)
        ext = 1
        for step, count in x.ap:
            ext += abs(int(step)) * (int(count) - 1)
        return (name, off, off + ext)
    except AttributeError:
        return None


class _EngQ:
    def __init__(self, k, name):
        self._k, self._name = k, name

    def __getattr__(self, meth):
        k, name = self._k, self._name

        def call(*a, **kw):
            if meth == "drain" and name in ("vector", "scalar", "gpsimd"):
                st = k.chain[name]
                lastrec = st["recs"].get(st["cur"] - 1)
                if lastrec is not None and not lastrec.incs:
                    return lastrec
            rec = _Rec(k, is_hwdge_dma=(meth == "dma_start"
                                        and name in ("sync", "scalar", "vector")))
            if name in ("vector", "scalar", "gpsimd") and meth not in (
                    "wait_ge", "drain", "nop", "dma_start", "collective_compute"):
                rd, wr = [], []
                for i, x in enumerate(a):
                    s = _ap_span(x)
                    if s:
                        (wr if i == 0 else rd).append(s)
                for kk, x in kw.items():
                    s = _ap_span(x)
                    if s:
                        (wr if kk in ("out", "accum_out") else rd).append(s)
                st = k.chain[name]
                dW, dR = st["dW"], st["dR"]
                ret = st["retired"]

                def maxdep(spans, dd):
                    m = -1
                    for n, lo, hi in spans:
                        for dlo, dhi, oi in dd.get(n, ()):
                            if oi > ret and lo < dhi and dlo < hi and oi > m:
                                m = oi
                    return m

                dep = max(maxdep(rd, dW), maxdep(wr, dW), maxdep(wr, dR))
                if dep >= 0:
                    att = st["att"]
                    cover = None
                    for oi, sm, val in att:
                        if oi >= dep:
                            cover = (oi, sm, val)
                            break
                    if cover is None:
                        oi = None
                        for cand in range(dep, st["cur"]):
                            r = st["recs"].get(cand)
                            if r is not None and not r.incs:
                                oi = cand
                                break
                        if oi is not None:
                            ch = k.sem("ch_" + name)
                            st["recs"][oi].then_inc(ch, 1)
                            cover = att[-1]
                            assert cover[0] == oi
                        else:
                            k.steps.append((name, "drain", (), {}, _Rec(k)))
                            st["retired"] = st["cur"] - 1
                            dW.clear()
                            dR.clear()
                            att.clear()
                    if cover is not None:
                        k.steps.append((name, "wait_ge",
                                        (cover[1], cover[2]), {}, _Rec(k)))
                        newret = cover[0]
                        st["retired"] = newret
                        for dd in (dW, dR):
                            for n in list(dd):
                                kept = [t for t in dd[n] if t[2] > newret]
                                if kept:
                                    dd[n] = kept
                                else:
                                    del dd[n]
                        st["att"] = [x for x in st["att"] if x[0] > newret]
                oi = st["cur"]
                st["cur"] = oi + 1
                st["recs"][oi] = rec
                rec.eng, rec.opidx = name, oi
                for n, lo, hi in wr:
                    dW.setdefault(n, []).append((lo, hi, oi))
                for n, lo, hi in rd:
                    dR.setdefault(n, []).append((lo, hi, oi))
            k.steps.append((name, meth, a, kw, rec))
            if meth == "drain" and name in k.chain:
                st = k.chain[name]
                st["retired"] = st["cur"] - 1
                st["dW"].clear()
                st["dR"].clear()
                st["att"].clear()
            return rec

        return call


class Arena:
    def __init__(self, k, name, total_elems, dt):
        self.tile = k.sbuf(name, [128, total_elems], dt)
        self.name = name
        self.total = total_elems
        self.off = 0
        self.marks = []
        self.peak = 0

    def alloc(self, elems):
        elems = (elems + 63) & ~63
        assert self.off + elems <= self.total, \
            f"arena {self.name} overflow {self.off}+{elems} > {self.total}"
        ap = self.tile[:, self.off:self.off + elems]
        self.off += elems
        self.peak = max(self.peak, self.off)
        return ap

    def push(self):
        self.marks.append(self.off)

    def pop(self):
        self.off = self.marks.pop()


class DBuf:

    def __init__(self, k, name, nslots, channels=("y",)):
        self.k, self.name, self.n = k, name, nslots
        self.channels = channels
        self.rounds = [0] * nslots
        self.seen = set()
        for i in range(nslots):
            for ch in channels:
                k.sem(f"{name}{ch}_l{i}")
            k.sem(f"{name}_f{i}")

    def nexti(self):
        self._gi = getattr(self, "_gi", -1) + 1
        return self._gi

    def pre_load(self, eng, i):
        s = i % self.n
        if i not in self.seen:
            self.seen.add(i)
            self.rounds[s] += 1
        r = self.rounds[s] - 1
        if r >= 1:
            self.k.wait(eng, f"{self.name}_f{s}", val=r)

    def load_dma(self, eng, i, ch="y", **dma):
        s = i % self.n
        eng.dma_start(**dma).then_inc(f"{self.name}{ch}_l{s}", 16)

    def ready(self, eng, i):
        s = i % self.n
        for ch in self.channels:
            self.k.wait(eng, f"{self.name}{ch}_l{s}")

    def free(self, rec, i):
        rec.then_inc(f"{self.name}_f{i % self.n}", 1)

    def wait_freed(self, engq, i):
        self.k.wait(engq, f"{self.name}_f{i % self.n}", val=i // self.n + 1)


class KBuild:
    ENGS = ("tensor", "vector", "scalar", "gpsimd", "sync")

    def __init__(self, tok=1024, debug=False):
        self.nc = bass.Bass("TRN2", target_bir_lowering=False)
        self.es = contextlib.ExitStack()
        self.steps = []
        self.sems = {}
        self.cnt = {}
        self.tok = tok
        self.debug = debug
        self.dbg_out = {}
        self.chain = {e: {"cur": 0, "retired": -1, "dW": {}, "dR": {},
                          "att": [], "recs": {}} for e in self.ENGS}
        self.hwdge_sems = set()
        self.dbufs = {}
        self.T = _EngQ(self, "tensor")
        self.V = _EngQ(self, "vector")
        self.S = _EngQ(self, "scalar")
        self.G = _EngQ(self, "gpsimd")
        self.Y = _EngQ(self, "sync")

    def sem(self, name):
        if name not in self.sems:
            self.sems[name] = self.es.enter_context(self.nc.semaphore(name))
            self.cnt[name] = 0
        return name

    def sbuf(self, name, shape, dt=F32):
        return self.es.enter_context(self.nc.sbuf_tensor(name, shape, dt))

    def psum(self, name, shape, dt=F32):
        return self.es.enter_context(self.nc.psum_tensor(name, shape, dt))

    def dram(self, name, shape, dt=F32, kind=None):
        if kind is None:
            return self.nc.dram_tensor(name, shape, dt).ap()
        return self.nc.dram_tensor(name, shape, dt, kind=kind).ap()

    def dbg(self, name, shape, dt=F32):
        ap = self.dram("dbg_" + name, shape, dt, kind="ExternalOutput")
        self.dbg_out[name] = ap
        return ap

    def at(self, sem):
        return self.cnt.get(sem, 0)

    def wait(self, engq, sem, val=None):
        engq.wait_ge(sem, self.at(sem) if val is None else val)

    def dbuf(self, name, nslots, channels=("y",)):
        if name not in self.dbufs:
            self.dbufs[name] = DBuf(self, name, nslots, channels)
        return self.dbufs[name]

    def fence(self):
        engs = [("tensor", self.T), ("vector", self.V), ("scalar", self.S),
                ("gpsimd", self.G), ("sync", self.Y)]
        for e, _ in engs:
            self.sem("bar_" + e)
        for s in sorted(self.hwdge_sems):
            self.wait(self.G, s)
        marks = {}
        for e, q in engs:
            q.drain().then_inc("bar_" + e, 1)
            marks[e] = self.at("bar_" + e)
        for e, q in engs:
            for o, _ in engs:
                if o != e:
                    q.wait_ge("bar_" + o, marks[o])

    def finalize(self):
        nc = self.nc
        block = self.es.enter_context(nc.Block())
        per = {e: [s for s in self.steps if s[0] == e] for e in self.ENGS}
        sems = self.sems

        def run(eng_obj, lst):
            for _, meth, a, kw, rec in lst:
                if meth == "wait_ge":
                    eng_obj.wait_ge(sems[a[0]], a[1])
                    continue
                instr = getattr(eng_obj, meth)(*a, **kw)
                for s, n in rec.incs:
                    instr.then_inc(sems[s], n)

        @block.tensor
        def _(e):
            run(e, per["tensor"])

        @block.vector
        def _(e):
            run(e, per["vector"])

        @block.scalar
        def _(e):
            run(e, per["scalar"])

        @block.gpsimd
        def _(e):
            run(e, per["gpsimd"])

        @block.sync
        def _(e):
            run(e, per["sync"])

        return nc


class Ctx:
    pass


def bank_split(k):
    return (k.tok // 2) * 4 >= 2048


def mm3(k, ps, whi_c, wlo_c, m1, m2, start, stop):
    T = k.T
    T.matmul(ps, whi_c, m1, start=start, stop=False)
    T.matmul(ps, wlo_c, m1, start=False, stop=False)
    return T.matmul(ps, whi_c, m2, start=False, stop=stop)


def emit_ln_split(k, cx, src_dram, ring, nchunks, out1, out2, dim,
                  gamma_cols=None, beta_cols=None, tag="", fout=None,
                  fstage=None):
    T, V, S, Y = k.T, k.V, k.S, k.Y
    tok, tokh = k.tok, k.tok // 2
    sA, sB = cx.lnsc1, cx.lnsc2
    mm_s, ev_s, sc_s, ld_s = ("lnmm" + tag, "lnev" + tag, "lnsc" + tag,
                              "lnld" + tag)
    for s in (mm_s, ev_s, sc_s):
        k.sem(s)
    db = k.dbuf(ld_s, 2)

    def load(c, con_eng):
        i = db.nexti()
        db.pre_load(Y, i)
        t = ring[:, (i % 2) * tok:(i % 2) * tok + tok]
        db.load_dma(Y, i, out=t, in_=src_dram[c * 128:(c + 1) * 128, :])
        db.ready(con_eng, i)
        return t, i

    iring = []
    for c in range(nchunks):
        x_c, i = load(c, V)
        iring.append(i)
        db.ready(T, i)
        sq = (sA if c % 2 == 0 else sB)[:, 0:tok]
        if c >= 2:
            db.wait_freed(V, iring[c - 1])
        V.tensor_tensor(sq, x_c, x_c, ALU.mult).then_inc(ev_s, 1)
        k.wait(T, ev_s)
        for half in range(2):
            hs = slice(half * tokh, (half + 1) * tokh)
            first = (c == 0 and (half == 0 or bank_split(k)))
            last = (c == nchunks - 1 and (half == 1 or bank_split(k)))
            T.matmul(cx.ps_pv[0:1, hs], cx.ones_col[:, :], x_c[:, hs],
                     start=first, stop=last)
            mmi = T.matmul(cx.ps_sg[0:1, hs], cx.ones_col[:, :], sq[:, hs],
                           start=first, stop=last)
        db.free(mmi, i)
    db.wait_freed(V, iring[-1])
    mu = cx.row(0)
    ex2 = cx.row(1)
    mmu = cx.row(2)
    rsg = cx.row(3)
    rsg2 = cx.row(4)
    V.tensor_scalar(mu, cx.ps_pv[0:1, 0:tok], 1.0 / dim, None, ALU.mult)
    V.tensor_scalar(ex2, cx.ps_sg[0:1, 0:tok], 1.0 / dim, None, ALU.mult)
    V.tensor_tensor(mmu, mu, mu, ALU.mult)
    V.tensor_tensor(ex2, ex2, mmu, ALU.subtract)
    V.tensor_scalar(ex2, ex2, LN_EPS, None, ALU.add)
    V.reciprocal(rsg, ex2)
    V.drain().then_inc(ev_s, 1)
    k.wait(S, ev_s)
    S.activation(rsg2, rsg, AF.Sqrt).then_inc(sc_s, 1)
    k.wait(T, sc_s)
    for half in range(2):
        sl = slice(half * tokh, (half + 1) * tokh)
        T.matmul(cx.ps_mm0[:, sl], cx.ones_row[:, :], mu[:, sl],
                 start=True, stop=True)
        bci = T.matmul(cx.ps_mm1[:, sl], cx.ones_row[:, :], rsg2[:, sl],
                       start=True, stop=True)
    bci.then_inc(mm_s, 1)
    k.wait(V, mm_s)
    mu_b = cx.bcA[:, 0:tok]
    rs_b = cx.bcB[:, 0:tok]
    V.tensor_copy(mu_b, cx.ps_mm0[:, 0:tok])
    V.tensor_copy(rs_b, cx.ps_mm1[:, 0:tok]).then_inc(ev_s, 1)
    k.wait(T, ev_s)
    st_s = "lnst" + tag
    k.sem(st_s)
    for c0 in range(0, nchunks, 2):
        cs = [c for c in (c0, c0 + 1) if c < nchunks]
        grp = []
        for c in cs:
            x_c, i = load(c, V)
            grp.append((x_c, (sA if c % 2 == 0 else sB)[:, 0:tok], c, i))
        for xc, t, c, i in grp:
            db.free(V.tensor_tensor(t, xc, mu_b, ALU.subtract), i)
        for xc, t, c, i in grp:
            V.tensor_tensor(t, t, rs_b, ALU.mult)
        if gamma_cols is not None:
            for xc, t, c, i in grp:
                V.tensor_scalar(t, t, gamma_cols(c), beta_cols(c),
                                ALU.mult, ALU.add)
        if fout is not None:
            for xc, t, c, i in grp:
                k.wait(V, st_s)
                stg = fstage[:, (c % 2) * tok:(c % 2) * tok + tok]
                V.tensor_copy(stg, t).then_inc(ev_s, 1)
                k.wait(Y, ev_s)
                Y.dma_start(out=fout[c * 128:(c + 1) * 128, :],
                            in_=stg).then_inc(st_s, 16)
            continue
        for xc, t, c, i in grp:
            V.tensor_copy(out1(c), t)
        for xc, t, c, i in grp:
            V.tensor_tensor(out2(c), t, out1(c),
                            ALU.subtract).then_inc(ev_s, 1)
    if fout is not None:
        k.wait(Y, st_s)
    return ev_s


def _dbg_fm(k, cx, name, tiles, nchunks):
    V, Y = k.V, k.Y
    tok = k.tok
    d = k.dbg(name, [nchunks * 128, tok], F32)
    s = "dbgs"
    k.sem(s)
    for c in range(nchunks):
        t = (cx.lnsc1 if c % 2 == 0 else cx.lnsc2)[:, 0:tok]
        if c >= 2:
            k.wait(V, s)
        a, b = tiles(c)
        V.tensor_tensor(t, a, b, ALU.add).then_inc("ev", 1)
        k.wait(Y, "ev")
        Y.dma_start(out=d[c * 128:(c + 1) * 128, :], in_=t).then_inc(s, 16)
    k.wait(V, s)


def _emit_trig_tables(k, cx, tokoff):
    T, V, S, G, Y = k.T, k.V, k.S, k.G, k.Y
    tok = k.tok
    tokh = tok // 2
    k.sem("tr")
    k.sem("trs")
    G.iota(cx.row(5), [[1, tok]], base=0, channel_multiplier=0,
           allow_small_or_imprecise_dtypes=True)
    G.iota(cx.row(6)[0:1, 0:64], [[0, 2], [1, 32]], base=0, channel_multiplier=0,
           allow_small_or_imprecise_dtypes=True).then_inc("tr", 1)
    Y.dma_start(out=cx.row(7)[0:1, 0:1], in_=tokoff[:, :]).then_inc("tr", 16)
    k.wait(V, "tr")
    trow = cx.row(5)
    V.tensor_scalar(trow, trow, cx.row(7)[0:1, 0:1], None, ALU.add)
    V.drain().then_inc("tr", 1)
    LNC1, LNC2 = 0.2878231406211853, -3.9969294540753708e-09
    V.tensor_scalar(cx.row(7)[0:1, 0:64], cx.row(6)[0:1, 0:64], -LNC1, None,
                    ALU.mult)
    V.scalar_tensor_tensor(cx.row(6)[0:1, 0:64], cx.row(6)[0:1, 0:64], -LNC2,
                           cx.row(7)[0:1, 0:64], ALU.mult, ALU.add)
    V.drain().then_inc("tr", 1)
    k.wait(S, "tr")
    S.activation(cx.row(6)[0:1, 0:64], cx.row(6)[0:1, 0:64],
                 AF.Exp).then_inc("trs", 1)
    k.wait(T, "trs")
    k.wait(T, "tr")
    for half in range(2):
        hs = slice(half * tokh, (half + 1) * tokh)
        r = T.matmul(cx.ps_pv[0:64, hs], cx.row(6)[0:1, 0:64],
                     trow[:, hs], start=True, stop=True)
    r.then_inc("tr", 1)
    k.wait(V, "tr")
    ang = cx.lnsc1[0:64, 0:tok]
    u = cx.lnsc2[0:64, 0:tok]
    w = cx.bcA[0:64, 0:tok]
    cmp = cx.bcB[0:64, 0:tok]
    V.tensor_copy(ang, cx.ps_pv[0:64, 0:tok])
    V.tensor_scalar(u, ang, 1.0 / (2 * PI), 0.5, ALU.mult, ALU.add)
    V.tensor_copy(w, u)
    for j in range(11, -1, -1):
        V.tensor_scalar(cmp, w, float(2 ** j), None, ALU.is_ge)
        V.scalar_tensor_tensor(w, cmp, -float(2 ** j), w, ALU.mult, ALU.add)
    V.tensor_tensor(u, u, w, ALU.subtract)
    CW1, CW2, CW3 = 6.28125, 0.0019350051879882812, 3.0199160505617328e-07
    V.scalar_tensor_tensor(w, u, -CW1, ang, ALU.mult, ALU.add)
    V.scalar_tensor_tensor(w, u, -CW2, w, ALU.mult, ALU.add)
    V.scalar_tensor_tensor(u, u, -CW3, w, ALU.mult, ALU.add)
    for _ in range(2):
        V.tensor_scalar(cmp, u, PI, None, ALU.is_gt)
        V.scalar_tensor_tensor(u, cmp, -2 * PI, u, ALU.mult, ALU.add)
        V.tensor_scalar(cmp, u, -PI, None, ALU.is_lt)
        V.scalar_tensor_tensor(u, cmp, 2 * PI, u, ALU.mult, ALU.add)
    ca = cx.lnsc1[64:128, 0:tok]
    V.tensor_scalar(ca, u[0:64, 0:tok].base_shift(64) if False else u, PI / 2,
                    None, ALU.add) if False else None
    ca = w
    V.tensor_scalar(ca, u, PI / 2, None, ALU.add)
    V.tensor_scalar(cmp, ca, PI, None, ALU.is_gt)
    V.scalar_tensor_tensor(ca, cmp, -2 * PI, ca, ALU.mult, ALU.add)
    V.drain().then_inc("tr", 1)
    k.wait(S, "tr")
    cosd = cx.trig[0:64, 0:1024]
    sind = cx.trig[0:64, 1024:2048]
    S.activation(cosd[0:64, 0:tok], ca, AF.Sin)
    S.activation(sind[0:64, 0:tok], u, AF.Sin).then_inc("trs", 1)
    k.wait(V, "trs")
    V.tensor_scalar(sind[0:32, 0:tok], sind[0:32, 0:tok], -1.0, None, ALU.mult)
    V.drain().then_inc("tr", 1)
    k.wait(T, "tr")




def emit_attention(k, cx, af32, abf, tag, rope, q_scr, qn_scr, nga_out,
                   kga_outs, vga_outs, KROWS, cc0, out_writer):
    T, V, S, G, Y = k.T, k.V, k.S, k.G, k.Y
    tok = k.tok
    tokh = tok // 2
    NKC = tok // 128

    gn = af32.alloc(tok)
    k.wait(Y, "cc", val=cc0 + 7)
    Y.dma_start(out=gn[0:NCORES * NH, 0:tok], in_=nga_out[:, :]).then_inc("xld", 16)
    k.wait(V, "xld")
    V.tensor_reduce(cx.lnsc1[0:NCORES * NH, 0:1], gn[0:NCORES * NH, 0:tok],
                    AX.X, ALU.max)
    V.drain().then_inc("ev", 1)
    k.wait(T, "ev")
    T.transpose(cx.ps_sg[0:1, 0:128], cx.lnsc1[:, 0:1],
                cx.permt[:, :]).then_inc("mm", 1)
    k.wait(V, "mm")
    V.tensor_copy(cx.knrow[0:1, 0:128], cx.ps_sg[0:1, 0:128])
    for h in range(NH):
        V.tensor_reduce(cx.knrow[0:1, 128 + h:129 + h],
                        cx.knrow[0:1, h * NCORES:(h + 1) * NCORES], AX.X, ALU.max)
    V.drain().then_inc("ev", 1)
    k.wait(S, "ev")
    S.activation(cx.knrow[0:1, 128:128 + NH], cx.knrow[0:1, 128:128 + NH],
                 AF.Sqrt).then_inc("sc", 1)
    k.wait(V, "sc")
    k.wait(T, "sc")
    kn_s = cx.knrow[0:1, 128:128 + NH]

    nq = 4 if rope else 2
    qdb = k.dbuf("atqdb", 1)
    qslot = abf.alloc(nq * tok)
    nkv = 5 if rope else 4
    kvdb = k.dbuf("atkvdb", 2, channels=("y", "g", "s"))
    kvslots = [abf.alloc(nkv * tok) for _ in range(2)]
    pf32 = [af32.alloc(tok) for _ in range(3)]
    p1r = [abf.alloc(tok) for _ in range(3)]
    p2r = [abf.alloc(tok) for _ in range(3)]
    sigr = [af32.alloc(tok) for _ in range(2)]
    sigi = af32.alloc(tok)
    sd_s, pe_s, pp_s, pu_s, hf_s, mh_s = ("atsd", "atpe", "atpp", "atpu",
                                          "athf", "atmh")
    pc_s, pg_s, sg_s = "atpc", "atpg", "atsg"
    for s in (sd_s, pe_s, pp_s, pu_s, hf_s, mh_s, pc_s, pg_s, sg_s):
        k.sem(s)
    nsd, npe = k.at(sd_s), k.at(pe_s)
    nps, npu = k.at(pp_s), k.at(pu_s)
    npc, npg, nsg = k.at(pc_s), k.at(pg_s), k.at(sg_s)
    hf0 = k.at(hf_s)
    jloc = 0
    ev_sigi = [k.at("ev")]
    pend_kv = [None]
    pend_pv = []

    def emit_pv(pv):
        nonlocal npu
        k.wait(T, pp_s, val=pv["ppt"])
        if pv["first"] and pv["h"] > 0:
            k.wait(T, "ev")
        for qb in range(2):
            qsl = slice(qb * tokh, (qb + 1) * tokh)
            rpv = T.matmul(cx.ps_pv[:, qsl], pv["v1"][:, pv["kcs"]],
                           pv["p1"][:, qsl],
                           start=(pv["first"] and (qb == 0 or bank_split(k))),
                           stop=False)
            if qb == 0 and pend_kv[0] is not None:
                kvdb.free(rpv, pend_kv[0])
                pend_kv[0] = None
            T.matmul(cx.ps_pv[:, qsl], pv["v2"][:, pv["kcs"]],
                     pv["p1"][:, qsl], start=False, stop=False)
            r3 = T.matmul(cx.ps_pv[:, qsl], pv["v1"][:, pv["kcs"]],
                          pv["p2"][:, qsl], start=False,
                          stop=(pv["last"] and (qb == 1 or bank_split(k))))
        r3.then_inc(pu_s, 1)
        npu += 1
        if pv["endrr"]:
            pend_kv[0] = pv["gi"]

    mr2 = [cx.mrow, cx.mrow2]
    mh_ev = [0] * (NH + 1)
    head_pu0 = []

    def mprep(hh):
        if hh > 0:
            k.wait(Y, "ev", val=mh_ev[hh - 1])
        Y.dma_start(out=cx.bcB[0:1, 0:tok],
                    in_=qn_scr[hh:hh + 1, :]).then_inc(mh_s, 16)
        k.wait(S, mh_s)
        S.activation(cx.bcB[0:1, 0:tok], cx.bcB[0:1, 0:tok],
                     AF.Sqrt).then_inc("sc", 1)
        k.wait(V, "sc")
        if hh >= 2:
            k.wait(V, pu_s, val=head_pu0[hh - 1])
        V.tensor_scalar(mr2[hh % 2][0:1, 0:tok], cx.bcB[0:1, 0:tok],
                        kn_s[0:1, hh:hh + 1], -1.0, ALU.mult, ALU.mult)
        V.drain().then_inc("ev", 1)
        mh_ev[hh] = k.at("ev")

    for h in range(NH):
        head_pu0.append(npu)
        if h == 0:
            mprep(0)
        if h > 0 or hf0 > 0:
            k.wait(Y, hf_s, val=hf0 + h)
        k.wait(T, "ev", val=mh_ev[h])
        qi = qdb.nexti()
        qdb.pre_load(Y, qi)
        for kind in range(nq):
            qdb.load_dma(Y, qi, out=qslot[:, kind * tok:(kind + 1) * tok],
                         in_=q_scr[(h * 4 + kind) * 128:(h * 4 + kind + 1) * 128, :])
        qc1, qc2 = qslot[:, 0:tok], qslot[:, tok:2 * tok]
        if rope:
            qra, qrb = qslot[:, 2 * tok:3 * tok], qslot[:, 3 * tok:4 * tok]
        qdb.ready(T, qi)
        kidx_h0 = npu
        kidx_loc0 = jloc
        kga_out = kga_outs[h // 4]
        vga_out = vga_outs[h // 4]
        for rr in range(NCORES):
            gi = kvdb.nexti()
            ch, le = "y", Y
            g4 = h // 4
            k.wait(le, "cc", val=cc0 + (9 if g4 == 3 else 2 * (g4 + 1)))
            kvdb.pre_load(le, gi)
            kv = kvslots[gi % 2]
            kb = rr * (NH // 4) * KROWS + (h % 4) * KROWS
            kvdb.load_dma(le, gi, ch=ch, out=kv[:, 0:tok],
                          in_=kga_out[kb:kb + 128, :])
            kvdb.load_dma(le, gi, ch=ch, out=kv[:, tok:2 * tok],
                          in_=kga_out[kb + 128:kb + 256, :])
            if rope:
                kvdb.load_dma(le, gi, ch=ch, out=kv[0:64, 2 * tok:3 * tok],
                              in_=kga_out[kb + 256:kb + 320, :])
                kvdb.load_dma(le, gi, ch=ch, out=kv[64:128, 2 * tok:3 * tok],
                              in_=kga_out[kb + 320:kb + 384, :])
            vof = 3 * tok if rope else 2 * tok
            vb = rr * (NH // 4) * 2 * tok + (h % 4) * 2 * tok
            kvdb.load_dma(le, gi, ch=ch, out=kv[:, vof:vof + tok],
                          in_=vga_out[vb:vb + tok, :].rearrange("(t p) d -> p t d", p=128))
            kvdb.load_dma(le, gi, ch=ch, out=kv[:, vof + tok:vof + 2 * tok],
                          in_=vga_out[vb + tok:vb + 2 * tok, :].rearrange("(t p) d -> p t d", p=128))
            if gi >= 2:
                while pend_pv:
                    emit_pv(pend_pv.pop(0))
            kvdb.ready(T, gi)
            khi, klo = kv[:, 0:tok], kv[:, tok:2 * tok]
            if rope:
                krp = kv[:, 2 * tok:3 * tok]
            v1, v2 = kv[:, vof:vof + tok], kv[:, vof + tok:vof + 2 * tok]
            for kc in range(NKC):
                kidx = npe
                first = rr == 0 and kc == 0
                last = rr == NCORES - 1 and kc == NKC - 1
                ps = cx.ps_mm0 if kidx % 2 == 0 else cx.ps_mm1
                if kidx >= 2:
                    k.wait(T, pe_s, val=npe - 1)
                if h > 0 and jloc - kidx_loc0 < 2:
                    k.wait(T, "ev", val=ev_sigi[0])
                kcs = slice(kc * 128, (kc + 1) * 128)
                for qb in range(2):
                    qsl = slice(qb * tokh, (qb + 1) * tokh)
                    pss = ps[:, qsl]
                    T.matmul(pss, khi[:, kcs], qc1[:, qsl],
                             start=(qb == 0 or bank_split(k)), stop=False)
                    T.matmul(pss, klo[:, kcs], qc1[:, qsl], start=False, stop=False)
                    T.matmul(pss, khi[:, kcs], qc2[:, qsl], start=False, stop=False)
                    if rope:
                        T.matmul(pss, krp[:, kcs], qra[:, qsl], start=False, stop=False)
                        T.matmul(pss, krp[:, kcs], qrb[:, qsl], start=False, stop=False)
                    r2 = T.matmul(pss, cx.ones_rowb[:, :],
                                  mr2[h % 2][0:1, qsl],
                                  start=False, stop=(qb == 1 or bank_split(k)))
                r2.then_inc(sd_s, 1)
                nsd += 1
                if len(pend_pv) >= 2:
                    emit_pv(pend_pv.pop(0))
                k.wait(S, sd_s, val=nsd)
                pf = pf32[jloc % 3]
                p1, p2 = p1r[jloc % 3], p2r[jloc % 3]
                if jloc >= 3:
                    k.wait(S, pu_s, val=npu - 1)
                    k.wait(S, pg_s, val=npg - 2)
                S.activation(pf[:, 0:tok], ps[:, 0:tok], AF.Exp).then_inc(pe_s, 1)
                npe += 1
                S.activation(p1[:, 0:tok], pf[:, 0:tok],
                             AF.Copy).then_inc(pc_s, 1)
                npc += 1
                k.wait(G, pe_s, val=npe)
                Rh = sigr[h % 2]
                if first:
                    if h >= 2:
                        k.wait(G, sg_s, val=nsg - 1)
                    G.tensor_copy(Rh[:, 0:tok], pf[:, 0:tok]).then_inc(pg_s, 1)
                else:
                    G.tensor_tensor(Rh[:, 0:tok], Rh[:, 0:tok], pf[:, 0:tok],
                                    ALU.add).then_inc(pg_s, 1)
                npg += 1
                k.wait(V, pc_s, val=npc)
                if jloc >= 3:
                    k.wait(V, pu_s, val=npu - 1)
                V.tensor_tensor(p2[:, 0:tok], pf[:, 0:tok], p1[:, 0:tok],
                                ALU.subtract).then_inc(pp_s, 1)
                nps += 1
                pend_pv.append({"first": first, "last": last, "h": h,
                                "v1": v1, "v2": v2, "kcs": kcs,
                                "p1": p1, "p2": p2, "gi": gi,
                                "endrr": kc == NKC - 1, "ppt": nps})
                jloc += 1
            if rr == 1 and h + 1 < NH:
                mprep(h + 1)
        while pend_pv:
            emit_pv(pend_pv.pop(0))
        k.wait(T, pg_s, val=npg)
        k.wait(T, "ev")
        r5s = []
        for half in range(2):
            hs = slice(half * tokh, (half + 1) * tokh)
            r5s.append(T.matmul(cx.ps_sg[0:1, hs], cx.ones_col[:, :],
                                Rh[:, hs], start=True, stop=True))
        if pend_kv[0] is not None:
            kvdb.free(r5s[0], pend_kv[0])
            pend_kv[0] = None
        r5s[1].then_inc(sg_s, 1)
        nsg += 1
        k.wait(V, sg_s, val=nsg)
        V.reciprocal(cx.row(7), cx.ps_sg[0:1, 0:tok]).then_inc("ev", 1)
        k.wait(T, "ev")
        bb = cx.ps_mm1 if npe % 2 == 0 else cx.ps_mm0
        r4s = []
        for half in range(2):
            hs = slice(half * tokh, (half + 1) * tokh)
            r4s.append(T.matmul(bb[:, hs], cx.ones_row[:, :],
                                cx.row(7)[0:1, hs], start=True, stop=True))
        qdb.free(r4s[0], qi)
        r4s[1].then_inc(hf_s, 1)
        k.wait(V, hf_s)
        V.tensor_copy(sigi[:, 0:tok], bb[:, 0:tok]).then_inc("ev", 1)
        ev_sigi[0] = k.at("ev")
        out_writer(h, cx.ps_pv, sigi)


class WQState:
    pass


def wq_setup(k, cx, shards, dims):
    T, V, S, G, Y = k.T, k.V, k.S, k.G, k.Y
    st = WQState()
    st.ring = k.sbuf("wqring", [128, 1024], F32)
    st.tq = k.sbuf("wqtq", [128, 1024], BF16)
    st.acc = k.sbuf("wqacc", [128, 128], F32)
    st.dbs = k.dbuf("wqs", 1, channels=("s",))
    st.shards, st.dims = shards, dims
    st.nst = 0
    st.scale_units = []
    st.quant_units = []
    st.bases = []
    k.sem("wqev")
    base = 0
    for i, ((wsh, tg, tfull), (nr, ncl, mode)) in enumerate(zip(shards, dims)):
        nr8 = nr // NCORES
        nrc, cw = nr8 // 128, min(ncl, 1024)
        ncc = max(ncl // 1024, 1)
        ntl = nrc * ncc
        st.bases.append((base, ntl))
        for rc in range(nrc):
            for cc in range(ncc):
                def su(wsh=wsh, rc=rc, cc=cc, cw=cw, col=base + rc * ncc + cc):
                    ti = st.dbs.nexti()
                    st.dbs.pre_load(S, ti)
                    t = st.ring[:, 0:cw]
                    st.dbs.load_dma(S, ti, ch="s", out=t,
                                    in_=wsh[rc * 128:(rc + 1) * 128,
                                            cc * cw:(cc + 1) * cw])
                    st.dbs.ready(V, ti)
                    st.dbs.free(
                        V.tensor_reduce(st.acc[:, col:col + 1], t, AX.X,
                                        ALU.add, apply_absolute_value=True),
                        ti)
                st.scale_units.append(su)

                def qu(wsh=wsh, tg=tg, rc=rc, cc=cc, cw=cw, i=i):
                    ti = st.dbs.nexti()
                    st.dbs.pre_load(S, ti)
                    t = st.ring[:, 0:cw]
                    st.dbs.load_dma(S, ti, ch="s", out=t,
                                    in_=wsh[rc * 128:(rc + 1) * 128,
                                            cc * cw:(cc + 1) * cw])
                    st.dbs.ready(V, ti)
                    if st.nst >= 1:
                        k.wait(V, "wqst", val=16 * st.nst)
                    rcol = cx.wq[:, 2 * i + 1:2 * i + 2]
                    sAq = cx.bcA[:, 0:cw]
                    sBq = cx.bcB[:, 0:cw]
                    to = st.tq[:, 0:cw]
                    V.tensor_scalar(sAq, t, rcol, 0.5, ALU.mult, ALU.is_ge)
                    st.dbs.free(V.tensor_scalar(sBq, t, rcol, -0.5,
                                                ALU.mult, ALU.is_le), ti)
                    V.tensor_tensor(to, sAq, sBq,
                                    ALU.subtract).then_inc("wqev", 1)
                    k.wait(Y, "wqev")
                    Y.dma_start(out=tg[rc * 128:(rc + 1) * 128,
                                       cc * cw:(cc + 1) * cw],
                                in_=to).then_inc("wqst", 16)
                    st.nst += 1
                st.quant_units.append(qu)
        base += ntl + 1
    return st


def wq_scales_finish(k, cx, st, wqsum_in, wqsum_out):
    T, V, S, G, Y = k.T, k.V, k.S, k.G, k.Y
    dims = st.dims
    prow = cx.row(4)
    G.memset(prow[0:1, 7:8], 1.0).then_inc("gp", 1)
    for base, ntl in st.bases:
        V.tensor_reduce(st.acc[:, base + ntl:base + ntl + 1],
                        st.acc[:, base:base + ntl], AX.X, ALU.add)
    V.drain().then_inc("ev", 1)
    k.wait(T, "ev")
    for i, (base, ntl) in enumerate(st.bases):
        T.matmul(cx.ps_sg[0:1, i:i + 1], cx.ones_col[:, :],
                 st.acc[:, base + ntl:base + ntl + 1],
                 start=True, stop=True).then_inc("mm", 1)
    k.wait(V, "mm")
    for i in range(7):
        V.tensor_copy(prow[0:1, i:i + 1], cx.ps_sg[0:1, i:i + 1])
    V.drain().then_inc("ev", 1)
    k.wait(Y, "ev")
    k.wait(Y, "gp")
    Y.dma_start(out=wqsum_in[0:1, 0:8], in_=prow[0:1, 0:8]).then_inc("st", 16)
    k.wait(G, "st")
    G.collective_compute("AllGather", ALU.bypass,
                         replica_groups=[list(range(NCORES))],
                         ins=[wqsum_in.opt()],
                         outs=[wqsum_out.opt()]).then_inc("ccq", 1)
    k.wait(Y, "ccq")
    Y.dma_start(out=cx.bcA[0:8, 0:8], in_=wqsum_out[:, :]).then_inc("xld", 16)
    k.wait(T, "xld")
    T.matmul(cx.ps_sg[0:1, 0:8], cx.ones_col[0:8, 0:1], cx.bcA[0:8, 0:8],
             start=True, stop=True).then_inc("mm", 1)
    for i in range(8):
        v = 1.0 / (dims[i][0] * dims[i][1]) if i < len(dims) else 1.0
        G.memset(cx.row(5)[0:1, i:i + 1], v)
    G.drain().then_inc("gp", 1)
    k.wait(V, "mm")
    k.wait(V, "gp")
    m8 = cx.row(6)[0:1, 0:16]
    V.tensor_tensor(m8[0:1, 0:8], cx.ps_sg[0:1, 0:8], cx.row(5)[0:1, 0:8],
                    ALU.mult)
    V.tensor_scalar(m8[0:1, 0:3], m8[0:1, 0:3], 1e-8, None, ALU.add)
    V.tensor_scalar(m8[0:1, 3:7], m8[0:1, 3:7], 1e-5, None, ALU.max)
    V.reciprocal(m8[0:1, 8:16], m8[0:1, 0:8])
    V.drain().then_inc("ev", 1)
    k.wait(T, "ev")
    T.matmul(cx.ps_mm0[:, 0:16], cx.ones_row[:, :], m8[0:1, 0:16],
             start=True, stop=True).then_inc("mm", 1)
    k.wait(V, "mm")
    for i in range(7):
        V.tensor_copy(cx.wq[:, 2 * i:2 * i + 1], cx.ps_mm0[:, i:i + 1])
        V.tensor_copy(cx.wq[:, 2 * i + 1:2 * i + 2],
                      cx.ps_mm0[:, 8 + i:9 + i])
    V.drain().then_inc("ev", 1)
    k.wait(T, "ev")
    k.wait(G, "ev")
    k.wait(Y, "ev")


def emit_ffn(k, cx, af32, abf, xn1, xn2, x_in, x_out, t_up, t_gate, t_down,
             ffnacc, tag):
    T, V, S, G, Y = k.T, k.V, k.S, k.G, k.Y
    k.wait(Y, "ccq")
    tok = k.tok
    tokh = tok // 2
    ET = 512
    NET = DE // ET
    ECT = ET // 128

    af32.push()
    abf.push()
    af32.push()
    ring = af32.alloc(2 * tok)
    emit_ln_split(k, cx, x_in, ring, DC,
                  lambda c: xn1[:, c * tok:(c + 1) * tok],
                  lambda c: xn2[:, c * tok:(c + 1) * tok], D, tag="ln")
    af32.pop()
    k.fence()

    su_col = cx.wq[:, 0:1]
    sg_col = cx.wq[:, 2:3]
    sd_col = cx.wq[:, 4:5]
    hid1 = abf.alloc(ET * tok // 128)
    hid2 = abf.alloc(ET * tok // 128)
    tdb = k.dbuf("fftdb", 2, channels=("y", "g"))
    tslots = [abf.alloc(2 * DC * 128) for _ in range(2)]
    ddb = k.dbuf("ffddb", 2, channels=("s",))
    dslots = [abf.alloc(ECT * 128) for _ in range(2)]
    af32.push()
    su_sb2 = [af32.alloc(tok), af32.alloc(tok)]
    hf = af32.alloc(tok)
    stg = af32.alloc(2 * tok)
    hs_s, rs_s = "ffhs", "ffrs"
    for s in (hs_s, rs_s):
        k.sem(s)
    for j in range(4):
        k.sem(f"ffdnc{j}")
    nhs = k.at(hs_s)
    gec0 = 1 if k.at(hs_s) > 0 else 0

    for et in range(NET):
        for ec in range(ECT):
            gec = tdb.nexti()
            e0 = et * ET + ec * 128
            tdb.pre_load(Y, gec)
            tdb.pre_load(G, gec)
            ts_ = tslots[gec % 2]
            tdb.load_dma(Y, gec, ch="y", out=ts_[:, 0:DC * 128].rearrange("p (c e) -> p c e", e=128),
                         in_=t_up[:, e0:e0 + 128].rearrange("(c p) e -> p c e", p=128))
            tdb.load_dma(G, gec, ch="g", out=ts_[:, DC * 128:2 * DC * 128].rearrange("p (c e) -> p c e", e=128),
                         in_=t_gate[:, e0:e0 + 128].rearrange("(c p) e -> p c e", p=128))
            tdb.ready(T, gec)
            tu = ts_[:, 0:DC * 128]
            tg = ts_[:, DC * 128:2 * DC * 128]
            if ec == 0:
                k.wait(T, "ev")
            if gec >= 1 or gec0:
                k.wait(T, hs_s, val=nhs)
            for c in range(DC):
                cs = slice(c * 128, (c + 1) * 128)
                for half in range(2):
                    hsl = slice(half * tokh, (half + 1) * tokh)
                    st0 = (c == 0 and (half == 0 or bank_split(k)))
                    sp0 = (c == DC - 1 and (half == 1 or bank_split(k)))
                    T.matmul(cx.ps_mm0[:, hsl], tu[:, cs],
                             xn1[:, c * tok + half * tokh:c * tok + half * tokh + tokh],
                             start=st0, stop=False)
                    ru = T.matmul(cx.ps_mm0[:, hsl], tu[:, cs],
                                  xn2[:, c * tok + half * tokh:c * tok + half * tokh + tokh],
                                  start=False, stop=sp0)
            ru.then_inc("mm", 1)
            su_sb = su_sb2[gec % 2]
            k.wait(S, "mm")
            S.activation(su_sb[:, 0:tok], cx.ps_mm0[:, 0:tok], AF.Silu,
                         scale=su_col).then_inc(hs_s, 1)
            nhs += 1
            k.wait(T, "ev")
            for c in range(DC):
                cs = slice(c * 128, (c + 1) * 128)
                for half in range(2):
                    hsl = slice(half * tokh, (half + 1) * tokh)
                    st0 = (c == 0 and (half == 0 or bank_split(k)))
                    sp0 = (c == DC - 1 and (half == 1 or bank_split(k)))
                    T.matmul(cx.ps_mm1[:, hsl], tg[:, cs],
                             xn1[:, c * tok + half * tokh:c * tok + half * tokh + tokh],
                             start=st0, stop=False)
                    rg = T.matmul(cx.ps_mm1[:, hsl], tg[:, cs],
                                  xn2[:, c * tok + half * tokh:c * tok + half * tokh + tokh],
                                  start=False, stop=sp0)
            tdb.free(rg, gec)
            tdb.wait_freed(V, gec)
            k.wait(V, hs_s, val=nhs)
            h1 = hid1[:, ec * tok:(ec + 1) * tok]
            h2 = hid2[:, ec * tok:(ec + 1) * tok]
            V.scalar_tensor_tensor(hf[:, 0:tok], cx.ps_mm1[:, 0:tok], sg_col,
                                   su_sb[:, 0:tok], ALU.mult, ALU.mult)
            V.tensor_copy(h1, hf[:, 0:tok])
            V.tensor_tensor(h2, hf[:, 0:tok], h1, ALU.subtract).then_inc("ev", 1)
        k.wait(T, "ev")
        ev_d0 = k.at("ev")
        for j in range(DC):
            dj = ddb.nexti()
            ddb.pre_load(S, dj)
            ds = dslots[dj % 2]
            ddb.load_dma(S, dj, ch="s", out=ds[:, 0:ECT * 128].rearrange("p (a e) -> p a e", e=128),
                         in_=t_down[et * ET:(et + 1) * ET, j * 128:(j + 1) * 128].rearrange("(a p) e -> p a e", p=128))
            ddb.ready(T, dj)
            ps_d = cx.ps_pv if j % 2 == 0 else cx.ps_mm0
            if j >= 2:
                k.wait(T, "ev", val=ev_d0 + j - 1)
            else:
                k.wait(T, "ev")
            for ec in range(ECT):
                ecs = slice(ec * 128, (ec + 1) * 128)
                for half in range(2):
                    hsl = slice(half * tokh, (half + 1) * tokh)
                    st0 = (ec == 0 and (half == 0 or bank_split(k)))
                    sp0 = (ec == ECT - 1 and (half == 1 or bank_split(k)))
                    T.matmul(ps_d[:, hsl], ds[:, ecs],
                             hid1[:, ec * tok + half * tokh:ec * tok + half * tokh + tokh],
                             start=st0, stop=False)
                    rd = T.matmul(ps_d[:, hsl], ds[:, ecs],
                                  hid2[:, ec * tok + half * tokh:ec * tok + half * tokh + tokh],
                                  start=False, stop=sp0)
            ddb.free(rd, dj)
            ddb.wait_freed(V, dj)
            if et > 0 or j >= 2:
                k.wait(V, f"ffdnc{(j - 2) % 4}")
            so = stg[:, (j % 2) * tok:(j % 2) * tok + tok]
            V.tensor_scalar(so, ps_d[:, 0:tok], sd_col, None,
                            ALU.mult).then_inc("ev", 1)
            k.wait(G, "ev")
            if et > 0:
                k.wait(G, f"ffdnc{j % 4}")
            G.dma_start(out=ffnacc[j * 128:(j + 1) * 128, :], in_=so,
                        accum_op=(ALU.add if et > 0 else ALU.bypass)).then_inc(f"ffdnc{j % 4}", 16)
    for j in range(4):
        k.wait(Y, f"ffdnc{j}")
    af32.pop()
    k.fence()
    xres = af32.alloc(2 * tok)
    xo = af32.alloc(2 * tok)
    xdb = k.dbuf("ffxdb", 2)
    k.sem(rs_s)
    for j in range(DC):
        xi = xdb.nexti()
        xdb.pre_load(Y, xi)
        xdb.load_dma(Y, xi, out=xres[:, (j % 2) * tok:(j % 2) * tok + tok],
                     in_=x_in[j * 128:(j + 1) * 128, :])
        xdb.load_dma(Y, xi, out=xo[:, (j % 2) * tok:(j % 2) * tok + tok],
                     in_=ffnacc[j * 128:(j + 1) * 128, :])
        xdb.ready(V, xi)
        k.wait(V, rs_s)
        vv = V.tensor_tensor(xo[:, (j % 2) * tok:(j % 2) * tok + tok],
                             xo[:, (j % 2) * tok:(j % 2) * tok + tok],
                             xres[:, (j % 2) * tok:(j % 2) * tok + tok], ALU.add)
        xdb.free(vv, xi)
        xdb.wait_freed(Y, xi)
        Y.dma_start(out=x_out[j * 128:(j + 1) * 128, :],
                    in_=xo[:, (j % 2) * tok:(j % 2) * tok + tok]).then_inc(rs_s, 16)
    k.wait(Y, rs_s)
    abf.pop()
    af32.pop()
    k.fence()


def emit_actquant(k, cx, af32, abf, src_dram, xq1, gb, tag):
    T, V, S, G, Y = k.T, k.V, k.S, k.G, k.Y
    tok = k.tok
    tokh = tok // 2
    ring = af32.alloc(2 * tok)
    sb = af32.alloc(tok)
    db = k.dbuf("lnldln", 2)

    def load(c, eng):
        i = db.nexti()
        db.pre_load(Y, i)
        t = ring[:, (i % 2) * tok:(i % 2) * tok + tok]
        db.load_dma(Y, i, out=t, in_=src_dram[c * 128:(c + 1) * 128, :])
        db.ready(eng, i)
        return t, i

    Ma, Mb = cx.lnsc1[:, 0:tok], cx.lnsc2[:, 0:tok]
    for c in range(DC):
        x_c, i = load(c, S)
        S.activation(x_c, x_c, AF.Abs).then_inc("sc", 1)
        k.wait(V, "sc")
        M = Ma if c % 2 == 0 else Mb
        if c < 2:
            vmx = V.tensor_copy(M, x_c)
        else:
            vmx = V.tensor_tensor(M, M, x_c, ALU.max)
        db.free(vmx, i)
    V.tensor_tensor(cx.bcA[:, 0:tok], Ma, Mb, ALU.max)
    V.drain().then_inc("ev", 1)
    k.wait(T, "ev")
    nb = tok // 128
    for b in range(nb):
        T.transpose(cx.ps_mm0[:, b * 128:(b + 1) * 128],
                    cx.bcA[:, b * 128:(b + 1) * 128],
                    cx.identt[:, :]).then_inc("mm", 1)
    k.wait(V, "mm")
    for b in range(nb):
        V.tensor_reduce(cx.bcB[:, b:b + 1], cx.ps_mm0[:, b * 128:(b + 1) * 128],
                        AX.X, ALU.max)
    V.drain().then_inc("ev", 1)
    k.wait(T, "ev")
    T.transpose(cx.ps_mm1[0:nb, 0:128], cx.bcB[:, 0:nb],
                cx.identt[:, :]).then_inc("mm", 1)
    k.wait(V, "mm")
    V.tensor_copy(cx.lnsc1[0:nb, 0:128], cx.ps_mm1[0:nb, 0:128])
    V.drain().then_inc("ev", 1)
    k.wait(Y, "ev")
    aq_s = "st"
    k.sem(aq_s)
    Y.dma_start(out=cx.row(5), in_=cx.lnsc1[0:nb, 0:128]).then_inc(aq_s, 16)
    k.wait(V, aq_s)
    mrowq = cx.row(5)
    srow = cx.row(6)
    grow = cx.row(7)
    V.tensor_scalar(mrowq, mrowq, 1e-5, None, ALU.max)
    V.drain()
    V.reciprocal(srow, mrowq)
    V.tensor_scalar(grow, mrowq, 1.0 / 127.0, None, ALU.mult)
    V.drain()
    V.tensor_scalar(srow, srow, 127.0, None, ALU.mult)
    V.drain().then_inc("ev", 1)
    k.wait(T, "ev")
    for half in range(2):
        hs = slice(half * tokh, (half + 1) * tokh)
        T.matmul(cx.ps_mm0[:, hs], cx.ones_row[:, :], srow[0:1, hs],
                 start=True, stop=True)
        rb = T.matmul(cx.ps_mm1[:, hs], cx.ones_row[:, :], grow[0:1, hs],
                      start=True, stop=True)
    rb.then_inc("mm", 1)
    k.wait(V, "mm")
    V.tensor_copy(sb[:, 0:tok], cx.ps_mm0[:, 0:tok])
    V.tensor_copy(gb[:, 0:tok], cx.ps_mm1[:, 0:tok]).then_inc("ev", 1)
    k.wait(T, "ev")
    for c0 in range(0, DC, 2):
        grp = []
        for c in (c0, c0 + 1):
            x_c, i = load(c, V)
            w = (cx.lnsc1 if c % 2 == 0 else cx.lnsc2)[:, 0:tok]
            cmp = (cx.bcA if c % 2 == 0 else cx.bcB)[:, 0:tok]
            grp.append((x_c, w, cmp, c, i))
        for x_c, w, cmp, c, i in grp:
            V.tensor_tensor(x_c, x_c, sb[:, 0:tok], ALU.mult)
        for x_c, w, cmp, c, i in grp:
            V.tensor_scalar(x_c, x_c, 128.5, None, ALU.add)
        for x_c, w, cmp, c, i in grp:
            V.tensor_copy(w, x_c)
        for j in range(7, -1, -1):
            for x_c, w, cmp, c, i in grp:
                V.tensor_scalar(cmp, w, float(2 ** j), None, ALU.is_ge)
            for x_c, w, cmp, c, i in grp:
                V.scalar_tensor_tensor(w, cmp, -float(2 ** j), w,
                                       ALU.mult, ALU.add)
        for x_c, w, cmp, c, i in grp:
            V.tensor_tensor(x_c, x_c, w, ALU.subtract)
        for x_c, w, cmp, c, i in grp:
            db.free(V.tensor_scalar(xq1(c), x_c, -128.0, None, ALU.add), i)


def emit_attn_b(k, cx, af32, abf, x_in, x_out, xnb_scr, aob_scr,
                tb_q, tb_k, tb_v, tb_o, q_scr, qn_scr,
                kga_ins, kga_outs, vga_ins, vga_outs, nga_in, nga_out, KROWS,
                xn1, xn2):
    T, V, S, G, Y = k.T, k.V, k.S, k.G, k.Y
    tok = k.tok
    tokh = tok // 2
    NKC = tok // 128
    kgb0, vgb0 = k.at("kga"), k.at("vga")

    af32.push()
    abf.push()
    ring = af32.alloc(2 * tok)
    fstage = af32.alloc(2 * tok)
    emit_ln_split(k, cx, x_in, ring, DC, None, None, D, tag="ln",
                  fout=xnb_scr, fstage=fstage)
    k.fence()
    af32.pop()
    abf.pop()

    af32.push()
    abf.push()
    xq = xn1
    gb = af32.alloc(tok)
    emit_actquant(k, cx, af32, abf, xnb_scr,
                  lambda c: xq[:, c * tok:(c + 1) * tok], gb, "bx")
    k.fence()

    V.tensor_scalar(cx.wq[:, 14:15], cx.wq[:, 6:7], BSCALE, None, ALU.mult)
    V.drain().then_inc("ev", 1)
    k.wait(T, "ev")
    mq_col = cx.wq[:, 14:15]
    mk_col = cx.wq[:, 8:9]
    mv_col = cx.wq[:, 10:11]

    uc = af32.alloc(tok)
    sq1 = af32.alloc(tok)
    bq1, bq2 = abf.alloc(tok), abf.alloc(tok)
    bk1, bk2 = abf.alloc(tok), abf.alloc(tok)
    bv1, bv2 = abf.alloc(tok), abf.alloc(tok)
    tbdb = k.dbuf("tbdb", 2, channels=("y", "g", "s"))
    tbslots = [abf.alloc(3 * DC * 128) for _ in range(2)]
    nbn = [0]
    pass

    def int_block(tb_dram, tslice, j, ps):
        for c in range(DC):
            cs = slice(c * 128, (c + 1) * 128)
            for half in range(2):
                hsl = slice(half * tokh, (half + 1) * tokh)
                r = T.matmul(ps[:, hsl], tslice[:, cs],
                             xq[:, c * tok + half * tokh:c * tok + half * tokh + tokh],
                             start=(c == 0 and (half == 0 or bank_split(k))),
                             stop=(c == DC - 1 and (half == 1 or bank_split(k))))
        return r

    def bnorm(src_ap, dst_row):
        V.tensor_tensor(sq1[:, 0:tok], src_ap, src_ap, ALU.mult).then_inc("ev", 1)
        k.wait(T, "ev")
        for half in range(2):
            hs = slice(half * tokh, (half + 1) * tokh)
            r = T.matmul(cx.ps_sg[0:1, hs], cx.ones_col[:, :], sq1[:, hs],
                         start=(half == 0 or bank_split(k)),
                         stop=(half == 1 or bank_split(k)))
        r.then_inc("mm", 1)
        row = cx.row(4) if nbn[0] % 2 == 0 else cx.row(6)
        nbn[0] += 1
        k.wait(V, "nga")
        k.wait(V, "mm")
        V.tensor_copy(row, cx.ps_sg[0:1, 0:tok])
        V.drain().then_inc("ev", 1)
        k.wait(Y, "ev")
        Y.dma_start(out=dst_row, in_=row).then_inc("nga", 16)

    k.wait(Y, "ccq")
    k.wait(G, "ccq")
    k.wait(S, "ccq")
    for h in range(NH):
        tbi = tbdb.nexti()
        ts_ = tbslots[tbi % 2]
        for mi, (tb, le, ch) in enumerate(((tb_q, Y, "y"), (tb_k, G, "g"),
                                           (tb_v, S, "s"))):
            tbdb.pre_load(le, tbi)
            tbdb.load_dma(le, tbi, ch=ch,
                          out=ts_[:, mi * DC * 128:(mi + 1) * DC * 128].rearrange("p (c e) -> p c e", e=128),
                          in_=tb[:, h * 128:(h + 1) * 128].rearrange("(c p) e -> p c e", p=128))
        tbdb.ready(T, tbi)
        tq = ts_[:, 0:DC * 128]
        tk = ts_[:, DC * 128:2 * DC * 128]
        tv = ts_[:, 2 * DC * 128:3 * DC * 128]
        ps = (cx.ps_mm0 if h % 2 == 0 else cx.ps_mm1)[:, 0:tok]
        int_block(tb_q, tq, h, ps).then_inc("mm", 1)
        k.wait(V, "mm")
        V.scalar_tensor_tensor(uc[:, 0:tok], ps, mq_col, gb[:, 0:tok],
                               ALU.mult, ALU.mult).then_inc("ev", 1)
        k.wait(T, "ev")
        k.wait(V, "qst")
        V.tensor_copy(bq1[:, 0:tok], uc[:, 0:tok])
        V.tensor_tensor(bq2[:, 0:tok], uc[:, 0:tok], bq1[:, 0:tok],
                        ALU.subtract)
        bnorm(uc[:, 0:tok], qn_scr[h:h + 1, :])
        V.drain().then_inc("ev", 1)
        k.wait(Y, "ev")
        base = (h * 4) * 128
        Y.dma_start(out=q_scr[base:base + 128, :], in_=bq1[:, 0:tok]).then_inc("qst", 16)
        Y.dma_start(out=q_scr[base + 128:base + 256, :], in_=bq2[:, 0:tok]).then_inc("qst", 16)
        psk = (cx.ps_mm1 if h % 2 == 0 else cx.ps_mm0)[:, 0:tok]
        int_block(tb_k, tk, h, psk).then_inc("mm", 1)
        k.wait(V, "mm")
        V.scalar_tensor_tensor(uc[:, 0:tok], psk, mk_col, gb[:, 0:tok],
                               ALU.mult, ALU.mult).then_inc("ev", 1)
        k.wait(T, "ev")
        k.wait(V, "kga")
        V.tensor_copy(bk1[:, 0:tok], uc[:, 0:tok])
        V.tensor_tensor(bk2[:, 0:tok], uc[:, 0:tok], bk1[:, 0:tok],
                        ALU.subtract)
        bnorm(uc[:, 0:tok], nga_in[h:h + 1, :])
        V.drain().then_inc("ev", 1)
        k.wait(Y, "ev")
        kgi = kga_ins[h // 4]
        kb = (h % 4) * KROWS
        Y.dma_start(out=kgi[kb:kb + 128, :], in_=bk1[:, 0:tok]).then_inc("kga", 16)
        Y.dma_start(out=kgi[kb + 128:kb + 256, :], in_=bk2[:, 0:tok]).then_inc("kga", 16)
        psv = (cx.ps_mm0 if h % 2 == 0 else cx.ps_mm1)[:, 0:tok]
        rv = int_block(tb_v, tv, h, psv)
        tbdb.free(rv, tbi)
        tbdb.wait_freed(V, tbi)
        V.scalar_tensor_tensor(uc[:, 0:tok], psv, mv_col, gb[:, 0:tok],
                               ALU.mult, ALU.mult).then_inc("ev", 1)
        k.wait(T, "ev")
        k.wait(V, "vga")
        for t0 in range(0, NKC, 2):
            ts2 = [t for t in (t0, t0 + 1) if t < NKC]
            for i, t in enumerate(ts2):
                T.transpose(cx.ps_pv[:, i * 512:i * 512 + 128],
                            uc[:, t * 128:(t + 1) * 128],
                            cx.identt[:, :]).then_inc("mm", 1)
            k.wait(V, "mm")
            for i, t in enumerate(ts2):
                psl = cx.ps_pv[:, i * 512:i * 512 + 128]
                V.tensor_copy(bv1[:, t * 128:(t + 1) * 128], psl)
            for i, t in enumerate(ts2):
                psl = cx.ps_pv[:, i * 512:i * 512 + 128]
                V.tensor_tensor(bv2[:, t * 128:(t + 1) * 128], psl,
                                bv1[:, t * 128:(t + 1) * 128],
                                ALU.subtract).then_inc("ev", 1)
            k.wait(T, "ev")
        k.wait(Y, "ev")
        vgi = vga_ins[h // 4]
        vb = (h % 4) * 2 * tok
        Y.dma_start(out=vgi[vb:vb + tok, :].rearrange("(t p) d -> p t d", p=128),
                    in_=bv1[:, 0:tok]).then_inc("vga", 16)
        Y.dma_start(out=vgi[vb + tok:vb + 2 * tok, :].rearrange("(t p) d -> p t d", p=128),
                    in_=bv2[:, 0:tok]).then_inc("vga", 16)
    abf.pop()
    af32.pop()
    k.fence()

    cc0b = k.at("cc")
    for g in range(4):
        if g == 3:
            k.wait(G, "nga")
            k.wait(G, "qst")
            G.collective_compute("AllGather", ALU.bypass,
                                 replica_groups=[list(range(NCORES))],
                                 ins=[nga_in.opt()],
                                 outs=[nga_out.opt()]).then_inc("cc", 1)
        k.wait(G, "kga", val=kgb0 + 32 * 4 * (g + 1))
        G.collective_compute("AllGather", ALU.bypass,
                             replica_groups=[list(range(NCORES))],
                             ins=[kga_ins[g].opt()],
                             outs=[kga_outs[g].opt()]).then_inc("cc", 1)
        k.wait(G, "vga", val=vgb0 + 32 * 4 * (g + 1))
        G.collective_compute("AllGather", ALU.bypass,
                             replica_groups=[list(range(NCORES))],
                             ins=[vga_ins[g].opt()],
                             outs=[vga_outs[g].opt()]).then_inc("cc", 1)

    af32.push()
    abf.push()
    stg = af32.alloc(2 * tok)
    pass

    def b_writer(h, ps_pv, sigi):
        so = stg[:, (h % 2) * tok:(h % 2) * tok + tok]
        k.wait(V, "st")
        V.tensor_tensor(so, ps_pv[:, 0:tok], sigi[:, 0:tok],
                        ALU.mult).then_inc("ev", 1)
        k.wait(Y, "ev")
        Y.dma_start(out=aob_scr[h * 128:(h + 1) * 128, :], in_=so).then_inc("st", 16)

    emit_attention(k, cx, af32, abf, "bb", False, q_scr, qn_scr, nga_out,
                   kga_outs, vga_outs, KROWS, cc0b, b_writer)
    k.wait(Y, "st")
    abf.pop()
    af32.pop()
    k.fence()

    af32.push()
    abf.push()
    oq = xn1
    gb2 = af32.alloc(tok)
    emit_actquant(k, cx, af32, abf, aob_scr,
                  lambda c: oq[:, c * tok:(c + 1) * tok], gb2, "bo")
    k.fence()
    V.tensor_scalar(cx.wq[:, 15:16], cx.wq[:, 12:13], 1.0, None, ALU.mult)
    V.drain().then_inc("ev", 1)
    k.wait(T, "ev")
    mo_col = cx.wq[:, 15:16]
    todb = k.dbuf("tbdb", 2)
    toslots = [abf.alloc(DC * 128) for _ in range(2)]
    xrdb2 = k.dbuf("xrdb", 2)
    xres = af32.alloc(2 * tok)
    xo2 = af32.alloc(2 * tok)
    pass
    for j in range(DC):
        toi = todb.nexti()
        todb.pre_load(Y, toi)
        ts_ = toslots[toi % 2]
        todb.load_dma(Y, toi, out=ts_[:, 0:DC * 128].rearrange("p (c e) -> p c e", e=128),
                      in_=tb_o[:, j * 128:(j + 1) * 128].rearrange("(c p) e -> p c e", p=128))
        todb.ready(T, toi)
        ps = (cx.ps_mm0 if j % 2 == 0 else cx.ps_mm1)[:, 0:tok]
        for c in range(DC):
            cs = slice(c * 128, (c + 1) * 128)
            for half in range(2):
                hsl = slice(half * tokh, (half + 1) * tokh)
                r = T.matmul(ps[:, hsl], ts_[:, cs],
                             oq[:, c * tok + half * tokh:c * tok + half * tokh + tokh],
                             start=(c == 0 and (half == 0 or bank_split(k))),
                             stop=(c == DC - 1 and (half == 1 or bank_split(k))))
        todb.free(r, toi)
        x2i = xrdb2.nexti()
        xrdb2.pre_load(Y, x2i)
        xrdb2.load_dma(Y, x2i, out=xres[:, (j % 2) * tok:(j % 2) * tok + tok],
                       in_=x_in[j * 128:(j + 1) * 128, :])
        xrdb2.ready(V, x2i)
        todb.wait_freed(V, toi)
        k.wait(V, "xr1d")
        V.scalar_tensor_tensor(xo2[:, (j % 2) * tok:(j % 2) * tok + tok], ps,
                               mo_col, gb2[:, 0:tok], ALU.mult, ALU.mult)
        vv = V.tensor_tensor(xo2[:, (j % 2) * tok:(j % 2) * tok + tok],
                             xo2[:, (j % 2) * tok:(j % 2) * tok + tok],
                             xres[:, (j % 2) * tok:(j % 2) * tok + tok], ALU.add)
        xrdb2.free(vv, x2i)
        xrdb2.wait_freed(k.T, x2i)
        xrdb2.wait_freed(Y, x2i)
        Y.dma_start(out=x_out[j * 128:(j + 1) * 128, :],
                    in_=xo2[:, (j % 2) * tok:(j % 2) * tok + tok]).then_inc("xr1d", 16)
    k.wait(Y, "xr1d")
    abf.pop()
    af32.pop()
    k.fence()


def emit_program(k: KBuild):
    T, V, S, G, Y = k.T, k.V, k.S, k.G, k.Y
    tok = k.tok
    tokh = tok // 2
    NKC = tok // 128

    xfm = k.dram("xfm", [D, tok], F32, kind="ExternalInput")
    ident = k.dram("ident", [128, 128], F32, kind="ExternalInput")
    permhr = k.dram("permhr", [128, 128], F32, kind="ExternalInput")
    tokoff = k.dram("tokoff", [1, 1], F32, kind="ExternalInput")
    w_qdown = k.dram("w_qdown", [D, QR], F32, kind="ExternalInput")
    q_gamma = k.dram("q_gamma", [QR, 1], F32, kind="ExternalInput")
    q_beta = k.dram("q_beta", [QR, 1], F32, kind="ExternalInput")
    w_qup = k.dram("w_qup", [QR, D], F32, kind="ExternalInput")
    w_qrope = k.dram("w_qrope", [QR, NH * RD], F32, kind="ExternalInput")
    w_kvdown = k.dram("w_kvdown", [D, KVR], F32, kind="ExternalInput")
    kv_gamma = k.dram("kv_gamma", [KVR, 1], F32, kind="ExternalInput")
    kv_beta = k.dram("kv_beta", [KVR, 1], F32, kind="ExternalInput")
    w_kup = k.dram("w_kup", [KVR, D], F32, kind="ExternalInput")
    w_vup = k.dram("w_vup", [KVR, D], F32, kind="ExternalInput")
    w_krope = k.dram("w_krope", [KVR, NH * RD], F32, kind="ExternalInput")
    w_out = k.dram("w_out", [D, D], F32, kind="ExternalInput")
    wsh_up = k.dram("wsh_up", [D // NCORES, DE], F32, kind="ExternalInput")
    wsh_gate = k.dram("wsh_gate", [D // NCORES, DE], F32, kind="ExternalInput")
    wsh_down = k.dram("wsh_down", [DE // NCORES, D], F32, kind="ExternalInput")
    wsh_bq = k.dram("wsh_bq", [D // NCORES, D], F32, kind="ExternalInput")
    wsh_bk = k.dram("wsh_bk", [D // NCORES, D], F32, kind="ExternalInput")
    wsh_bv = k.dram("wsh_bv", [D // NCORES, D], F32, kind="ExternalInput")
    wsh_bo = k.dram("wsh_bo", [D // NCORES, D], F32, kind="ExternalInput")
    out_fm = k.dram("out_fm", [D, tok], F32, kind="ExternalOutput")

    xr1 = k.dram("xr1", [D, tok], F32)
    xr2 = k.dram("xr2", [D, tok], F32)
    xr3 = k.dram("xr3", [D, tok], F32)
    xnb_scr = k.dram("xnb_scr", [D, tok], F32)
    aob_scr = k.dram("aob_scr", [D, tok], F32)
    ffnacc = k.dram("ffnacc", [D, tok], F32)
    t_up = k.dram("t_up", [D, DE], BF16)
    t_gate = k.dram("t_gate", [D, DE], BF16)
    t_down = k.dram("t_down", [DE, D], BF16)
    tb_q = k.dram("tb_q", [D, D], BF16)
    tb_k = k.dram("tb_k", [D, D], BF16)
    tb_v = k.dram("tb_v", [D, D], BF16)
    tb_o = k.dram("tb_o", [D, D], BF16)
    tg_up = k.dram("tg_up", [D // NCORES, DE], BF16)
    tg_gate = k.dram("tg_gate", [D // NCORES, DE], BF16)
    tg_down = k.dram("tg_down", [DE // NCORES, D], BF16)
    tg_bq = k.dram("tg_bq", [D // NCORES, D], BF16)
    tg_bk = k.dram("tg_bk", [D // NCORES, D], BF16)
    tg_bv = k.dram("tg_bv", [D // NCORES, D], BF16)
    tg_bo = k.dram("tg_bo", [D // NCORES, D], BF16)
    wqsum_in = k.dram("wqsum_in", [1, 8], F32)
    wqsum_out = k.dram("wqsum_out", [8, 8], F32)
    cq_scr = k.dram("cq_scr", [QR, tok], F32)
    ckv_scr = k.dram("ckv_scr", [KVR, tok], F32)
    qn_scr = k.dram("qn_scr", [NH, tok], F32)
    q_scr = k.dram("q_scr", [NH * 4 * 128, tok], BF16)
    KROWS = 384
    NHG = NH // 4
    kga_ins = [k.dram(f"kga_in{g}", [NHG * KROWS, tok], BF16)
               for g in range(4)]
    kga_outs = [k.dram(f"kga_out{g}", [NCORES * NHG * KROWS, tok], BF16)
                for g in range(4)]
    vga_ins = [k.dram(f"vga_in{g}", [NHG * 2 * tok, HD], BF16)
               for g in range(4)]
    vga_outs = [k.dram(f"vga_out{g}", [NCORES * NHG * 2 * tok, HD], BF16)
                for g in range(4)]
    nga_in = k.dram("nga_in", [NH, tok], F32)
    nga_out = k.dram("nga_out", [NCORES * NH, tok], F32)

    cx = Ctx()
    cx.ps_mm0 = k.psum("ps_mm0", [128, 1024], F32)
    cx.ps_mm1 = k.psum("ps_mm1", [128, 1024], F32)
    cx.ps_pv = k.psum("ps_pv", [128, 1024], F32)
    cx.ps_sg = k.psum("ps_sg", [1, 1024], F32)

    cx.ones_col = k.sbuf("ones_col", [128, 1], F32)
    cx.ones_row = k.sbuf("ones_row", [1, 128], F32)
    cx.ones_colb = k.sbuf("ones_colb", [128, 1], BF16)
    cx.ones_rowb = k.sbuf("ones_rowb", [1, 128], BF16)
    cx.identt = k.sbuf("identt", [128, 128], F32)
    cx.permt = k.sbuf("permt", [128, 128], F32)
    cx.bcA = k.sbuf("bcA", [128, 1024], F32)
    cx.bcB = k.sbuf("bcB", [128, 1024], F32)
    cx.lnsc1 = k.sbuf("lnsc1", [128, 1024], F32)
    cx.lnsc2 = k.sbuf("lnsc2", [128, 1024], F32)
    cx.trig = k.sbuf("trig", [128, 2 * 1024], F32)
    cx.gbq = k.sbuf("gbq", [128, 2 * QC], F32)
    cx.gbkv = k.sbuf("gbkv", [128, 2 * KC], F32)
    cx.rows = k.sbuf("rows", [1, 7 * 1024], F32)
    cx.knrow = k.sbuf("knrow", [1, 256], F32)
    cx.mrow = k.sbuf("mrow", [1, 1024], BF16)
    cx.mrow2 = k.sbuf("mrow2", [1, 1024], BF16)
    cx.wq = k.sbuf("wq", [128, 16], F32)

    def rowv(i):
        if i == 7:
            i = 2
        return cx.rows[0:1, i * 1024:i * 1024 + k.tok]
    cx.row = rowv

    af32 = Arena(k, "af32", 9216, F32)
    abf = Arena(k, "abf", 52 * 1024, BF16)

    for s in ("gp", "xld", "mm", "ev", "sc", "st", "cc", "ccq", "wqst",
              "kga", "vga", "nga", "qst", "wsp"):
        k.sem(s)

    wq_dims = [(D, DE, "ffn"), (D, DE, "ffn"), (DE, D, "ffn"),
               (D, D, "b"), (D, D, "b"), (D, D, "b"), (D, D, "b")]
    wq_shards = [(wsh_up, tg_up, t_up), (wsh_gate, tg_gate, t_gate),
                 (wsh_down, tg_down, t_down), (wsh_bq, tg_bq, tb_q),
                 (wsh_bk, tg_bk, tb_k), (wsh_bv, tg_bv, tb_v),
                 (wsh_bo, tg_bo, tb_o)]
    wqst = wq_setup(k, cx, wq_shards, wq_dims)

    G.memset(cx.ones_col[:, :], 1.0)
    G.memset(cx.ones_row[:, :], 1.0)
    G.memset(cx.ones_colb[:, :], 1.0)
    G.memset(cx.ones_rowb[:, :], 1.0).then_inc("gp", 1)
    Y.dma_start(out=cx.identt[:, :], in_=ident[:, :]).then_inc("gp", 16)
    Y.dma_start(out=cx.permt[:, :], in_=permhr[:, :]).then_inc("gp", 16)

    xn1 = abf.alloc(DC * tok)
    xn2 = abf.alloc(DC * tok)
    af32.push()
    xring = af32.alloc(2 * tok)
    k.wait(T, "gp")
    emit_ln_split(k, cx, xfm, xring, DC,
                  lambda c: xn1[:, c * tok:(c + 1) * tok],
                  lambda c: xn2[:, c * tok:(c + 1) * tok], D, tag="ln")
    af32.pop()
    k.fence()

    if k.debug:
        _dbg_fm(k, cx, "xn_a", lambda c: (xn1[:, c * tok:(c + 1) * tok],
                                          xn2[:, c * tok:(c + 1) * tok]), DC)

    for c in range(QC):
        Y.dma_start(out=cx.gbq[:, c:c + 1],
                    in_=q_gamma[c * 128:(c + 1) * 128, :]).then_inc("xld", 16)
        Y.dma_start(out=cx.gbq[:, QC + c:QC + c + 1],
                    in_=q_beta[c * 128:(c + 1) * 128, :]).then_inc("xld", 16)
    for c in range(KC):
        Y.dma_start(out=cx.gbkv[:, c:c + 1],
                    in_=kv_gamma[c * 128:(c + 1) * 128, :]).then_inc("xld", 16)
        Y.dma_start(out=cx.gbkv[:, KC + c:KC + c + 1],
                    in_=kv_beta[c * 128:(c + 1) * 128, :]).then_inc("xld", 16)

    wdb = k.dbuf("wdb", 2, channels=("g", "y"))
    abf.push()
    af32.push()
    wslots = [(abf.alloc(DC * 128), abf.alloc(DC * 128), af32.alloc(DC * 128))
              for _ in range(2)]
    evring = af32.alloc(2 * tok)

    nevp = [k.at("ev")]

    def proj_block(jg_unused, w_dram, j, dst_dram, dst_row0):
        jg = wdb.nexti()
        whi, wlo, wraw = wslots[jg % 2]
        wdb.pre_load(G, jg)
        wdb.pre_load(Y, jg)
        for c in range(DC):
            rs = slice(c * 128, (c + 1) * 128)
            cs = slice(c * 128, (c + 1) * 128)
            wdb.load_dma(G, jg, ch="g", out=whi[:, cs], in_=w_dram[rs, j * 128:(j + 1) * 128])
            wdb.load_dma(Y, jg, ch="y", out=wraw[:, cs], in_=w_dram[rs, j * 128:(j + 1) * 128])
        wdb.ready(V, jg)
        for c in range(DC):
            cs = slice(c * 128, (c + 1) * 128)
            V.tensor_tensor(wlo[:, cs], wraw[:, cs], whi[:, cs],
                            ALU.subtract).then_inc("wsp", 1)
        k.wait(T, "wsp")
        if jg >= 2:
            k.wait(T, "ev", val=nevp[0] + jg - 1)
        ps = (cx.ps_mm0 if jg % 2 == 0 else cx.ps_mm1)[:, 0:tok]
        for c in range(DC):
            cs = slice(c * 128, (c + 1) * 128)
            for half in range(2):
                hs = slice(half * tokh, (half + 1) * tokh)
                r = mm3(k, ps[:, hs], whi[:, cs], wlo[:, cs],
                        xn1[:, c * tok + half * tokh:c * tok + half * tokh + tokh],
                        xn2[:, c * tok + half * tokh:c * tok + half * tokh + tokh],
                        start=(c == 0 and (half == 0 or bank_split(k))),
                        stop=(c == DC - 1 and (half == 1 or bank_split(k))))
        wdb.free(r, jg)
        wdb.wait_freed(V, jg)
        k.wait(V, "st")
        ev = evring[:, (jg % 2) * tok:(jg % 2) * tok + tok]
        V.tensor_copy(ev, ps).then_inc("ev", 1)
        k.wait(Y, "ev")
        Y.dma_start(out=dst_dram[dst_row0:dst_row0 + 128, :], in_=ev).then_inc("st", 16)

    jg = 0
    for j in range(QC):
        proj_block(jg, w_qdown, j, cq_scr, j * 128)
        for u in wqst.scale_units[jg * 4:jg * 4 + 4]:
            u()
        jg += 1
    for j in range(KC):
        proj_block(jg, w_kvdown, j, ckv_scr, j * 128)
        for u in wqst.scale_units[jg * 4:jg * 4 + 4]:
            u()
        jg += 1
    abf.pop()
    af32.pop()
    k.fence()

    cqn1 = xn1[:, 0:QC * tok]
    cqn2 = xn2[:, 0:QC * tok]
    ckvn1 = xn1[:, QC * tok:(QC + KC) * tok]
    ckvn2 = xn2[:, QC * tok:(QC + KC) * tok]
    k.wait(V, "xld")
    k.wait(Y, "st")
    af32.push()
    lring = af32.alloc(2 * tok)
    emit_ln_split(k, cx, cq_scr, lring, QC,
                  lambda c: cqn1[:, c * tok:(c + 1) * tok],
                  lambda c: cqn2[:, c * tok:(c + 1) * tok], QR,
                  gamma_cols=lambda c: cx.gbq[:, c:c + 1],
                  beta_cols=lambda c: cx.gbq[:, QC + c:QC + c + 1], tag="ln")
    k.fence()
    emit_ln_split(k, cx, ckv_scr, lring, KC,
                  lambda c: ckvn1[:, c * tok:(c + 1) * tok],
                  lambda c: ckvn2[:, c * tok:(c + 1) * tok], KVR,
                  gamma_cols=lambda c: cx.gbkv[:, c:c + 1],
                  beta_cols=lambda c: cx.gbkv[:, KC + c:KC + c + 1], tag="ln")
    af32.pop()
    k.fence()

    if k.debug:
        _dbg_fm(k, cx, "cqn", lambda c: (cqn1[:, c * tok:(c + 1) * tok],
                                         cqn2[:, c * tok:(c + 1) * tok]), QC)
        _dbg_fm(k, cx, "ckvn", lambda c: (ckvn1[:, c * tok:(c + 1) * tok],
                                          ckvn2[:, c * tok:(c + 1) * tok]), KC)

    wq_scales_finish(k, cx, wqst, wqsum_in, wqsum_out)

    kga0, vga0 = k.at("kga"), k.at("vga")
    _emit_trig_tables(k, cx, tokoff)

    af32.push()
    abf.push()
    uc = af32.alloc(tok)
    ur = af32.alloc(tok)
    usw = af32.alloc(tok)
    ut2 = af32.alloc(tok)
    sq1 = af32.alloc(tok)
    b1 = abf.alloc(tok)
    b2 = abf.alloc(tok)
    b3 = abf.alloc(tok)
    b4 = abf.alloc(tok)
    wudb = k.dbuf("wudb", 2, channels=("g", "y"))
    wkdb = wudb
    wuslots = [(abf.alloc(QC * 128), abf.alloc(QC * 128), af32.alloc(QC * 128))
               for _ in range(2)]
    k.sem("swp")

    def up_block(db, ug_unused, w_dram, col0, ncol, nch, ps, c1, c2):
        ug = db.nexti()
        whi, wlo, wraw = wuslots[ug % 2]
        db.pre_load(G, ug)
        db.pre_load(Y, ug)
        for c in range(nch):
            rs = slice(c * 128, (c + 1) * 128)
            cs = slice(c * ncol, (c + 1) * ncol)
            db.load_dma(G, ug, ch="g", out=whi[:, cs], in_=w_dram[rs, col0:col0 + ncol])
            db.load_dma(Y, ug, ch="y", out=wraw[:, cs], in_=w_dram[rs, col0:col0 + ncol])
        db.ready(V, ug)
        for c in range(nch):
            cs = slice(c * ncol, (c + 1) * ncol)
            V.tensor_tensor(wlo[:, cs], wraw[:, cs], whi[:, cs],
                            ALU.subtract).then_inc("wsp", 1)
        k.wait(T, "wsp")
        for c in range(nch):
            cs = slice(c * ncol, (c + 1) * ncol)
            for half in range(2):
                hs = slice(half * tokh, (half + 1) * tokh)
                r = mm3(k, ps[:, hs], whi[:, cs], wlo[:, cs],
                        c1[:, c * tok + half * tokh:c * tok + half * tokh + tokh],
                        c2[:, c * tok + half * tokh:c * tok + half * tokh + tokh],
                        start=(c == 0 and (half == 0 or bank_split(k))),
                        stop=(c == nch - 1 and (half == 1 or bank_split(k))))
        db.free(r, ug)
        return ug

    def rope_apply(u64, scale):
        k.wait(Y, "ev")
        Y.dma_start(out=usw[0:32, 0:tok], in_=u64[32:64, 0:tok]).then_inc("swp", 16)
        Y.dma_start(out=usw[32:64, 0:tok], in_=u64[0:32, 0:tok]).then_inc("swp", 16)
        k.wait(V, "swp")
        cosb = cx.trig[0:64, 0:tok]
        sinb = cx.trig[0:64, 1024:1024 + tok]
        V.tensor_tensor(u64, u64, cosb, ALU.mult)
        V.tensor_tensor(ut2[0:64, 0:tok], usw[0:64, 0:tok], sinb, ALU.mult)
        V.tensor_tensor(u64, u64, ut2[0:64, 0:tok], ALU.add)
        if scale != 1.0:
            V.tensor_scalar(u64, u64, scale, None, ALU.mult)

    def head_norm(content_ap, rope_ap, dst_dram_row):
        V.tensor_tensor(sq1[:, 0:tok], content_ap, content_ap,
                        ALU.mult).then_inc("ev", 1)
        k.wait(T, "ev")
        for half in range(2):
            hs = slice(half * tokh, (half + 1) * tokh)
            T.matmul(cx.ps_sg[0:1, hs], cx.ones_col[:, :], sq1[:, hs],
                     start=(half == 0 or bank_split(k)), stop=False)
        V.tensor_tensor(ut2[0:64, 0:tok], rope_ap, rope_ap,
                        ALU.mult).then_inc("ev", 1)
        k.wait(T, "ev")
        for half in range(2):
            hs = slice(half * tokh, (half + 1) * tokh)
            r = T.matmul(cx.ps_sg[0:1, hs], cx.ones_col[0:64, :],
                         ut2[0:64, hs], start=False,
                         stop=(half == 1 or bank_split(k)))
        r.then_inc("mm", 1)
        k.wait(V, "mm")
        V.tensor_copy(cx.row(7), cx.ps_sg[0:1, 0:tok])
        V.drain().then_inc("ev", 1)
        k.wait(T, "ev")
        k.wait(Y, "ev")
        Y.dma_start(out=dst_dram_row, in_=cx.row(7)).then_inc("nga", 16)
        k.wait(V, "nga")

    ug = 0
    for h in range(NH):
        k.wait(T, "ev")
        ps = (cx.ps_mm0 if h % 2 == 0 else cx.ps_mm1)[:, 0:tok]
        psr = (cx.ps_mm1 if h % 2 == 0 else cx.ps_mm0)[0:64, 0:tok]
        ugi = up_block(wudb, ug, w_qup, h * 128, 128, QC, ps, cqn1, cqn2)
        ug += 1
        wudb.wait_freed(V, ugi)
        V.tensor_scalar(uc[:, 0:tok], ps, ASCALE, None, ALU.mult).then_inc("ev", 1)
        ugi = up_block(wudb, ug, w_qrope, h * RD, RD, QC, psr, cqn1, cqn2)
        ug += 1
        wudb.wait_freed(V, ugi)
        k.wait(V, "swp")
        V.tensor_copy(ur[0:64, 0:tok], psr).then_inc("ev", 1)
        rope_apply(ur[0:64, 0:tok], ASCALE)
        V.tensor_copy(b1[:, 0:tok], uc[:, 0:tok])
        V.tensor_copy(b3[0:64, 0:tok], ur[0:64, 0:tok])
        V.tensor_tensor(b2[:, 0:tok], uc[:, 0:tok], b1[:, 0:tok], ALU.subtract)
        V.tensor_tensor(b4[0:64, 0:tok], ur[0:64, 0:tok], b3[0:64, 0:tok],
                        ALU.subtract)
        head_norm(uc[:, 0:tok], ur[0:64, 0:tok], qn_scr[h:h + 1, :])
        V.drain().then_inc("ev", 1)
        k.wait(Y, "ev")
        base = (h * 4) * 128
        Y.dma_start(out=q_scr[base:base + 128, :], in_=b1[:, 0:tok]).then_inc("qst", 16)
        Y.dma_start(out=q_scr[base + 128:base + 256, :], in_=b2[:, 0:tok]).then_inc("qst", 16)
        Y.dma_start(out=q_scr[base + 256:base + 320, :], in_=b3[0:64, 0:tok]).then_inc("qst", 16)
        Y.dma_start(out=q_scr[base + 320:base + 384, :], in_=b4[0:64, 0:tok]).then_inc("qst", 16)
        Y.dma_start(out=q_scr[base + 384:base + 448, :], in_=b4[0:64, 0:tok]).then_inc("qst", 16)
        Y.dma_start(out=q_scr[base + 448:base + 512, :], in_=b3[0:64, 0:tok]).then_inc("qst", 16)
        k.wait(V, "qst")
        for u in wqst.quant_units[2 * h:2 * h + 2]:
            u()

    for h in range(NH):
        k.wait(T, "ev")
        ps = (cx.ps_mm0 if h % 2 == 0 else cx.ps_mm1)[:, 0:tok]
        psr = (cx.ps_mm1 if h % 2 == 0 else cx.ps_mm0)[0:64, 0:tok]
        ugi = up_block(wkdb, ug, w_kup, h * 128, 128, KC, ps, ckvn1, ckvn2)
        ug += 1
        wkdb.wait_freed(V, ugi)
        V.tensor_copy(uc[:, 0:tok], ps).then_inc("ev", 1)
        ugi = up_block(wkdb, ug, w_krope, h * RD, RD, KC, psr, ckvn1, ckvn2)
        ug += 1
        wkdb.wait_freed(V, ugi)
        k.wait(V, "swp")
        V.tensor_copy(ur[0:64, 0:tok], psr).then_inc("ev", 1)
        rope_apply(ur[0:64, 0:tok], 1.0)
        V.tensor_copy(b1[:, 0:tok], uc[:, 0:tok])
        V.tensor_copy(b3[0:64, 0:tok], ur[0:64, 0:tok])
        V.tensor_tensor(b2[:, 0:tok], uc[:, 0:tok], b1[:, 0:tok], ALU.subtract)
        V.tensor_tensor(b4[0:64, 0:tok], ur[0:64, 0:tok], b3[0:64, 0:tok],
                        ALU.subtract)
        head_norm(uc[:, 0:tok], ur[0:64, 0:tok], nga_in[h:h + 1, :])
        V.drain().then_inc("ev", 1)
        k.wait(Y, "ev")
        kgi = kga_ins[h // 4]
        kb = (h % 4) * KROWS
        Y.dma_start(out=kgi[kb:kb + 128, :], in_=b1[:, 0:tok]).then_inc("kga", 16)
        Y.dma_start(out=kgi[kb + 128:kb + 256, :], in_=b2[:, 0:tok]).then_inc("kga", 16)
        Y.dma_start(out=kgi[kb + 256:kb + 320, :], in_=b3[0:64, 0:tok]).then_inc("kga", 16)
        Y.dma_start(out=kgi[kb + 320:kb + 384, :], in_=b4[0:64, 0:tok]).then_inc("kga", 16)
        psv = (cx.ps_mm0 if h % 2 == 0 else cx.ps_mm1)[:, 0:tok]
        ugi = up_block(wkdb, ug, w_vup, h * 128, 128, KC, psv, ckvn1, ckvn2)
        ug += 1
        wkdb.wait_freed(V, ugi)
        k.wait(V, "kga")
        V.tensor_copy(uc[:, 0:tok], psv).then_inc("ev", 1)
        k.wait(T, "ev")
        for t0 in range(0, NKC, 2):
            ts2 = [t for t in (t0, t0 + 1) if t < NKC]
            for i, t in enumerate(ts2):
                T.transpose(cx.ps_pv[:, i * 512:i * 512 + 128],
                            uc[:, t * 128:(t + 1) * 128],
                            cx.identt[:, :]).then_inc("mm", 1)
            k.wait(V, "mm")
            for i, t in enumerate(ts2):
                psl = cx.ps_pv[:, i * 512:i * 512 + 128]
                V.tensor_copy(b1[:, t * 128:(t + 1) * 128], psl)
            for i, t in enumerate(ts2):
                psl = cx.ps_pv[:, i * 512:i * 512 + 128]
                V.tensor_tensor(b2[:, t * 128:(t + 1) * 128], psl,
                                b1[:, t * 128:(t + 1) * 128],
                                ALU.subtract).then_inc("ev", 1)
            k.wait(T, "ev")
        k.wait(Y, "ev")
        vgi = vga_ins[h // 4]
        vb = (h % 4) * 2 * tok
        Y.dma_start(out=vgi[vb:vb + tok, :].rearrange("(t p) d -> p t d", p=128),
                    in_=b1[:, 0:tok]).then_inc("vga", 16)
        Y.dma_start(out=vgi[vb + tok:vb + 2 * tok, :].rearrange("(t p) d -> p t d", p=128),
                    in_=b2[:, 0:tok]).then_inc("vga", 16)
        k.wait(V, "vga")
        for u in wqst.quant_units[32 + 2 * h:32 + 2 * h + 2]:
            u()

    abf.pop()
    af32.pop()
    k.fence()

    cc0a = k.at("cc")
    for g in range(4):
        if g == 3:
            k.wait(G, "nga")
            k.wait(G, "qst")
            G.collective_compute("AllGather", ALU.bypass,
                                 replica_groups=[list(range(NCORES))],
                                 ins=[nga_in.opt()],
                                 outs=[nga_out.opt()]).then_inc("cc", 1)
        k.wait(G, "kga", val=kga0 + 64 * 4 * (g + 1))
        G.collective_compute("AllGather", ALU.bypass,
                             replica_groups=[list(range(NCORES))],
                             ins=[kga_ins[g].opt()],
                             outs=[kga_outs[g].opt()]).then_inc("cc", 1)
        k.wait(G, "vga", val=vga0 + 32 * 4 * (g + 1))
        G.collective_compute("AllGather", ALU.bypass,
                             replica_groups=[list(range(NCORES))],
                             ins=[vga_ins[g].opt()],
                             outs=[vga_outs[g].opt()]).then_inc("cc", 1)
    k.wait(G, "wqst")
    for tin, tout in ((tg_up, t_up), (tg_gate, t_gate), (tg_down, t_down),
                      (tg_bq, tb_q), (tg_bk, tb_k), (tg_bv, tb_v),
                      (tg_bo, tb_o)):
        G.collective_compute("AllGather", ALU.bypass,
                             replica_groups=[list(range(NCORES))],
                             ins=[tin.opt()], outs=[tout.opt()]).then_inc("ccq", 1)

    af32.push()
    abf.push()
    aout1 = xn1
    aout2 = xn2

    def a_writer(h, ps_pv, sigi):
        V.tensor_tensor(cx.lnsc1[:, 0:tok], ps_pv[:, 0:tok], sigi[:, 0:tok],
                        ALU.mult)
        V.tensor_copy(aout1[:, h * tok:(h + 1) * tok], cx.lnsc1[:, 0:tok])
        V.tensor_tensor(aout2[:, h * tok:(h + 1) * tok], cx.lnsc1[:, 0:tok],
                        aout1[:, h * tok:(h + 1) * tok],
                        ALU.subtract).then_inc("ev", 1)

    emit_attention(k, cx, af32, abf, "aa", True, q_scr, qn_scr, nga_out,
                   kga_outs, vga_outs, KROWS, cc0a, a_writer)

    if k.debug:
        _dbg_fm(k, cx, "aout", lambda c: (aout1[:, c * tok:(c + 1) * tok],
                                          aout2[:, c * tok:(c + 1) * tok]), NH)

    abf.pop()
    af32.pop()
    k.fence()
    abf.push()
    af32.push()
    wodb = k.dbuf("wdb", 2, channels=("g", "y"))
    woslots = [(abf.alloc(DC * 128), abf.alloc(DC * 128), af32.alloc(DC * 128))
               for _ in range(2)]
    xrdb = k.dbuf("xrdb", 2)
    xres = af32.alloc(2 * tok)
    xo = af32.alloc(2 * tok)
    k.sem("wsp")
    k.sem("xr1d")
    for j in range(DC):
        wj = wodb.nexti()
        whi, wlo, wraw = woslots[wj % 2]
        wodb.pre_load(G, wj)
        wodb.pre_load(Y, wj)
        for c in range(DC):
            rs = slice(c * 128, (c + 1) * 128)
            cs = slice(c * 128, (c + 1) * 128)
            wodb.load_dma(G, wj, ch="g", out=whi[:, cs], in_=w_out[rs, j * 128:(j + 1) * 128])
            wodb.load_dma(Y, wj, ch="y", out=wraw[:, cs], in_=w_out[rs, j * 128:(j + 1) * 128])
        wodb.ready(V, wj)
        for c in range(DC):
            cs = slice(c * 128, (c + 1) * 128)
            V.tensor_tensor(wlo[:, cs], wraw[:, cs], whi[:, cs],
                            ALU.subtract).then_inc("wsp", 1)
        k.wait(T, "wsp")
        ps = (cx.ps_mm0 if j % 2 == 0 else cx.ps_mm1)[:, 0:tok]
        for c in range(DC):
            cs = slice(c * 128, (c + 1) * 128)
            for half in range(2):
                hs = slice(half * tokh, (half + 1) * tokh)
                r = mm3(k, ps[:, hs], whi[:, cs], wlo[:, cs],
                        aout1[:, c * tok + half * tokh:c * tok + half * tokh + tokh],
                        aout2[:, c * tok + half * tokh:c * tok + half * tokh + tokh],
                        start=(c == 0 and (half == 0 or bank_split(k))),
                        stop=(c == DC - 1 and (half == 1 or bank_split(k))))
        wodb.free(r, wj)
        xj = xrdb.nexti()
        xrdb.pre_load(Y, xj)
        xrdb.load_dma(Y, xj, out=xres[:, (j % 2) * tok:(j % 2) * tok + tok],
                      in_=xfm[j * 128:(j + 1) * 128, :])
        xrdb.ready(V, xj)
        wodb.wait_freed(V, wj)
        k.wait(V, "xr1d")
        vv = V.tensor_tensor(xo[:, (j % 2) * tok:(j % 2) * tok + tok], ps,
                             xres[:, (j % 2) * tok:(j % 2) * tok + tok], ALU.add)
        xrdb.free(vv, xj)
        xrdb.wait_freed(k.T, xj)
        xrdb.wait_freed(Y, xj)
        Y.dma_start(out=xr1[j * 128:(j + 1) * 128, :],
                    in_=xo[:, (j % 2) * tok:(j % 2) * tok + tok]).then_inc("xr1d", 16)
    abf.pop()
    af32.pop()
    k.fence()

    emit_ffn(k, cx, af32, abf, xn1, xn2, xr1, xr2, t_up, t_gate, t_down,
             ffnacc, "f1")
    if k.debug:
        _dbg_dram(k, cx, af32, "x2", xr2)

    emit_attn_b(k, cx, af32, abf, xr2, xr3, xnb_scr, aob_scr,
                tb_q, tb_k, tb_v, tb_o, q_scr, qn_scr,
                kga_ins, kga_outs, vga_ins, vga_outs, nga_in, nga_out, KROWS,
                xn1, xn2)
    if k.debug:
        _dbg_dram(k, cx, af32, "x3", xr3)

    emit_ffn(k, cx, af32, abf, xn1, xn2, xr3, out_fm, t_up, t_gate, t_down,
             ffnacc, "f2")


def _dbg_dram(k, cx, af32, name, src_dram):
    V, Y = k.V, k.Y
    tok = k.tok
    d = k.dbg(name, [D, tok], F32)
    s = "dbgs"
    k.sem(s)
    af32.push()
    tmp = af32.alloc(2 * tok)
    for c in range(DC):
        k.wait(Y, s)
        Y.dma_start(out=tmp[:, (c % 2) * tok:(c % 2) * tok + tok],
                    in_=src_dram[c * 128:(c + 1) * 128, :]).then_inc("xld", 16)
        k.wait(Y, "xld")
        Y.dma_start(out=d[c * 128:(c + 1) * 128, :],
                    in_=tmp[:, (c % 2) * tok:(c % 2) * tok + tok]).then_inc(s, 16)
    k.wait(Y, s)
    af32.pop()


_NC_CACHE = {}


def build_nc(tok=1024, debug=False):
    key = (tok, bool(debug))
    if key not in _NC_CACHE:
        k = KBuild(tok=tok, debug=debug)
        emit_program(k)
        nc = k.finalize()
        _NC_CACHE[key] = (nc, sorted(k.dbg_out))
    return _NC_CACHE[key]


def _in_maps(inputs, tok=1024):
    x = np.asarray(inputs["L0_attn_a_x"])
    ident = np.eye(128, dtype=np.float32)
    permhr = np.zeros((128, 128), dtype=np.float32)
    for p in range(128):
        permhr[p, (p % NH) * NCORES + p // NH] = 1.0
    wmap = {
        "w_qdown": inputs["L0_attn_a_Wq_down"],
        "q_gamma": np.asarray(inputs["L0_attn_a_q_gamma"]).reshape(-1, 1),
        "q_beta": np.asarray(inputs["L0_attn_a_q_beta"]).reshape(-1, 1),
        "w_qup": inputs["L0_attn_a_Wq_up"],
        "w_qrope": inputs["L0_attn_a_Wq_rope"],
        "w_kvdown": inputs["L0_attn_a_Wkv_down"],
        "kv_gamma": np.asarray(inputs["L0_attn_a_kv_gamma"]).reshape(-1, 1),
        "kv_beta": np.asarray(inputs["L0_attn_a_kv_beta"]).reshape(-1, 1),
        "w_kup": inputs["L0_attn_a_Wk_up"],
        "w_vup": inputs["L0_attn_a_Wv_up"],
        "w_krope": inputs["L0_attn_a_Wk_rope"],
        "w_out": inputs["L0_attn_a_Wout"],
    }
    wmap = {kk: np.ascontiguousarray(np.asarray(v), dtype=np.float32)
            for kk, v in wmap.items()}
    shsrc = {
        "wsh_up": np.asarray(inputs["L0_ffn_W_up"]).T,
        "wsh_gate": np.asarray(inputs["L0_ffn_W_gate"]).T,
        "wsh_down": np.asarray(inputs["L0_ffn_W_down"]).T,
        "wsh_bq": np.asarray(inputs["L0_attn_b_w_q"]).T,
        "wsh_bk": np.asarray(inputs["L0_attn_b_w_k"]).T,
        "wsh_bv": np.asarray(inputs["L0_attn_b_w_v"]).T,
        "wsh_bo": np.asarray(inputs["L0_attn_b_w_o"]).T,
    }
    maps = []
    for c in range(NCORES):
        sl = np.ascontiguousarray(x[0, c * tok:(c + 1) * tok, :].T)
        m = {"xfm": sl, "ident": ident, "permhr": permhr,
             "tokoff": np.full((1, 1), float(c * tok), np.float32)}
        m.update(wmap)
        for kk, w in shsrc.items():
            nr8 = w.shape[0] // NCORES
            m[kk] = np.ascontiguousarray(
                w[c * nr8:(c + 1) * nr8, :], dtype=np.float32)
        maps.append(m)
    return maps


def kernel(**inputs):
    tok = np.asarray(inputs["L0_attn_a_x"]).shape[1] // NCORES
    nc, _ = build_nc(tok=tok, debug=False)
    res = run_bass_kernel_spmd(nc, _in_maps(inputs, tok), core_ids=list(range(NCORES)))
    out = np.concatenate([np.asarray(r["out_fm"]).T[None] for r in res.results], axis=1)
    return out


def kernel_debug(**inputs):
    tok = np.asarray(inputs["L0_attn_a_x"]).shape[1] // NCORES
    nc, dbg_names = build_nc(tok=tok, debug=True)
    res = run_bass_kernel_spmd(nc, _in_maps(inputs, tok), core_ids=list(range(NCORES)))
    out = np.concatenate([np.asarray(r["out_fm"]).T[None] for r in res.results], axis=1)
    dbg = {n: [np.asarray(r["dbg_" + n]) for r in res.results] for n in dbg_names}
    return out, dbg
